# version 03
# baseline (speedup 1.0000x reference)
import os
import numpy as np
import ml_dtypes
import concourse.bass as bass
import concourse.mybir as mybir
from concourse.bass_utils import run_bass_kernel_spmd
from contextlib import ExitStack

F32 = mybir.dt.float32
BF16 = mybir.dt.bfloat16
AF = mybir.ActivationFunctionType
ALU = mybir.AluOpType

T = 2048
HALO = 256
D = 1024
NH = 16
G = 512
NG = T // G
NT = T // 128
NJT = (T + HALO) // 128
KR = 6
EPS = 1e-5
NGRP = int(os.environ.get("KNGRP", str(NG)))


class Rec:
    QUEUES = ("pe", "act", "dve", "gp", "sp")
    CSEM = {"pe": "s_pe", "act": "s_act", "dve": "s_dve", "gp": "s_gp", "sp": "s_sp"}
    DSEM = {"sp": "d_sp", "act": "d_act", "gp": "d_gp"}

    def __init__(self):
        self.ops = {q: [] for q in self.QUEUES}
        self.cnt = {s: 0 for s in list(self.CSEM.values()) + list(self.DSEM.values())}
        self.last_dmat = None

    def add(self, q, emit, deps=(), dma=False, tag="", keep_same=False):
        sem = self.DSEM[q] if dma else self.CSEM[q]
        inc = 16 if dma else 1
        dd = {}
        for t in deps:
            if t is None:
                continue
            s, v = t
            if (not dma) and (not keep_same) and s == self.CSEM[q]:
                continue
            dd[s] = max(dd.get(s, 0), v)
        if dma and self.last_dmat is not None:
            s, v = self.last_dmat
            dd[s] = max(dd.get(s, 0), v)
        self.cnt[sem] += inc
        self.ops[q].append((emit, sorted(dd.items()), sem, inc, tag))
        return (sem, self.cnt[sem])

    def dma_quiesce(self):
        return [(s, self.cnt[s]) for s in self.DSEM.values() if self.cnt[s] > 0]


class Buf:
    __slots__ = ("w", "r")

    def __init__(self):
        self.w = None
        self.r = []


def mkuse(rec):
    def use(q, emit, reads=(), writes=(), dma=False, tag="", extra=(), sync_same=False):
        deps = list(extra)
        for b in reads:
            deps.append(b.w)
        for b in writes:
            deps.append(b.w)
            deps.extend(b.r)
        tok = rec.add(q, emit, deps, dma=dma, tag=tag, keep_same=sync_same)
        for b in reads:
            b.r.append(tok)
        for b in writes:
            b.w = tok
            b.r = []
        return tok
    return use


def build_nc(debug_keys=()):
    nc = bass.Bass("TRN2", target_bir_lowering=False)

    def din(name, shape, dtype=F32):
        return nc.dram_tensor(name, shape, dtype, kind="ExternalInput").ap()

    x_own = din("x_own", [T, D])
    x_halo = din("x_halo", [HALO, D])
    x_other = din("x_other", [T, D])
    wq = din("wq", [D, D]); wk = din("wk", [D, D]); wv = din("wv", [D, D])
    wgq = din("wgq", [D, 512]); wgk = din("wgk", [D, 512]); wgv = din("wgv", [D, 512])
    r_out = din("r_out", [512, D]); w_out = din("w_out", [D, D])
    win = din("win", [D, 64]); binv = din("binv", [64, 1])
    woutf = din("woutf", [64, D]); boutv = din("boutv", [1, D]); b2v = din("b2v", [64, 1])
    gateblk8 = din("gateblk8", [128, 8 * 32])
    selg = din("selg", [16, 8 * 128])
    rrep = din("rrep", [64, 8 * 128])
    bdh = din("bdh", [128, 8 * 64])
    w1scat = din("w1scat", [128, 8]); b1scat = din("b1scat", [128, 8])
    maskj = din("maskj", [128, 1536]); maskj0 = din("maskj0", [128, 1536])
    maskr = din("maskr", [128, 2048])
    pm32 = din("pm32", [128, 16 * 32])
    identc = din("identc", [64, 64])
    onesr = din("onesr", [1, 128])
    ones64c = din("ones64c", [128, 64])

    y = nc.dram_tensor("y", [T, D], F32, kind="ExternalOutput").ap()
    dbg = None
    if debug_keys:
        dbg = nc.dram_tensor("dbg", [128, 16384], F32, kind="ExternalOutput").ap()
    scr1 = nc.dram_tensor("scr1", [T + HALO, D], BF16, kind="Internal").ap()
    scr2 = nc.dram_tensor("scr2", [T, D], BF16, kind="Internal").ap()
    scrx1 = nc.dram_tensor("scrx1", [T, D], F32, kind="Internal").ap()

    rec = Rec()
    use = mkuse(rec)
    st = ExitStack()
    en = st.enter_context

    def sb(name, shape, dtype):
        return en(nc.sbuf_tensor(name, shape, dtype))

    w_q = sb("w_q", [128, 8 * 1024], BF16)
    w_k = sb("w_k", [128, 8 * 1024], BF16)
    w_v = sb("w_v", [128, 8 * 1024], BF16)
    w_o = sb("w_o", [128, 8 * 1024], BF16)
    w_gq = sb("w_gq", [128, 8 * 512], BF16)
    r_o = sb("r_o", [128, 4 * 1024], BF16)
    w_in = sb("w_in", [128, 8 * 64], BF16)
    w_of = sb("w_of", [64, 1024], BF16)
    rrep_s = sb("rrep_s", [64, 8 * 128], BF16)
    bdh_s = sb("bdh_s", [128, 8 * 64], BF16)
    gblk = sb("gblk", [128, 8 * 32], BF16)
    selg_s = sb("selg_s", [16, 8 * 128], BF16)
    mj = sb("mj", [128, 1536], BF16)
    mr_s = sb("mr_s", [128, 512], BF16)
    ident_s = sb("ident_s", [64, 64], BF16)
    ones1_s = sb("ones1_s", [1, 128], BF16)
    ones64_s = sb("ones64_s", [128, 64], BF16)
    w1s = sb("w1s", [128, 8], F32)
    b1s = sb("b1s", [128, 8], F32)
    bin_s = sb("bin_s", [64, 1], F32)
    b2_s = sb("b2_s", [64, 1], F32)
    epsv = sb("epsv", [128, 1], F32)

    arena = sb("arena", [128, 4096], BF16)
    grbc_s = sb("grbc_s", [128, 512], BF16)
    t1_s = sb("t1_s", [128, 512], BF16)
    xa = sb("xa", [128, 2 * 1024], F32)
    lnst = sb("lnst", [128, 3 * 1024], BF16)
    ln1xT = sb("ln1xT", [128, 8 * 768], BF16)
    q_g = sb("q_g", [128, 8 * 512], BF16)
    ln2xT = q_g
    gq_g = sb("gq_g", [128, 4 * 512], BF16)
    k_sb = sb("k_sb", [128, 8 * KR * 128], BF16)
    v_sb = sb("v_sb", [128, KR * 1024], BF16)
    pmb = sb("pmb", [128, 1536], BF16)
    ps3 = sb("ps3", [128, 2 * 512], BF16)
    sbc = sb("sbc", [128, 512], BF16)
    rcp = sb("rcp", [128, 512], F32)
    comb = sb("comb", [128, 8 * 512], BF16)
    gsig = sb("gsig", [16, 1024], BF16)
    union = sb("union", [128, 3072], BF16)
    pooled = union[:, 0:1024]
    pooledT = union[:, 1024:1536]
    pm32_v = union[:, 1536:2048]
    sT_v = union[:, 0:512]
    h_v = union[:, 512:1536]
    gm_v = union[:, 1536:2048]
    bout_v = union[:, 2048:3072]
    gkT = sb("gkT", [128, 4 * 64], BF16)
    gv_l = sb("gv_l", [128, 512], BF16)
    x1_sb = sb("x1_sb", [128, 2 * 1024], F32)

    st_s1 = sb("st_s1", [128, 16], F32)
    st_s5 = sb("st_s5", [128, 16], F32)
    st_s7 = sb("st_s7", [128, 16], F32)
    rst_s1 = sb("rst_s1", [128, 4], F32)
    rst_s5 = sb("rst_s5", [128, 8], F32)
    rst_s7 = sb("rst_s7", [128, 4], F32)
    wrm = rcp
    wrmb = sbc

    psA = en(nc.psum_tensor("psA", [128, 2048], F32))
    psB = en(nc.psum_tensor("psB", [128, 2048], F32))

    def bank(i):
        t = psA if i < 4 else psB
        return t[:, (i % 4) * 512:((i % 4) + 1) * 512]

    def bank_t(i):
        return (psA if i < 4 else psB, (i % 4) * 512)

    sems = {}
    for s in list(Rec.CSEM.values()) + list(Rec.DSEM.values()):
        sems[s] = en(nc.semaphore(s))

    PB = [Buf() for _ in range(8)]
    B = Buf
    bw = {n: B() for n in ["w_q", "w_k", "w_v", "w_o", "w_gq", "r_o", "w_in", "w_of",
                           "rrep", "bdh", "gblk", "selg", "mj", "mj0", "mr",
                           "ident", "ones1", "ones64", "w1s", "b1s", "bin", "b2",
                           "eps"]}
    b_xa = [B(), B()]
    b_lnst = [B() for _ in range(3)]
    b_ln1xT = B(); b_qg = B(); b_gqg = B()
    b_ln2xT = b_qg
    b_k = [B() for _ in range(KR)]
    b_v = [B() for _ in range(KR)]
    b_pmb = [B()]
    b_ps3 = [B(), B()]
    b_sbc = B(); b_rcp = B()
    b_comb = [B() for _ in range(8)]
    b_gsig = B()
    b_union = B()
    b_pooled = b_union; b_pooledT = b_union; b_gkT = B(); b_gvl = B()
    b_x1 = [B(), B()]
    b_scrx1 = [B() for _ in range(NT)]
    b_sT = b_union; b_h = [b_union, b_union]; b_gm = b_union
    b_stat = {"s1": B(), "s5": B(), "s7": B()}
    b_rst = {"s1": B(), "s5": B(), "s7": B()}
    b_scr1 = [B() for _ in range(NJT)]
    b_scr2 = [B() for _ in range(NT)]
    b_arena = B()
    b_prem = B(); b_pmrem = B(); b_reml = B(); b_grbc = B(); b_t1 = B(); b_recr = B()
    b_dbg = B()

    AV_PREM, AV_PM, AV_RECR, AV_REML = 0, 512, 1024, 2048

    dbg_done = []

    def dump(key, ap_fn, rows, cols, col0, reads=()):
        if dbg is None:
            return

        def e(e_):
            return e_.dma_start(out=dbg[0:rows, col0:col0 + cols], in_=ap_fn())
        use("gp", e, reads=list(reads), dma=True, tag=f"dbg_{key}")
        dbg_done.append(key)

    def w_dve(e):
        e.memset(wrm[:, :], 1.0)
        e.memset(wrmb[:, 256:512], 0.0)
        e.memset(epsv[:, :], EPS)
        r = None
        for _ in range(2):
            e.bn_stats(st_s1[:, 0:6], wrm[:, 0:512])
            e.bn_stats(st_s1[:, 6:12], wrm[:, 0:512])
            e.bn_aggr(st_s1[:, 12:14], st_s1[:, 0:12])
            e.reciprocal(wrm[:, 0:1], wrm[:, 1:2])
            e.reciprocal(wrm[:, 0:512], wrm[:, 0:512])
            e.memset(wrm[:, 0:512], 1.0)
            r = e.tensor_scalar(out=wrm[:, 2:3], in0=wrm[:, 3:4], scalar1=wrm[:, 4:5],
                                scalar2=wrm[:, 5:6], op0=ALU.mult, op1=ALU.add)
        return r
    t_dw = use("dve", w_dve, writes=[b_rcp, b_sbc, b_stat["s1"], bw["eps"]], tag="warm_dve")

    def w_pe(e):
        return e.matmul(psA[:, 0:512], wrmb[0:1, 256:384], wrmb[0:1, 0:512], start=True, stop=True)
    t_pw = use("pe", w_pe, reads=[b_sbc], writes=[PB[0]], tag="warm_pe")

    def w_dve2(e):
        e.bn_stats(st_s1[:, 0:6], psA[:, 0:512])
        e.bn_stats(st_s1[:, 6:12], psA[:, 0:512])
        e.bn_aggr(st_s1[:, 12:14], st_s1[:, 0:12])
        return e.reciprocal(wrm[:, 0:512], wrm[:, 0:512])
    t_dw2 = use("dve", w_dve2, reads=[PB[0]], writes=[b_rcp, b_stat["s1"]], tag="warm_dve2")

    def w_act(e):
        r = None
        for _ in range(2):
            e.activation(out=wrm[:, 6:7], in_=wrm[:, 7:8], func=AF.Ln, bias=epsv[:, :], scale=1.0)
            e.activation(out=wrm[:, 6:7], in_=wrm[:, 7:8], func=AF.Exp, bias=0.0, scale=-0.5)
            e.activation(out=wrmb[:, 0:64], in_=psA[:, 0:64], func=AF.Exp, bias=0.0, scale=0.125)
            e.activation(out=wrmb[:, 0:64], in_=wrm[:, 72:136], func=AF.Tanh, bias=0.0, scale=0.5)
            e.activation(out=wrmb[:, 0:64], in_=psA[:, 0:64], func=AF.Relu, bias=wrm[:, 1:2],
                         scale=wrm[:, 2:3])
            r = e.activation(out=wrmb[:, 0:64], in_=psA[:, 0:64], func=AF.Copy, bias=0.0, scale=1.0)
        return r
    t_aw = use("act", w_act, reads=[b_rcp, bw["eps"], PB[0]], writes=[b_rcp, b_sbc], tag="warm_act")

    def w_gp(e):
        e.memset(wrmb[:, 256:512], 1.0)
        e.tensor_tensor(out=wrmb[:, 0:128], in0=wrmb[:, 256:384], in1=wrmb[:, 384:512], op=ALU.add)
        e.tensor_copy(wrmb[:, 0:128], wrmb[:, 256:384])
        e.tensor_tensor(out=wrm[:, 128:256], in0=wrm[:, 256:384], in1=wrmb[:, 384:512], op=ALU.add)
        r = e.tensor_scalar(out=wrmb[:, 0:256], in0=wrm[:, 128:384], scalar1=wrm[:, 0:1],
                            scalar2=wrm[:, 1:2], op0=ALU.mult, op1=ALU.add)
        return r
    t_gw = use("gp", w_gp, reads=[b_rcp], writes=[b_rcp, b_sbc], extra=[t_dw], tag="warm_gp")

    def wload(q, dst, src, bufs, tag):
        return use(q, lambda e: e.dma_start(out=dst, in_=src), writes=bufs, dma=True, tag=tag)

    wload("gp", w_q[:, :].rearrange("p (c e) -> p c e", c=8), wq.rearrange("(c p) e -> p c e", p=128), [bw["w_q"]], "ld_wq")
    wload("gp", w_k[:, :].rearrange("p (c e) -> p c e", c=8), wk.rearrange("(c p) e -> p c e", p=128), [bw["w_k"]], "ld_wk")
    wload("gp", w_v[:, :].rearrange("p (c e) -> p c e", c=8), wv.rearrange("(c p) e -> p c e", p=128), [bw["w_v"]], "ld_wv")
    wload("gp", w_o[:, :].rearrange("p (c e) -> p c e", c=8), w_out.rearrange("(c p) e -> p c e", p=128), [bw["w_o"]], "ld_wo")
    wload("gp", w_gq[:, :].rearrange("p (c e) -> p c e", c=8), wgq.rearrange("(c p) e -> p c e", p=128), [bw["w_gq"]], "ld_wgq")
    wload("gp", arena[:, 0:4096].rearrange("p (c e) -> p c e", c=8), wgk.rearrange("(c p) e -> p c e", p=128), [b_arena], "ld_wgk")
    wload("gp", r_o[:, :].rearrange("p (c e) -> p c e", c=4), r_out.rearrange("(c p) e -> p c e", p=128), [bw["r_o"]], "ld_ro")
    wload("gp", w_in[:, :].rearrange("p (c e) -> p c e", c=8), win.rearrange("(c p) e -> p c e", p=128), [bw["w_in"]], "ld_win")
    wload("gp", w_of[:, :], woutf[:, :], [bw["w_of"]], "ld_wof")
    wload("gp", rrep_s[:, :], rrep[:, :], [bw["rrep"]], "ld_rrep")
    wload("gp", bdh_s[:, :], bdh[:, :], [bw["bdh"]], "ld_bdh")
    wload("gp", gblk[:, :], gateblk8[:, :], [bw["gblk"]], "ld_gblk")
    wload("gp", selg_s[:, :], selg[:, :], [bw["selg"]], "ld_selg")
    wload("gp", mj[:, :], maskj0[:, :], [bw["mj"]], "ld_mj0")
    wload("gp", pm32_v[:, :], pm32[:, :], [b_union], "ld_pm32")
    wload("gp", ident_s[:, :], identc[:, :], [bw["ident"]], "ld_ident")
    wload("gp", ones1_s[:, :], onesr[:, :], [bw["ones1"]], "ld_ones1")
    wload("gp", ones64_s[:, :], ones64c[:, :], [bw["ones64"]], "ld_ones64")
    wload("gp", bout_v[0:1, :], boutv[:, :], [b_union], "ld_bout")
    wload("sp", w1s[:, :], w1scat[:, :], [bw["w1s"]], "ld_w1s")
    wload("sp", b1s[:, :], b1scat[:, :], [bw["b1s"]], "ld_b1s")
    wload("sp", bin_s[:, :], binv[:, :], [bw["bin"]], "ld_bin")
    wload("sp", b2_s[:, :], b2v[:, :], [bw["b2"]], "ld_b2")

    pool_first = {"flag": True}

    def s1_tile(kind, i, xi):
        slot = xi % 2
        lsl = xi % 3
        xap = xa[:, slot * 1024:(slot + 1) * 1024]
        lap = lnst[:, lsl * 1024:(lsl + 1) * 1024]
        src = {"halo": x_halo, "own": x_own, "other": x_other}[kind]
        row = i * 128
        t_ld = use("sp", lambda e, xap=xap, src=src, row=row: e.dma_start(out=xap, in_=src[row:row + 128, :]),
                   writes=[b_xa[slot]], dma=True, tag=f"x_{kind}{i}")

        def e_bn(e, xap=xap):
            e.bn_stats(st_s1[:, 0:6], xap[:, 0:512])
            return e.bn_stats(st_s1[:, 6:12], xap[:, 512:1024])
        t_bn = use("dve", e_bn, reads=[b_xa[slot]], writes=[b_stat["s1"]], extra=[t_dw2], tag=f"bn1_{kind}{i}")
        t_ag = use("dve", lambda e: e.bn_aggr(st_s1[:, 12:14], st_s1[:, 0:12]),
                   reads=[b_stat["s1"]], writes=[b_stat["s1"]], extra=[t_bn], sync_same=True, tag=f"ag1_{kind}{i}")

        def e_rs(e):
            e.activation(out=rst_s1[:, 0:1], in_=st_s1[:, 13:14], func=AF.Ln, bias=epsv[:, :], scale=1.0)
            return e.activation(out=rst_s1[:, 0:1], in_=rst_s1[:, 0:1], func=AF.Exp, bias=0.0, scale=-0.5)
        t_rs = use("act", e_rs, reads=[b_stat["s1"], bw["eps"]], writes=[b_rst["s1"]], extra=[t_aw], tag=f"rs1_{kind}{i}")
        t_nm = use("dve", lambda e: e.tensor_scalar(out=rst_s1[:, 1:2], in0=st_s1[:, 12:13],
                                                    scalar1=rst_s1[:, 0:1], scalar2=-1.0,
                                                    op0=ALU.mult, op1=ALU.mult),
                   reads=[b_stat["s1"], b_rst["s1"]], writes=[b_rst["s1"]], extra=[t_rs], sync_same=True,
                   tag=f"nm1_{kind}{i}")
        t_ap = use("gp", lambda e, xap=xap, lap=lap: e.tensor_scalar(
            out=lap, in0=xap, scalar1=rst_s1[:, 0:1], scalar2=rst_s1[:, 1:2], op0=ALU.mult, op1=ALU.add),
            reads=[b_xa[slot], b_rst["s1"]], writes=[b_lnst[lsl]], extra=[t_nm, t_gw], tag=f"ap1_{kind}{i}")
        if kind != "other":
            jt = i if kind == "halo" else i + 2
            use("sp", lambda e, lap=lap, jt=jt: e.dma_start(out=scr1[jt * 128:(jt + 1) * 128, :], in_=lap),
                reads=[b_lnst[lsl]], writes=[b_scr1[jt]], dma=True, tag=f"w_scr1_{jt}")
        if kind != "halo":
            orow = slice(32, 64) if kind == "own" else slice(0, 32)
            first = pool_first["flag"]
            pool_first["flag"] = False
            last = (kind == "other" and i == NT - 1)

            def e_pool2(e, lap=lap, i=i, orow=orow, kind=kind):
                r = None
                for ds in range(2):
                    r = e.matmul(bank(6 + ds)[orow, :], pm32_v[:, i * 32:(i + 1) * 32],
                                 lap[:, ds * 512:(ds + 1) * 512],
                                 start=(i == 0), stop=(i == NT - 1))
                return r
            use("pe", e_pool2, reads=[b_lnst[lsl], b_union], writes=[PB[6], PB[7]],
                extra=[t_ap], tag=f"pool_{kind}{i}")

    for i in range(2):
        s1_tile("halo", i, i)
    s1_tile("own", 0, 2)
    dump("ln1x0", lambda: lnst[:, 2 * 1024:3 * 1024], 128, 1024, 0, reads=[b_lnst[2]])
    for i in range(1, NT):
        s1_tile("own", i, 2 + i)


    def b1_transpose(g):
        need = [b_scr1[j].w for j in range(4 * g, 4 * g + 6)]

        def e(e_):
            return e_.dma_start_transpose(out=ln1xT[:, :].rearrange("p (c t) -> p c t", c=8),
                                          in_=scr1[g * G:g * G + G + HALO, :])
        tok = use("sp", e, writes=[b_ln1xT], dma=True, tag=f"B1_{g}",
                  extra=[t for t in need if t] + rec.dma_quiesce())
        rec.last_dmat = tok
        return tok

    b1_transpose(0)
    dump("ln1xT", lambda: ln1xT[:, 0:768], 128, 768, 1024, reads=[b_ln1xT])
    for i in range(NT):
        s1_tile("other", i, 2 + NT + i)

    use("dve", lambda e: e.tensor_copy(pooled[0:64, 0:512], bank(6)[0:64, :]),
        reads=[PB[6]], writes=[b_pooled], tag="pool_ev0")
    use("dve", lambda e: e.tensor_copy(pooled[0:64, 512:1024], bank(7)[0:64, :]),
        reads=[PB[7]], writes=[b_pooled], tag="pool_ev1")
    for c in range(8):
        bi = 6 + (c % 2)

        def e_tp(e, c=c, bi=bi):
            return e.transpose(bank(bi).bitcast(BF16)[:, 0:64], pooled[0:64, c * 128:(c + 1) * 128],
                               ident_s[:, :])
        use("pe", e_tp, reads=[b_pooled, bw["ident"]], writes=[PB[bi]], tag=f"poolT{c}")
        use("dve", lambda e, c=c, bi=bi: e.tensor_copy(pooledT[:, c * 64:(c + 1) * 64],
                                                       bank(bi).bitcast(BF16)[:, 0:64]),
            reads=[PB[bi]], writes=[b_pooledT], tag=f"poolTe{c}")
    for gc in range(4):
        def e_gk(e, gc=gc):
            r = None
            for kc in range(8):
                r = e.matmul(bank(6)[:, 0:64], arena[:, kc * 512 + gc * 128: kc * 512 + gc * 128 + 128],
                             pooledT[:, kc * 64:(kc + 1) * 64], start=(kc == 0), stop=(kc == 7))
            return r
        use("pe", e_gk, reads=[b_arena, b_pooledT], writes=[PB[6]], tag=f"gk{gc}")
        use("dve", lambda e, gc=gc: e.tensor_copy(gkT[:, gc * 64:(gc + 1) * 64], bank(6)[:, 0:64]),
            reads=[PB[6]], writes=[b_gkT], tag=f"gke{gc}")

    wload("gp", arena[:, 0:4096].rearrange("p (c e) -> p c e", c=8),
          wgv.rearrange("(c p) e -> p c e", p=128), [b_arena], "ld_wgv")

    def e_gv(e):
        r = None
        for kc in range(8):
            r = e.matmul(bank(7)[0:64, :], pooledT[:, kc * 64:(kc + 1) * 64],
                         arena[:, kc * 512:(kc + 1) * 512], start=(kc == 0), stop=(kc == 7))
        return r
    use("pe", e_gv, reads=[b_arena, b_pooledT], writes=[PB[7]], tag="gv")
    use("dve", lambda e: e.tensor_copy(gv_l[0:64, :], bank(7)[0:64, :]),
        reads=[PB[7]], writes=[b_gvl], tag="gve")
    use("pe", lambda e: e.matmul(bank(7)[64:128, :], ident_s[:, :], gv_l[0:64, :], start=True, stop=True),
        reads=[b_gvl, bw["ident"]], writes=[PB[7]], tag="gv64")
    use("dve", lambda e: e.tensor_copy(gv_l[64:128, :], bank(7)[64:128, :]),
        reads=[PB[7]], writes=[b_gvl], tag="gv64e")
    dump("pooled", lambda: pooled[0:64, :], 64, 1024, 7936, reads=[b_union])

    GEOM = [(-2, 0, 1, 0, 0, 0), (-1, 0, 2, 0, 128, 128), (0, 0, 3, 1, 0, 384),
            (1, 1, 3, 2, 0, 768), (2, 2, 2, 3, 0, 1152), (3, 3, 1, 3, 256, 1408)]

    def s2_group(g):
        for ec in range(8):
            bi = 4 + (ec % 2)

            def e_q(e, ec=ec, bi=bi):
                r = None
                for kc in range(8):
                    r = e.matmul(bank(bi), w_q[:, kc * 1024 + ec * 128: kc * 1024 + (ec + 1) * 128],
                                 ln1xT[:, kc * 768 + 256: kc * 768 + 768], start=(kc == 0), stop=(kc == 7))
                return r
            use("pe", e_q, reads=[bw["w_q"], b_ln1xT], writes=[PB[bi]], tag=f"q{g}_{ec}")
            use("dve", lambda e, ec=ec, bi=bi: e.tensor_copy(q_g[:, ec * 512:(ec + 1) * 512], bank(bi)),
                reads=[PB[bi]], writes=[b_qg], tag=f"qe{g}_{ec}")
        jts_all = list(range(0, 6)) if g == 0 else list(range(4 * g + 2, 4 * g + 6))
        col0_all = 0 if g == 0 else 256
        kchunks = [jts_all[i:i + 4] for i in range(0, len(jts_all), 4)]
        for ci, jts in enumerate(kchunks):
            col0 = col0_all + ci * 512
            nj = len(jts)
            for ec in range(8):
                bi = 4 + (ec % 2)

                def e_k(e, ec=ec, bi=bi, col0=col0, nj=nj):
                    r = None
                    for kc in range(8):
                        r = e.matmul(bank(bi)[:, 0:nj * 128], w_k[:, kc * 1024 + ec * 128: kc * 1024 + (ec + 1) * 128],
                                     ln1xT[:, kc * 768 + col0: kc * 768 + col0 + nj * 128], start=(kc == 0), stop=(kc == 7))
                    return r
                use("pe", e_k, reads=[bw["w_k"], b_ln1xT], writes=[PB[bi]], tag=f"k{g}_{ci}_{ec}")

                def e_ke(e, ec=ec, bi=bi, jts=jts, nj=nj):
                    r = None
                    ji = 0
                    while ji < nj:
                        sl = jts[ji] % KR
                        run = 1
                        while ji + run < nj and (jts[ji + run] % KR) == sl + run:
                            run += 1
                        r = e.tensor_copy(k_sb[:, ec * (KR * 128) + sl * 128: ec * (KR * 128) + (sl + run) * 128],
                                          bank(bi)[:, ji * 128:(ji + run) * 128])
                        ji += run
                    return r
                use("dve", e_ke, reads=[PB[bi]], writes=[b_k[jt % KR] for jt in jts], tag=f"ke{g}_{ci}_{ec}")
        for ji, jt in enumerate(jts_all):
            lcol = col0_all + ji * 128
            for vs in range(2):
                bi = 4 + vs

                def e_v(e, jt=jt, lcol=lcol, vs=vs, bi=bi):
                    r = None
                    for kc in range(8):
                        r = e.matmul(bank(bi), ln1xT[:, kc * 768 + lcol: kc * 768 + lcol + 128],
                                     w_v[:, kc * 1024 + vs * 512: kc * 1024 + (vs + 1) * 512],
                                     start=(kc == 0), stop=(kc == 7))
                    return r
                use("pe", e_v, reads=[bw["w_v"], b_ln1xT], writes=[PB[bi]], tag=f"v{g}_{jt}_{vs}")
                use("dve", lambda e, jt=jt, vs=vs, bi=bi: e.tensor_copy(
                    v_sb[:, (jt % KR) * 1024 + vs * 512: (jt % KR) * 1024 + (vs + 1) * 512], bank(bi)),
                    reads=[PB[bi]], writes=[b_v[jt % KR]], tag=f"ve{g}_{jt}_{vs}")
        for pc in range(4):
            bi = 4 + (pc % 2)

            def e_gq(e, pc=pc, bi=bi):
                r = None
                for kc in range(8):
                    r = e.matmul(bank(bi), w_gq[:, kc * 512 + pc * 128: kc * 512 + (pc + 1) * 128],
                                 ln1xT[:, kc * 768 + 256: kc * 768 + 768], start=(kc == 0), stop=(kc == 7))
                return r
            use("pe", e_gq, reads=[bw["w_gq"], b_ln1xT], writes=[PB[bi]], tag=f"gq{g}_{pc}")
            use("act", lambda e, pc=pc, bi=bi: e.activation(out=gq_g[:, pc * 512:(pc + 1) * 512],
                                                            in_=bank(bi), func=AF.Copy, bias=0.0, scale=1.0),
                reads=[PB[bi]], writes=[b_gqg], extra=[t_aw], tag=f"gqe{g}_{pc}")

        def e_gate(e):
            r = None
            for c in range(8):
                e.matmul(bank(6)[0:16, :], gblk[:, c * 32:c * 32 + 16],
                         q_g[:, c * 512:(c + 1) * 512], start=(c == 0), stop=(c == 7))
            for c in range(8):
                r = e.matmul(bank(3)[0:16, :], gblk[:, c * 32 + 16:(c + 1) * 32],
                             q_g[:, c * 512:(c + 1) * 512], start=(c == 0), stop=(c == 7))
            return r
        use("pe", e_gate, reads=[bw["gblk"], b_qg], writes=[PB[6], PB[3]], tag=f"gate{g}")

        def e_sig(e):
            e.activation(out=gsig[0:16, 0:512], in_=bank(6)[0:16, :], func=AF.Tanh, bias=0.0, scale=0.5)
            return e.activation(out=gsig[0:16, 512:1024], in_=bank(3)[0:16, :], func=AF.Tanh,
                                bias=0.0, scale=0.5)
        use("act", e_sig, reads=[PB[6], PB[3]], writes=[b_gsig], extra=[t_aw], tag=f"sig{g}")
        use("dve", lambda e: e.tensor_scalar(out=gsig[0:16, :], in0=gsig[0:16, :], scalar1=0.5,
                                             scalar2=0.5, op0=ALU.mult, op1=ALU.add),
            reads=[b_gsig], writes=[b_gsig], tag=f"sigf{g}")

    def s3_local(g, dumps=False):
        mjbuf = bw["mj"]
        for pc in range(8):
            for par in (0, 1):
                h = 2 * pc + par
                rows = slice(par * 64, (par + 1) * 64)
                bb = 0 if par == 0 else 4

                def e_sc(e, pc=pc, rows=rows, bb=bb, g=g):
                    r = None
                    for (dk, q0, qn, bi, boff, pmoff) in GEOM:
                        jt = 4 * g + 2 + dk
                        sl = jt % KR
                        r = e.matmul(bank(bb + bi)[:, boff:boff + qn * 128],
                                     k_sb[rows, pc * (KR * 128) + sl * 128: pc * (KR * 128) + (sl + 1) * 128],
                                     q_g[rows, pc * 512 + q0 * 128: pc * 512 + (q0 + qn) * 128],
                                     start=True, stop=True)
                    return r
                kreads = [b_k[(4 * g + 2 + dk) % KR] for (dk, *_r) in GEOM]
                use("pe", e_sc, reads=kreads + [b_qg],
                    writes=[PB[bb], PB[bb + 1], PB[bb + 2], PB[bb + 3]], tag=f"sc{g}_{h}")
            for par in (0, 1):
                h = 2 * pc + par
                bb = 0 if par == 0 else 4
                tens, off0 = bank_t(bb)

                def e_exp(e, tens=tens, off0=off0):
                    src = bass.AP(tens, off0, [[2048, 128], [512, 4], [1, 384]])
                    return e.activation(out=pmb[:, :], in_=src, func=AF.Exp, bias=0.0, scale=0.125)
                use("act", e_exp, reads=[PB[bb], PB[bb + 1], PB[bb + 2], PB[bb + 3]],
                    writes=[b_pmb[0]], extra=[t_aw], tag=f"exp{g}_{h}")
                use("dve", lambda e: e.tensor_tensor(out=pmb[:, :], in0=pmb[:, :], in1=mj[:, :],
                                                     op=ALU.mult),
                    reads=[b_pmb[0], mjbuf], writes=[b_pmb[0]], tag=f"mk{g}_{h}")
                if dumps and pc == 0 and par == 0:
                    dump("pm", lambda: pmb[:, :], 128, 1536, 3840, reads=[b_pmb[0]])

                def e_ps3(e, par=par):
                    o = par * 512
                    e.tensor_copy(bass.AP(ps3, o, [[1024, 128], [128, 4], [1, 128]]),
                                  bass.AP(pmb, 128, [[1536, 128], [384, 4], [1, 128]]))
                    e.tensor_tensor(out=bass.AP(ps3, o, [[1024, 128], [128, 3], [1, 128]]),
                                    in0=bass.AP(ps3, o, [[1024, 128], [128, 3], [1, 128]]),
                                    in1=bass.AP(pmb, 384, [[1536, 128], [384, 3], [1, 128]]), op=ALU.add)
                    e.tensor_tensor(out=ps3[:, o + 384:o + 512], in0=ps3[:, o + 384:o + 512],
                                    in1=pmb[:, 1408:1536], op=ALU.add)
                    e.tensor_tensor(out=ps3[:, o:o + 128], in0=ps3[:, o:o + 128],
                                    in1=pmb[:, 0:128], op=ALU.add)
                    return e.tensor_tensor(out=bass.AP(ps3, o + 128, [[1024, 128], [128, 3], [1, 128]]),
                                           in0=bass.AP(ps3, o + 128, [[1024, 128], [128, 3], [1, 128]]),
                                           in1=bass.AP(pmb, 256, [[1536, 128], [384, 3], [1, 128]]), op=ALU.add)
                use("gp", e_ps3, reads=[b_pmb[0]], writes=[b_ps3[par]], extra=[t_gw], tag=f"ps3{g}_{h}")

                def e_pv(e, pc=pc, par=par, g=g):
                    r = None
                    h = 2 * pc + par
                    orow = slice(0, 64) if par == 0 else slice(64, 128)
                    for (dk, q0, qn, bi, boff, pmoff) in GEOM:
                        jt = 4 * g + 2 + dk
                        sl = jt % KR
                        r = e.matmul(bank(0)[orow, q0 * 128:(q0 + qn) * 128],
                                     v_sb[:, sl * 1024 + h * 64: sl * 1024 + (h + 1) * 64],
                                     pmb[:, pmoff: pmoff + qn * 128],
                                     start=(dk == -2), stop=(dk == 3))
                    return r
                vreads = [b_v[(4 * g + 2 + dk) % KR] for (dk, *_r) in GEOM]
                use("pe", e_pv, reads=vreads + [b_pmb[0]], writes=[PB[0]], tag=f"pv{g}_{h}")

            def e_den(e):
                e.matmul(bank(1)[0:64, :], ones64_s[:, :], ps3[:, 0:512], start=True, stop=True)
                return e.matmul(bank(1)[64:128, :], ones64_s[:, :], ps3[:, 512:1024], start=True, stop=True)
            use("pe", e_den, reads=[b_ps3[0], b_ps3[1], bw["ones64"]], writes=[PB[1]], tag=f"dn{g}_{pc}")
            use("dve", lambda e: e.reciprocal(rcp[:, :], bank(1)),
                reads=[PB[1]], writes=[b_rcp], tag=f"rc{g}_{pc}")
            if dumps and pc == 0:
                dump("rcp", lambda: rcp[:, :], 128, 512, 8960, reads=[b_rcp])
                dump("ps3d", lambda: ps3[:, :], 128, 1024, 9984, reads=[b_ps3[0], b_ps3[1]])

            def e_glbc(e, pc=pc):
                return e.matmul(bank(2), selg_s[:, pc * 128:(pc + 1) * 128], gsig[0:16, 0:512],
                                start=True, stop=True)
            use("pe", e_glbc, reads=[bw["selg"], b_gsig], writes=[PB[2]], tag=f"glbc{g}_{pc}")
            use("dve", lambda e: e.tensor_tensor(out=sbc[:, :], in0=bank(2), in1=rcp[:, :], op=ALU.mult),
                reads=[PB[2], b_rcp], writes=[b_sbc], sync_same=True, tag=f"sbc{g}_{pc}")
            if dumps and pc == 0:
                dump("sbcd", lambda: sbc[:, :], 128, 512, 9472, reads=[b_sbc])
            use("dve", lambda e, pc=pc: e.tensor_tensor(out=comb[:, pc * 512:(pc + 1) * 512],
                                                        in0=bank(0), in1=sbc[:, :], op=ALU.mult),
                reads=[PB[0], b_sbc], writes=[b_comb[pc]], tag=f"cmb{g}_{pc}")
            if dumps and pc == 0:
                dump("comb0", lambda: comb[:, 0:512], 128, 512, 5376, reads=[b_comb[0]])

    def s3_remote(g):
        use("gp", lambda e, g=g: e.dma_start(out=mr_s[:, :], in_=maskr[:, g * 512:(g + 1) * 512]),
            writes=[bw["mr"]], dma=True, tag=f"ld_mr{g}")
        for pc in range(4):
            def e_rsc(e, pc=pc):
                e.matmul(bank(3)[0:64, :], gkT[0:64, pc * 64:(pc + 1) * 64],
                         gq_g[0:64, pc * 512:(pc + 1) * 512], start=True, stop=True)
                return e.matmul(bank(3)[64:128, :], gkT[64:128, pc * 64:(pc + 1) * 64],
                                gq_g[64:128, pc * 512:(pc + 1) * 512], start=True, stop=True)
            use("pe", e_rsc, reads=[b_gkT, b_gqg], writes=[PB[3]], tag=f"rsc{g}_{pc}")
            use("act", lambda e: e.activation(out=arena[:, AV_PREM:AV_PREM + 512], in_=bank(3),
                                              func=AF.Exp, bias=0.0, scale=0.125),
                reads=[PB[3]], writes=[b_prem], extra=[t_aw], tag=f"rexp{g}_{pc}")
            use("dve", lambda e: e.tensor_tensor(out=arena[:, AV_PM:AV_PM + 512],
                                                 in0=arena[:, AV_PREM:AV_PREM + 512],
                                                 in1=mr_s[:, :], op=ALU.mult),
                reads=[b_prem, bw["mr"]], writes=[b_pmrem], tag=f"rmk{g}_{pc}")

            def e_rdn(e):
                e.matmul(bank(2)[0:64, :], ones64_s[0:64, :], arena[0:64, AV_PM:AV_PM + 512],
                         start=True, stop=True)
                return e.matmul(bank(2)[64:128, :], ones64_s[64:128, :], arena[64:128, AV_PM:AV_PM + 512],
                                start=True, stop=True)
            use("pe", e_rdn, reads=[b_pmrem, bw["ones64"]], writes=[PB[2]], tag=f"rdn{g}_{pc}")
            use("dve", lambda e: e.reciprocal(arena[:, AV_RECR:AV_RECR + 1024].bitcast(F32), bank(2)),
                reads=[PB[2]], writes=[b_recr], tag=f"rrc{g}_{pc}")

            def e_rpv(e, pc=pc):
                e.matmul(bank(3)[0:64, :], gv_l[0:64, (2 * pc) * 64:(2 * pc + 1) * 64],
                         arena[0:64, AV_PM:AV_PM + 512], start=True, stop=True)
                return e.matmul(bank(3)[64:128, :], gv_l[64:128, (2 * pc + 1) * 64:(2 * pc + 2) * 64],
                                arena[64:128, AV_PM:AV_PM + 512], start=True, stop=True)
            use("pe", e_rpv, reads=[b_gvl, b_pmrem], writes=[PB[3]], tag=f"rpv{g}_{pc}")
            use("dve", lambda e, pc=pc: e.tensor_tensor(
                out=arena[:, AV_REML + pc * 512:AV_REML + (pc + 1) * 512], in0=bank(3),
                in1=arena[:, AV_RECR:AV_RECR + 1024].bitcast(F32), op=ALU.mult),
                reads=[PB[3], b_recr], writes=[b_reml], sync_same=True, tag=f"rle{g}_{pc}")
        for oc in range(8):
            def e_ro(e, oc=oc):
                r = None
                for pc in range(4):
                    r = e.matmul(bank(3), r_o[:, pc * 1024 + oc * 128: pc * 1024 + (oc + 1) * 128],
                                 arena[:, AV_REML + pc * 512:AV_REML + (pc + 1) * 512],
                                 start=(pc == 0), stop=(pc == 3))
                return r
            use("pe", e_ro, reads=[bw["r_o"], b_reml], writes=[PB[3]], tag=f"ro{g}_{oc}")

            def e_grbc(e, oc=oc):
                return e.matmul(bank(2), selg_s[:, oc * 128:(oc + 1) * 128], gsig[0:16, 512:1024],
                                start=True, stop=True)
            use("pe", e_grbc, reads=[bw["selg"], b_gsig], writes=[PB[2]], tag=f"grbc{g}_{oc}")
            use("dve", lambda e: e.tensor_copy(grbc_s[:, :], bank(2)),
                reads=[PB[2]], writes=[b_grbc], tag=f"gre{g}_{oc}")
            use("dve", lambda e: e.tensor_tensor(out=t1_s[:, :], in0=bank(3), in1=grbc_s[:, :],
                                                 op=ALU.mult),
                reads=[PB[3], b_grbc], writes=[b_t1], tag=f"t1{g}_{oc}")
            use("gp", lambda e, oc=oc: e.tensor_tensor(out=comb[:, oc * 512:(oc + 1) * 512],
                                                       in0=comb[:, oc * 512:(oc + 1) * 512],
                                                       in1=t1_s[:, :], op=ALU.add),
                reads=[b_t1, b_comb[oc]], writes=[b_comb[oc]], extra=[t_gw], tag=f"cadd{g}_{oc}")

    def ln_tile(stage, src_is_psum, src_aps, src_bufs, rst_col, extra=()):
        stt = {"s1": st_s1, "s5": st_s5, "s7": st_s7}[stage]
        rt = {"s1": rst_s1, "s5": rst_s5, "s7": rst_s7}[stage]
        sbuf_b, rb = b_stat[stage], b_rst[stage]
        c = rst_col

        def e_bn(e):
            e.bn_stats(stt[:, 0:6], src_aps[0])
            return e.bn_stats(stt[:, 6:12], src_aps[1])
        t1 = use("dve", e_bn, reads=list(src_bufs), writes=[sbuf_b], extra=list(extra), tag=f"bn_{stage}")
        t2 = use("dve", lambda e: e.bn_aggr(stt[:, 12:14], stt[:, 0:12]),
                 reads=[sbuf_b], writes=[sbuf_b], extra=[t1], sync_same=True, tag=f"ag_{stage}")

        def e_rs(e):
            e.activation(out=rt[:, c:c + 1], in_=stt[:, 13:14], func=AF.Ln, bias=epsv[:, :], scale=1.0)
            return e.activation(out=rt[:, c:c + 1], in_=rt[:, c:c + 1], func=AF.Exp, bias=0.0, scale=-0.5)
        t3 = use("act", e_rs, reads=[sbuf_b, bw["eps"]], writes=[rb], extra=[t_aw], tag=f"rs_{stage}")
        t4 = use("dve", lambda e: e.tensor_scalar(out=rt[:, c + 1:c + 2], in0=stt[:, 12:13],
                                                  scalar1=rt[:, c:c + 1], scalar2=-1.0,
                                                  op0=ALU.mult, op1=ALU.mult),
                 reads=[sbuf_b, rb], writes=[rb], extra=[t3], sync_same=True, tag=f"nm_{stage}")
        return t4, rt, c

    def s45_group(g, dumps=False):
        for ts in range(4):
            xslot = ts % 2
            xap = xa[:, xslot * 1024:(xslot + 1) * 1024]
            row = g * G + ts * 128
            t_xb = use("sp", lambda e, xap=xap, row=row: e.dma_start(out=xap, in_=x_own[row:row + 128, :]),
                       writes=[b_xa[xslot]], dma=True, tag=f"xr{g}_{ts}")
            for dh in range(2):
                bi = 4 + dh

                def e_wo(e, ts=ts, dh=dh, bi=bi):
                    r = None
                    for pc in range(8):
                        r = e.matmul(bank(bi), comb[:, pc * 512 + ts * 128: pc * 512 + (ts + 1) * 128],
                                     w_o[:, pc * 1024 + dh * 512: pc * 1024 + (dh + 1) * 512],
                                     start=(pc == 0), stop=(pc == 7))
                    return r
                use("pe", e_wo, reads=[bw["w_o"]] + [b_comb[pc] for pc in range(8)],
                    writes=[PB[bi]], tag=f"wo{g}_{ts}_{dh}")
            t4, rt, c = ln_tile("s5", True, [bank(4), bank(5)], [PB[4], PB[5]], 0)
            if dumps and ts == 0:
                use("dve", lambda e: e.tensor_copy(rcp[:, :], bank(4)), reads=[PB[4]],
                    writes=[b_rcp], tag="dbgcp")
                dump("attnout", lambda: rcp[:, :], 128, 512, 12032, reads=[b_rcp])
                dump("rst5", lambda: rst_s5[:, :], 128, 8, 12544, reads=[b_rst["s5"]])
            lsl = ts % 3
            lap = lnst[:, lsl * 1024:(lsl + 1) * 1024]

            def e_ap(e, lap=lap):
                e.tensor_scalar(out=lap[:, 0:512], in0=bank(4), scalar1=rt[:, c:c + 1],
                                scalar2=rt[:, c + 1:c + 2], op0=ALU.mult, op1=ALU.add)
                return e.tensor_scalar(out=lap[:, 512:1024], in0=bank(5), scalar1=rt[:, c:c + 1],
                                       scalar2=rt[:, c + 1:c + 2], op0=ALU.mult, op1=ALU.add)
            t_ap = use("dve", e_ap, reads=[PB[4], PB[5], b_rst["s5"]], writes=[b_lnst[lsl]],
                       extra=[t4], sync_same=True, tag=f"apA{g}_{ts}")
            if dumps and ts == 0:
                dump("lna", lambda lap=lap: lap, 128, 1024, 11008, reads=[b_lnst[lsl]])
            x1slot = ts % 2
            x1ap = x1_sb[:, x1slot * 1024:(x1slot + 1) * 1024]
            t_x1 = use("gp", lambda e, xap=xap, x1ap=x1ap, lap=lap: e.tensor_tensor(
                out=x1ap, in0=xap, in1=lap, op=ALU.add),
                reads=[b_xa[xslot], b_lnst[lsl]], writes=[b_x1[x1slot]], extra=[t_gw, t_xb], tag=f"x1{g}_{ts}")
            ti = g * 4 + ts
            use("sp", lambda e, x1ap=x1ap, row=row: e.dma_start(out=scrx1[row:row + 128, :], in_=x1ap),
                reads=[b_x1[x1slot]], writes=[b_scrx1[ti]], dma=True, tag=f"wx1_{ti}")
            if dumps and ts == 0:
                dump("x1", lambda x1ap=x1ap: x1ap, 128, 1024, 6912, reads=[b_x1[0]])
            t4b, rt2, c2 = ln_tile("s5", False, [x1ap[:, 0:512], x1ap[:, 512:1024]], [b_x1[x1slot]], 2)

            def e_ap2(e, lap=lap, x1ap=x1ap):
                return e.tensor_scalar(out=lap, in0=x1ap, scalar1=rt2[:, c2:c2 + 1],
                                       scalar2=rt2[:, c2 + 1:c2 + 2], op0=ALU.mult, op1=ALU.add)
            use("dve", e_ap2, reads=[b_x1[x1slot], b_rst["s5"]], writes=[b_lnst[lsl]],
                extra=[t4b], sync_same=True, tag=f"ap2{g}_{ts}")
            use("sp", lambda e, lap=lap, row=row: e.dma_start(out=scr2[row:row + 128, :], in_=lap),
                reads=[b_lnst[lsl]], writes=[b_scr2[ti]], dma=True, tag=f"wscr2_{ti}")

    def b3_transpose(g):
        need = [b_scr2[g * 4 + i].w for i in range(4)]

        def e(e_):
            return e_.dma_start_transpose(out=ln2xT[:, :].rearrange("p (c t) -> p c t", c=8),
                                          in_=scr2[g * G:(g + 1) * G, :])
        tok = use("sp", e, writes=[b_ln2xT], dma=True, tag=f"B3_{g}",
                  extra=[t for t in need if t] + rec.dma_quiesce())
        rec.last_dmat = tok
        return tok

    def s6_group(g):
        def e_sT(e):
            r = None
            for kc in range(8):
                r = e.matmul(bank(6)[0:64, :], w_in[:, kc * 64:(kc + 1) * 64],
                             ln2xT[:, kc * 512:(kc + 1) * 512], start=(kc == 0), stop=(kc == 7))
            return r
        use("pe", e_sT, reads=[bw["w_in"], b_ln2xT], writes=[PB[6]], tag=f"sT{g}")
        use("dve", lambda e: e.tensor_scalar(out=sT_v[0:64, :], in0=bank(6)[0:64, :],
                                             scalar1=bin_s[:, :], scalar2=None, op0=ALU.add),
            reads=[PB[6], bw["bin"]], writes=[b_sT], tag=f"sTe{g}")
        for k in range(8):
            bi = 4 + (k % 2)
            hs = k % 2

            def e_rep(e, k=k, bi=bi):
                return e.matmul(bank(bi), rrep_s[:, k * 128:(k + 1) * 128], sT_v[0:64, :],
                                start=True, stop=True)
            use("pe", e_rep, reads=[bw["rrep"], b_sT], writes=[PB[bi]], tag=f"rep{g}_{k}")
            use("act", lambda e, k=k, bi=bi, hs=hs: e.activation(
                out=h_v[:, hs * 512:(hs + 1) * 512], in_=bank(bi), func=AF.Relu,
                bias=b1s[:, k:k + 1], scale=w1s[:, k:k + 1]),
                reads=[PB[bi], bw["w1s"], bw["b1s"]], writes=[b_h[hs]], extra=[t_aw], tag=f"relu{g}_{k}")

            def e_bd(e, k=k, hs=hs):
                return e.matmul(bank(7)[0:64, :], bdh_s[:, k * 64:(k + 1) * 64],
                                h_v[:, hs * 512:(hs + 1) * 512], start=(k == 0), stop=(k == 7))
            use("pe", e_bd, reads=[bw["bdh"], b_h[hs]], writes=[PB[7]], tag=f"bd{g}_{k}")
        use("dve", lambda e: e.tensor_scalar(out=gm_v[0:64, :], in0=bank(7)[0:64, :],
                                             scalar1=b2_s[:, :], scalar2=None, op0=ALU.add),
            reads=[PB[7], bw["b2"]], writes=[b_gm], tag=f"gme{g}")
        for ts in range(4):
            row = g * G + ts * 128
            x1slot = ts % 2
            x1ap = x1_sb[:, x1slot * 1024:(x1slot + 1) * 1024]
            ti = g * 4 + ts
            t_x1r = use("sp", lambda e, x1ap=x1ap, row=row: e.dma_start(out=x1ap, in_=scrx1[row:row + 128, :]),
                        reads=[b_scrx1[ti]], writes=[b_x1[x1slot]], dma=True, tag=f"x1r{g}_{ts}")
            for dh in range(2):
                bi = 4 + dh

                def e_f(e, ts=ts, dh=dh, bi=bi):
                    e.matmul(bank(bi), gm_v[0:64, ts * 128:(ts + 1) * 128],
                             w_of[0:64, dh * 512:(dh + 1) * 512], start=True, stop=False)
                    return e.matmul(bank(bi), ones1_s[:, :], bout_v[0:1, dh * 512:(dh + 1) * 512],
                                    start=False, stop=True)
                use("pe", e_f, reads=[b_gm, bw["w_of"], bw["ones1"], b_union],
                    writes=[PB[bi]], tag=f"f{g}_{ts}_{dh}")
            t4, rt, c = ln_tile("s7", True, [bank(4), bank(5)], [PB[4], PB[5]], 0)
            lsl = ts % 3
            lap = lnst[:, lsl * 1024:(lsl + 1) * 1024]

            def e_ap(e, lap=lap):
                e.tensor_scalar(out=lap[:, 0:512], in0=bank(4), scalar1=rt[:, c:c + 1],
                                scalar2=rt[:, c + 1:c + 2], op0=ALU.mult, op1=ALU.add)
                return e.tensor_scalar(out=lap[:, 512:1024], in0=bank(5), scalar1=rt[:, c:c + 1],
                                       scalar2=rt[:, c + 1:c + 2], op0=ALU.mult, op1=ALU.add)
            t_ap = use("dve", e_ap, reads=[PB[4], PB[5], b_rst["s7"]], writes=[b_lnst[lsl]],
                       extra=[t4], sync_same=True, tag=f"apF{g}_{ts}")
            yslot = ts % 2
            yap = xa[:, yslot * 1024:(yslot + 1) * 1024]
            use("gp", lambda e, x1ap=x1ap, yap=yap, lap=lap: e.tensor_tensor(
                out=yap, in0=x1ap, in1=lap, op=ALU.add),
                reads=[b_x1[x1slot], b_lnst[lsl]], writes=[b_xa[yslot]], extra=[t_gw, t_x1r], tag=f"yt{g}_{ts}")
            use("sp", lambda e, yap=yap, row=row: e.dma_start(out=y[row:row + 128, :], in_=yap),
                reads=[b_xa[yslot]], dma=True, tag=f"wy_{ti}")

    for g in range(NGRP):
        if g > 0:
            b1_transpose(g)
        s2_group(g)
        if g == 0:
            dump("qg", lambda: q_g[:, 0:1024], 128, 1024, 1792, reads=[b_qg])
            dump("gate", lambda: gsig[0:16, 0:1024], 16, 1024, 2816, reads=[b_gsig])
        s3_local(g, dumps=(g == 0))
        if g == 0:
            use("gp", lambda e: e.dma_start(out=mj[:, :], in_=maskj[:, :]),
                writes=[bw["mj"]], dma=True, tag="ld_mj")
        s3_remote(g)
        if g == 0:
            dump("reml", lambda: arena[:, AV_REML:AV_REML + 512], 128, 512, 6400, reads=[b_reml])
            dump("combF", lambda: comb[:, 0:512], 128, 512, 5888, reads=[b_comb[0]])
        s45_group(g, dumps=(g == 0))
        b3_transpose(g)
        s6_group(g)

    final_waits = [(s, rec.cnt[s]) for s in Rec.DSEM.values()]
    comp_waits = [(s, rec.cnt[s]) for s in Rec.CSEM.values() if rec.cnt[s] > 0 and s != "s_sp"]

    engmap = {"pe": "tensor", "act": "scalar", "dve": "vector", "gp": "gpsimd", "sp": "sync"}
    with nc.Block() as block:
        for q in Rec.QUEUES:
            def mk(q):
                def body(e):
                    hw = {}
                    for (emit, deps, sem, inc, tag) in rec.ops[q]:
                        for (s, v) in deps:
                            if hw.get(s, 0) >= v:
                                continue
                            e.wait_ge(sems[s], v)
                            hw[s] = v
                        inst = emit(e)
                        if isinstance(inst, tuple):
                            inst = inst[-1]
                        inst.then_inc(sems[sem], inc)
                    if q == "sp":
                        for (s, v) in comp_waits + final_waits:
                            if v > 0 and hw.get(s, 0) < v:
                                e.wait_ge(sems[s], v)
                return body
            getattr(block, engmap[q])(mk(q))
    st.close()
    return nc


def _bf(a):
    return np.asarray(a, dtype=np.float32)


def build_consts(W1, b1, W2, glw, grw):
    c = {}
    gateblk8 = np.zeros((128, 8 * 32), np.float32)
    for pc in range(8):
        for par in range(2):
            h = 2 * pc + par
            rows = slice(par * 64, (par + 1) * 64)
            gateblk8[rows, pc * 32 + h] = glw[h]
            gateblk8[rows, pc * 32 + 16 + h] = grw[h]
    c["gateblk8"] = gateblk8
    selgm = np.zeros((16, 8 * 128), np.float32)
    for pc in range(8):
        for par in range(2):
            selgm[2 * pc + par, pc * 128 + par * 64: pc * 128 + (par + 1) * 64] = 1.0
    c["selg"] = selgm
    rrep = np.zeros((64, 8 * 128), np.float32)
    for k in range(8):
        for mm in range(8):
            m = 8 * k + mm
            for i in range(16):
                rrep[m, k * 128 + mm * 16 + i] = 1.0
    c["rrep"] = rrep
    bdh = np.zeros((128, 8 * 64), np.float32)
    w1sc = np.zeros((128, 8), np.float32)
    b1sc = np.zeros((128, 8), np.float32)
    for k in range(8):
        for mm in range(8):
            m = 8 * k + mm
            for i in range(16):
                bdh[mm * 16 + i, k * 64 + m] = W2[m, i]
                w1sc[mm * 16 + i, k] = W1[m, i]
                b1sc[mm * 16 + i, k] = b1[m, i]
    c["bdh"] = bdh
    c["w1scat"] = w1sc
    c["b1scat"] = b1sc
    jj = np.arange(128)[:, None]
    tt = np.arange(128)[None, :]
    Lm = (jj > tt).astype(np.float32)
    Um = (jj <= tt).astype(np.float32)
    Fm = np.ones((128, 128), np.float32)
    mjc = np.concatenate([Lm, Fm, Lm, Um, Fm, Lm, Um, Fm, Lm, Um, Fm, Um], axis=1)
    c["maskj"] = mjc
    mj0 = mjc.copy()
    mj0[:, 0:384] = 0.0
    c["maskj0_first"] = mj0
    c["maskj0_second"] = mjc
    tglob_first = np.arange(T)
    slot = np.arange(64)
    def _mr(second):
        m = np.zeros((64, T), np.float32)
        for s in range(64):
            if second:
                valid = (s < 32) | (s - 32 <= (np.arange(T) // 64))
            else:
                valid = (s >= 32) & (s - 32 <= (np.arange(T) // 64))
            m[s] = valid.astype(np.float32)
        return np.concatenate([m, m], axis=0)
    c["maskr_first"] = _mr(False)
    c["maskr_second"] = _mr(True)
    pm = np.zeros((128, 16 * 32), np.float32)
    for i in range(16):
        for t in range(128):
            pm[t, i * 32 + 2 * i + t // 64] = 1.0 / 64.0
    c["pm32"] = pm
    c["identc"] = np.eye(64, dtype=np.float32)
    c["onesr"] = np.ones((1, 128), np.float32)
    c["ones64c"] = np.ones((128, 64), np.float32)
    return c


_NC_CACHE = {}


def kernel(L0_attn_x, L0_attn_wq, L0_attn_wk, L0_attn_wv, L0_attn_wgq, L0_attn_wgk,
           L0_attn_wgv, L0_attn_r_out_w, L0_attn_gate_local_w, L0_attn_gate_remote_w,
           L0_attn_w_out, L0_ffn_Win, L0_ffn_bin_, L0_ffn_W1, L0_ffn_b1, L0_ffn_W2,
           L0_ffn_b2, L0_ffn_Wout, L0_ffn_bout):
    x = np.ascontiguousarray(np.asarray(L0_attn_x, dtype=np.float32))
    Bx, S, Dm = x.shape
    consts = build_consts(_bf(L0_ffn_W1), _bf(L0_ffn_b1), _bf(L0_ffn_W2),
                          _bf(L0_attn_gate_local_w), _bf(L0_attn_gate_remote_w))
    shared = {
        "wq": _bf(L0_attn_wq), "wk": _bf(L0_attn_wk), "wv": _bf(L0_attn_wv),
        "wgq": _bf(L0_attn_wgq), "wgk": _bf(L0_attn_wgk), "wgv": _bf(L0_attn_wgv),
        "r_out": _bf(L0_attn_r_out_w), "w_out": _bf(L0_attn_w_out),
        "win": _bf(L0_ffn_Win), "binv": _bf(L0_ffn_bin_).reshape(64, 1),
        "woutf": _bf(L0_ffn_Wout), "boutv": _bf(L0_ffn_bout).reshape(1, D),
        "b2v": _bf(L0_ffn_b2).reshape(64, 1),
        "gateblk8": consts["gateblk8"], "selg": consts["selg"], "rrep": consts["rrep"],
        "bdh": consts["bdh"], "w1scat": consts["w1scat"], "b1scat": consts["b1scat"],
        "maskj": consts["maskj"], "pm32": consts["pm32"], "identc": consts["identc"],
        "onesr": consts["onesr"], "ones64c": consts["ones64c"],
    }
    in_maps = []
    zeros_h = np.zeros((HALO, D), np.float32)
    zeros_o = np.zeros((T, D), np.float32)
    for core in range(8):
        b = core // 2
        hf = core % 2
        m = dict(shared)
        m["x_own"] = np.ascontiguousarray(x[b, hf * T:(hf + 1) * T, :])
        m["x_halo"] = np.ascontiguousarray(x[b, T - HALO:T, :]) if hf == 1 else zeros_h
        m["x_other"] = np.ascontiguousarray(x[b, 0:T, :]) if hf == 1 else zeros_o
        m["maskj0"] = consts["maskj0_second"] if hf == 1 else consts["maskj0_first"]
        m["maskr"] = consts["maskr_second"] if hf == 1 else consts["maskr_first"]
        in_maps.append(m)

    key = "main"
    if key not in _NC_CACHE:
        _NC_CACHE[key] = build_nc()
    nc = _NC_CACHE[key]
    res = run_bass_kernel_spmd(nc, in_maps, core_ids=list(range(8)))
    out = np.empty((Bx, S, Dm), np.float32)
    for core in range(8):
        b = core // 2
        hf = core % 2
        out[b, hf * T:(hf + 1) * T, :] = res.results[core]["y"]
    return out


if __name__ == "__main__":
    pass


# version 05
# speedup vs baseline: 1.0350x; 1.0350x over previous
import os
import numpy as np
import ml_dtypes
import concourse.bass as bass
import concourse.mybir as mybir
from concourse.bass_utils import run_bass_kernel_spmd
from contextlib import ExitStack

F32 = mybir.dt.float32
BF16 = mybir.dt.bfloat16
AF = mybir.ActivationFunctionType
ALU = mybir.AluOpType

T = 2048
HALO = 256
D = 1024
NH = 16
G = 512
NG = T // G
NT = T // 128
NJT = (T + HALO) // 128
KR = 6
EPS = 1e-5
NGRP = int(os.environ.get("KNGRP", str(NG)))


class Rec:
    QUEUES = ("pe", "act", "dve", "gp", "sp")
    CSEM = {"pe": "s_pe", "act": "s_act", "dve": "s_dve", "gp": "s_gp", "sp": "s_sp"}
    DSEM = {"sp": "d_sp", "act": "d_act", "gp": "d_gp"}

    def __init__(self):
        self.ops = {q: [] for q in self.QUEUES}
        self.cnt = {s: 0 for s in list(self.CSEM.values()) + list(self.DSEM.values())}
        self.last_dmat = None

    def add(self, q, emit, deps=(), dma=False, tag="", keep_same=False):
        sem = self.DSEM[q] if dma else self.CSEM[q]
        inc = 16 if dma else 1
        dd = {}
        for t in deps:
            if t is None:
                continue
            s, v = t
            if (not dma) and (not keep_same) and s == self.CSEM[q]:
                continue
            dd[s] = max(dd.get(s, 0), v)
        if dma and self.last_dmat is not None:
            s, v = self.last_dmat
            dd[s] = max(dd.get(s, 0), v)
        self.cnt[sem] += inc
        self.ops[q].append((emit, sorted(dd.items()), sem, inc, tag))
        return (sem, self.cnt[sem])

    def dma_quiesce(self):
        return [(s, self.cnt[s]) for s in self.DSEM.values() if self.cnt[s] > 0]


class Buf:
    __slots__ = ("w", "r")

    def __init__(self):
        self.w = None
        self.r = []


def mkuse(rec):
    def use(q, emit, reads=(), writes=(), dma=False, tag="", extra=(), sync_same=False):
        deps = list(extra)
        for b in reads:
            deps.append(b.w)
        for b in writes:
            deps.append(b.w)
            deps.extend(b.r)
        tok = rec.add(q, emit, deps, dma=dma, tag=tag, keep_same=sync_same)
        for b in reads:
            b.r.append(tok)
        for b in writes:
            b.w = tok
            b.r = []
        return tok
    return use


def build_nc(debug_keys=()):
    nc = bass.Bass("TRN2", target_bir_lowering=False)

    def din(name, shape, dtype=F32):
        return nc.dram_tensor(name, shape, dtype, kind="ExternalInput").ap()

    x_own = din("x_own", [T, D])
    x_halo = din("x_halo", [HALO, D])
    x_other = din("x_other", [T, D])
    wq = din("wq", [D, D]); wk = din("wk", [D, D]); wv = din("wv", [D, D])
    wgq = din("wgq", [D, 512]); wgk = din("wgk", [D, 512]); wgv = din("wgv", [D, 512])
    r_out = din("r_out", [512, D]); w_out = din("w_out", [D, D])
    win = din("win", [D, 64]); binv = din("binv", [64, 1])
    woutf = din("woutf", [64, D]); boutv = din("boutv", [1, D]); b2v = din("b2v", [64, 1])
    gateblk8 = din("gateblk8", [128, 8 * 32])
    selg = din("selg", [16, 8 * 128])
    rrep = din("rrep", [64, 8 * 128])
    bdh = din("bdh", [128, 8 * 64])
    w1scat = din("w1scat", [128, 8]); b1scat = din("b1scat", [128, 8])
    maskj = din("maskj", [128, 1536]); maskj0 = din("maskj0", [128, 1536])
    maskr = din("maskr", [128, 2048])
    pm32 = din("pm32", [128, 16 * 32])
    identc = din("identc", [64, 64])
    onesr = din("onesr", [1, 128])
    ones64c = din("ones64c", [128, 64])

    y = nc.dram_tensor("y", [T, D], F32, kind="ExternalOutput").ap()
    dbg = None
    if debug_keys:
        dbg = nc.dram_tensor("dbg", [128, 16384], F32, kind="ExternalOutput").ap()
    scr1 = nc.dram_tensor("scr1", [T + HALO, D], BF16, kind="Internal").ap()
    scr2 = nc.dram_tensor("scr2", [T, D], BF16, kind="Internal").ap()
    scrx1 = nc.dram_tensor("scrx1", [T, D], F32, kind="Internal").ap()

    rec = Rec()
    use = mkuse(rec)
    st = ExitStack()
    en = st.enter_context

    def sb(name, shape, dtype):
        return en(nc.sbuf_tensor(name, shape, dtype))

    w_q = sb("w_q", [128, 8 * 1024], BF16)
    w_k = sb("w_k", [128, 8 * 1024], BF16)
    w_v = sb("w_v", [128, 8 * 1024], BF16)
    w_o = sb("w_o", [128, 8 * 1024], BF16)
    w_gq = sb("w_gq", [128, 8 * 512], BF16)
    r_o = sb("r_o", [128, 4 * 1024], BF16)
    w_in = sb("w_in", [128, 8 * 64], BF16)
    w_of = sb("w_of", [64, 1024], BF16)
    rrep_s = sb("rrep_s", [64, 8 * 128], BF16)
    bdh_s = sb("bdh_s", [128, 8 * 64], BF16)
    gblk = sb("gblk", [128, 8 * 32], BF16)
    selg_s = sb("selg_s", [16, 8 * 128], BF16)
    mj = sb("mj", [128, 1536], BF16)
    mr_s = sb("mr_s", [128, 512], BF16)
    ident_s = sb("ident_s", [64, 64], BF16)
    ones1_s = sb("ones1_s", [1, 128], BF16)
    ones64_s = sb("ones64_s", [128, 64], BF16)
    w1s = sb("w1s", [128, 8], F32)
    b1s = sb("b1s", [128, 8], F32)
    bin_s = sb("bin_s", [64, 1], F32)
    b2_s = sb("b2_s", [64, 1], F32)
    epsv = sb("epsv", [128, 1], F32)

    arena = sb("arena", [128, 4096], BF16)
    grbc_s = sb("grbc_s", [128, 512], BF16)
    t1_s = sb("t1_s", [128, 512], BF16)
    xa = sb("xa", [128, 2 * 1024], F32)
    lnst = sb("lnst", [128, 3 * 1024], BF16)
    ln1xT = sb("ln1xT", [128, 8 * 768], BF16)
    q_g = sb("q_g", [128, 8 * 512], BF16)
    ln2xT = sb("ln2xT", [128, 8 * 512], BF16)
    gq_g = sb("gq_g", [128, 4 * 512], BF16)
    k_sb = sb("k_sb", [128, 8 * KR * 128], BF16)
    v_sb = sb("v_sb", [128, KR * 1024], BF16)
    pmb = sb("pmb", [128, 1536], BF16)
    ps3 = sb("ps3", [128, 2 * 512], BF16)
    sbc = sb("sbc", [128, 512], BF16)
    rcp = sb("rcp", [128, 512], F32)
    comb = sb("comb", [128, 8 * 512], BF16)
    gsig = sb("gsig", [16, 1024], BF16)
    union = sb("union", [128, 3072], BF16)
    pooled = union[:, 0:1024]
    pooledT = union[:, 1024:1536]
    pm32_v = union[:, 1536:2048]
    sT_v = union[:, 0:512]
    h_v = union[:, 512:1536]
    gm_v = union[:, 1536:2048]
    bout_v = union[:, 2048:3072]
    gkT = sb("gkT", [128, 4 * 64], BF16)
    gv_l = sb("gv_l", [128, 512], BF16)
    x1_sb = sb("x1_sb", [128, 2 * 1024], F32)

    st_s1 = sb("st_s1", [128, 16], F32)
    st_s5 = sb("st_s5", [128, 16], F32)
    st_s7 = sb("st_s7", [128, 16], F32)
    rst_s1 = sb("rst_s1", [128, 4], F32)
    rst_s5 = sb("rst_s5", [128, 8], F32)
    rst_s7 = sb("rst_s7", [128, 4], F32)
    wrm = rcp
    wrmb = sbc

    psA = en(nc.psum_tensor("psA", [128, 2048], F32))
    psB = en(nc.psum_tensor("psB", [128, 2048], F32))

    def bank(i):
        t = psA if i < 4 else psB
        return t[:, (i % 4) * 512:((i % 4) + 1) * 512]

    def bank_t(i):
        return (psA if i < 4 else psB, (i % 4) * 512)

    sems = {}
    for s in list(Rec.CSEM.values()) + list(Rec.DSEM.values()):
        sems[s] = en(nc.semaphore(s))

    PB = [Buf() for _ in range(8)]
    B = Buf
    bw = {n: B() for n in ["w_q", "w_k", "w_v", "w_o", "w_gq", "r_o", "w_in", "w_of",
                           "rrep", "bdh", "gblk", "selg", "mj", "mj0", "mr",
                           "ident", "ones1", "ones64", "w1s", "b1s", "bin", "b2",
                           "eps"]}
    b_xa = [B(), B()]
    b_lnst = [B() for _ in range(3)]
    b_ln1xT = B(); b_qg = B(); b_gqg = B()
    b_ln2xT = B()
    b_k = [B() for _ in range(KR)]
    b_v = [B() for _ in range(KR)]
    b_pmb = [B()]
    b_ps3 = [B(), B()]
    b_sbc = B(); b_rcp = B()
    b_comb = [B() for _ in range(8)]
    b_gsig = B()
    b_union = B()
    b_pooled = b_union; b_pooledT = b_union; b_gkT = B(); b_gvl = B()
    b_x1 = [B(), B()]
    b_scrx1 = [B() for _ in range(NT)]
    b_sT = b_union; b_h = [b_union, b_union]; b_gm = b_union
    b_stat = {"s1": B(), "s5": B(), "s7": B()}
    b_rst = {"s1": B(), "s5": B(), "s7": B()}
    b_scr1 = [B() for _ in range(NJT)]
    b_scr2 = [B() for _ in range(NT)]
    b_arena = B()
    b_prem = B(); b_pmrem = B(); b_reml = B(); b_grbc = B(); b_t1 = B(); b_recr = B()
    b_dbg = B()

    AV_PREM, AV_PM, AV_RECR, AV_REML = 0, 512, 1024, 2048

    dbg_done = []

    def dump(key, ap_fn, rows, cols, col0, reads=()):
        if dbg is None:
            return

        def e(e_):
            return e_.dma_start(out=dbg[0:rows, col0:col0 + cols], in_=ap_fn())
        use("gp", e, reads=list(reads), dma=True, tag=f"dbg_{key}")
        dbg_done.append(key)

    def w_dve(e):
        e.memset(wrm[:, :], 1.0)
        e.memset(wrmb[:, 256:512], 0.0)
        e.memset(epsv[:, :], EPS)
        r = None
        for _ in range(2):
            e.bn_stats(st_s1[:, 0:6], wrm[:, 0:512])
            e.bn_stats(st_s1[:, 6:12], wrm[:, 0:512])
            e.bn_aggr(st_s1[:, 12:14], st_s1[:, 0:12])
            e.reciprocal(wrm[:, 0:1], wrm[:, 1:2])
            e.reciprocal(wrm[:, 0:512], wrm[:, 0:512])
            e.memset(wrm[:, 0:512], 1.0)
            r = e.tensor_scalar(out=wrm[:, 2:3], in0=wrm[:, 3:4], scalar1=wrm[:, 4:5],
                                scalar2=wrm[:, 5:6], op0=ALU.mult, op1=ALU.add)
        return r
    t_dw = use("dve", w_dve, writes=[b_rcp, b_sbc, b_stat["s1"], bw["eps"]], tag="warm_dve")

    def w_pe(e):
        return e.matmul(psA[:, 0:512], wrmb[0:1, 256:384], wrmb[0:1, 0:512], start=True, stop=True)
    t_pw = use("pe", w_pe, reads=[b_sbc], writes=[PB[0]], tag="warm_pe")

    def w_dve2(e):
        e.bn_stats(st_s1[:, 0:6], psA[:, 0:512])
        e.bn_stats(st_s1[:, 6:12], psA[:, 0:512])
        e.bn_aggr(st_s1[:, 12:14], st_s1[:, 0:12])
        return e.reciprocal(wrm[:, 0:512], wrm[:, 0:512])
    t_dw2 = use("dve", w_dve2, reads=[PB[0]], writes=[b_rcp, b_stat["s1"]], tag="warm_dve2")

    def w_act(e):
        r = None
        for _ in range(2):
            e.activation(out=wrm[:, 6:7], in_=wrm[:, 7:8], func=AF.Ln, bias=epsv[:, :], scale=1.0)
            e.activation(out=wrm[:, 6:7], in_=wrm[:, 7:8], func=AF.Exp, bias=0.0, scale=-0.5)
            e.activation(out=wrmb[:, 0:64], in_=psA[:, 0:64], func=AF.Exp, bias=0.0, scale=0.125)
            e.activation(out=wrmb[:, 0:64], in_=wrm[:, 72:136], func=AF.Tanh, bias=0.0, scale=0.5)
            e.activation(out=wrmb[:, 0:64], in_=psA[:, 0:64], func=AF.Relu, bias=wrm[:, 1:2],
                         scale=wrm[:, 2:3])
            r = e.activation(out=wrmb[:, 0:64], in_=psA[:, 0:64], func=AF.Copy, bias=0.0, scale=1.0)
        return r
    t_aw = use("act", w_act, reads=[b_rcp, bw["eps"], PB[0]], writes=[b_rcp, b_sbc], tag="warm_act")

    def w_gp(e):
        e.memset(wrmb[:, 256:512], 1.0)
        e.tensor_tensor(out=wrmb[:, 0:128], in0=wrmb[:, 256:384], in1=wrmb[:, 384:512], op=ALU.add)
        e.tensor_copy(wrmb[:, 0:128], wrmb[:, 256:384])
        e.tensor_tensor(out=wrm[:, 128:256], in0=wrm[:, 256:384], in1=wrmb[:, 384:512], op=ALU.add)
        r = e.tensor_scalar(out=wrmb[:, 0:256], in0=wrm[:, 128:384], scalar1=wrm[:, 0:1],
                            scalar2=wrm[:, 1:2], op0=ALU.mult, op1=ALU.add)
        return r
    t_gw = use("gp", w_gp, reads=[b_rcp], writes=[b_rcp, b_sbc], extra=[t_dw], tag="warm_gp")

    def wload(q, dst, src, bufs, tag):
        return use(q, lambda e: e.dma_start(out=dst, in_=src), writes=bufs, dma=True, tag=tag)

    pool_first = {"flag": True}

    def s1_tile(kind, i, xi):
        slot = xi % 2
        lsl = xi % 3
        xap = xa[:, slot * 1024:(slot + 1) * 1024]
        lap = lnst[:, lsl * 1024:(lsl + 1) * 1024]
        src = {"halo": x_halo, "own": x_own, "other": x_other}[kind]
        row = i * 128
        t_ld = use("sp", lambda e, xap=xap, src=src, row=row: e.dma_start(out=xap, in_=src[row:row + 128, :]),
                   writes=[b_xa[slot]], dma=True, tag=f"x_{kind}{i}")

        def e_bn(e, xap=xap):
            e.bn_stats(st_s1[:, 0:6], xap[:, 0:512])
            return e.bn_stats(st_s1[:, 6:12], xap[:, 512:1024])
        t_bn = use("dve", e_bn, reads=[b_xa[slot]], writes=[b_stat["s1"]], extra=[t_dw2], tag=f"bn1_{kind}{i}")
        t_ag = use("dve", lambda e: e.bn_aggr(st_s1[:, 12:14], st_s1[:, 0:12]),
                   reads=[b_stat["s1"]], writes=[b_stat["s1"]], extra=[t_bn], sync_same=True, tag=f"ag1_{kind}{i}")

        def e_rs(e):
            e.activation(out=rst_s1[:, 0:1], in_=st_s1[:, 13:14], func=AF.Ln, bias=epsv[:, :], scale=1.0)
            return e.activation(out=rst_s1[:, 0:1], in_=rst_s1[:, 0:1], func=AF.Exp, bias=0.0, scale=-0.5)
        t_rs = use("act", e_rs, reads=[b_stat["s1"], bw["eps"]], writes=[b_rst["s1"]], extra=[t_aw], tag=f"rs1_{kind}{i}")
        t_nm = use("dve", lambda e: e.tensor_scalar(out=rst_s1[:, 1:2], in0=st_s1[:, 12:13],
                                                    scalar1=rst_s1[:, 0:1], scalar2=-1.0,
                                                    op0=ALU.mult, op1=ALU.mult),
                   reads=[b_stat["s1"], b_rst["s1"]], writes=[b_rst["s1"]], extra=[t_rs], sync_same=True,
                   tag=f"nm1_{kind}{i}")
        t_ap = use("gp", lambda e, xap=xap, lap=lap: e.tensor_scalar(
            out=lap, in0=xap, scalar1=rst_s1[:, 0:1], scalar2=rst_s1[:, 1:2], op0=ALU.mult, op1=ALU.add),
            reads=[b_xa[slot], b_rst["s1"]], writes=[b_lnst[lsl]], extra=[t_nm, t_gw], tag=f"ap1_{kind}{i}")
        if kind != "other":
            jt = i if kind == "halo" else i + 2
            use("sp", lambda e, lap=lap, jt=jt: e.dma_start(out=scr1[jt * 128:(jt + 1) * 128, :], in_=lap),
                reads=[b_lnst[lsl]], writes=[b_scr1[jt]], dma=True, tag=f"w_scr1_{jt}")
        if kind != "halo":
            orow = slice(32, 64) if kind == "own" else slice(0, 32)
            first = pool_first["flag"]
            pool_first["flag"] = False
            last = (kind == "other" and i == NT - 1)

            def e_pool2(e, lap=lap, i=i, orow=orow, kind=kind):
                r = None
                for ds in range(2):
                    r = e.matmul(bank(6 + ds)[orow, :], pm32_v[:, i * 32:(i + 1) * 32],
                                 lap[:, ds * 512:(ds + 1) * 512],
                                 start=(i == 0), stop=(i == NT - 1))
                return r
            use("pe", e_pool2, reads=[b_lnst[lsl], b_union], writes=[PB[6], PB[7]],
                extra=[t_ap], tag=f"pool_{kind}{i}")

    wload("gp", pm32_v[:, :], pm32[:, :], [b_union], "ld_pm32")
    for i in range(2):
        s1_tile("halo", i, i)
    for i in range(0, 4):
        s1_tile("own", i, 2 + i)


    def b1_transpose(g):
        need = [b_scr1[j].w for j in range(4 * g, 4 * g + 6)]

        def e(e_):
            return e_.dma_start_transpose(out=ln1xT[:, :].rearrange("p (c t) -> p c t", c=8),
                                          in_=scr1[g * G:g * G + G + HALO, :])
        tok = use("sp", e, writes=[b_ln1xT], dma=True, tag=f"B1_{g}",
                  extra=[t for t in need if t] + rec.dma_quiesce())
        rec.last_dmat = tok
        return tok

    b1_transpose(0)
    wload("gp", w_q[:, :].rearrange("p (c e) -> p c e", c=8), wq.rearrange("(c p) e -> p c e", p=128), [bw["w_q"]], "ld_wq")
    wload("gp", w_k[:, :].rearrange("p (c e) -> p c e", c=8), wk.rearrange("(c p) e -> p c e", p=128), [bw["w_k"]], "ld_wk")
    wload("gp", w_v[:, :].rearrange("p (c e) -> p c e", c=8), wv.rearrange("(c p) e -> p c e", p=128), [bw["w_v"]], "ld_wv")
    wload("gp", w_o[:, :].rearrange("p (c e) -> p c e", c=8), w_out.rearrange("(c p) e -> p c e", p=128), [bw["w_o"]], "ld_wo")
    wload("gp", w_gq[:, :].rearrange("p (c e) -> p c e", c=8), wgq.rearrange("(c p) e -> p c e", p=128), [bw["w_gq"]], "ld_wgq")
    wload("gp", arena[:, 0:4096].rearrange("p (c e) -> p c e", c=8), wgk.rearrange("(c p) e -> p c e", p=128), [b_arena], "ld_wgk")
    wload("gp", r_o[:, :].rearrange("p (c e) -> p c e", c=4), r_out.rearrange("(c p) e -> p c e", p=128), [bw["r_o"]], "ld_ro")
    wload("gp", w_in[:, :].rearrange("p (c e) -> p c e", c=8), win.rearrange("(c p) e -> p c e", p=128), [bw["w_in"]], "ld_win")
    wload("gp", w_of[:, :], woutf[:, :], [bw["w_of"]], "ld_wof")
    wload("gp", rrep_s[:, :], rrep[:, :], [bw["rrep"]], "ld_rrep")
    wload("gp", bdh_s[:, :], bdh[:, :], [bw["bdh"]], "ld_bdh")
    wload("gp", gblk[:, :], gateblk8[:, :], [bw["gblk"]], "ld_gblk")
    wload("gp", selg_s[:, :], selg[:, :], [bw["selg"]], "ld_selg")
    wload("gp", mj[:, :], maskj0[:, :], [bw["mj"]], "ld_mj0")
    wload("gp", ident_s[:, :], identc[:, :], [bw["ident"]], "ld_ident")
    wload("gp", ones1_s[:, :], onesr[:, :], [bw["ones1"]], "ld_ones1")
    wload("gp", ones64_s[:, :], ones64c[:, :], [bw["ones64"]], "ld_ones64")
    wload("gp", bout_v[0:1, :], boutv[:, :], [b_union], "ld_bout")
    wload("sp", w1s[:, :], w1scat[:, :], [bw["w1s"]], "ld_w1s")
    wload("sp", b1s[:, :], b1scat[:, :], [bw["b1s"]], "ld_b1s")
    wload("sp", bin_s[:, :], binv[:, :], [bw["bin"]], "ld_bin")
    wload("sp", b2_s[:, :], b2v[:, :], [bw["b2"]], "ld_b2")
    for i in range(4, NT):
        s1_tile("own", i, 2 + i)
    for i in range(NT):
        s1_tile("other", i, 2 + NT + i)

    use("dve", lambda e: e.tensor_copy(pooled[0:64, 0:512], bank(6)[0:64, :]),
        reads=[PB[6]], writes=[b_pooled], tag="pool_ev0")
    use("dve", lambda e: e.tensor_copy(pooled[0:64, 512:1024], bank(7)[0:64, :]),
        reads=[PB[7]], writes=[b_pooled], tag="pool_ev1")
    for c in range(8):
        bi = 6 + (c % 2)

        def e_tp(e, c=c, bi=bi):
            return e.transpose(bank(bi).bitcast(BF16)[:, 0:64], pooled[0:64, c * 128:(c + 1) * 128],
                               ident_s[:, :])
        use("pe", e_tp, reads=[b_pooled, bw["ident"]], writes=[PB[bi]], tag=f"poolT{c}")
        use("dve", lambda e, c=c, bi=bi: e.tensor_copy(pooledT[:, c * 64:(c + 1) * 64],
                                                       bank(bi).bitcast(BF16)[:, 0:64]),
            reads=[PB[bi]], writes=[b_pooledT], tag=f"poolTe{c}")
    for gc in range(4):
        def e_gk(e, gc=gc):
            r = None
            for kc in range(8):
                r = e.matmul(bank(6)[:, 0:64], arena[:, kc * 512 + gc * 128: kc * 512 + gc * 128 + 128],
                             pooledT[:, kc * 64:(kc + 1) * 64], start=(kc == 0), stop=(kc == 7))
            return r
        use("pe", e_gk, reads=[b_arena, b_pooledT], writes=[PB[6]], tag=f"gk{gc}")
        use("dve", lambda e, gc=gc: e.tensor_copy(gkT[:, gc * 64:(gc + 1) * 64], bank(6)[:, 0:64]),
            reads=[PB[6]], writes=[b_gkT], tag=f"gke{gc}")

    wload("gp", arena[:, 0:4096].rearrange("p (c e) -> p c e", c=8),
          wgv.rearrange("(c p) e -> p c e", p=128), [b_arena], "ld_wgv")

    def e_gv(e):
        r = None
        for kc in range(8):
            r = e.matmul(bank(7)[0:64, :], pooledT[:, kc * 64:(kc + 1) * 64],
                         arena[:, kc * 512:(kc + 1) * 512], start=(kc == 0), stop=(kc == 7))
        return r
    use("pe", e_gv, reads=[b_arena, b_pooledT], writes=[PB[7]], tag="gv")
    use("dve", lambda e: e.tensor_copy(gv_l[0:64, :], bank(7)[0:64, :]),
        reads=[PB[7]], writes=[b_gvl], tag="gve")
    use("pe", lambda e: e.matmul(bank(7)[64:128, :], ident_s[:, :], gv_l[0:64, :], start=True, stop=True),
        reads=[b_gvl, bw["ident"]], writes=[PB[7]], tag="gv64")
    use("dve", lambda e: e.tensor_copy(gv_l[64:128, :], bank(7)[64:128, :]),
        reads=[PB[7]], writes=[b_gvl], tag="gv64e")
    dump("pooled", lambda: pooled[0:64, :], 64, 1024, 7936, reads=[b_union])

    GEOM = [(-2, 0, 1, 0, 0, 0), (-1, 0, 2, 0, 128, 128), (0, 0, 3, 1, 0, 384),
            (1, 1, 3, 2, 0, 768), (2, 2, 2, 3, 0, 1152), (3, 3, 1, 3, 256, 1408)]

    def s2_group(g):
        for ec in range(8):
            bi = 4 + (ec % 2)

            def e_q(e, ec=ec, bi=bi):
                r = None
                for kc in range(8):
                    r = e.matmul(bank(bi), w_q[:, kc * 1024 + ec * 128: kc * 1024 + (ec + 1) * 128],
                                 ln1xT[:, kc * 768 + 256: kc * 768 + 768], start=(kc == 0), stop=(kc == 7))
                return r
            use("pe", e_q, reads=[bw["w_q"], b_ln1xT], writes=[PB[bi]], tag=f"q{g}_{ec}")
            use("dve", lambda e, ec=ec, bi=bi: e.tensor_copy(q_g[:, ec * 512:(ec + 1) * 512], bank(bi)),
                reads=[PB[bi]], writes=[b_qg], tag=f"qe{g}_{ec}")
        jts_all = list(range(0, 6)) if g == 0 else list(range(4 * g + 2, 4 * g + 6))
        col0_all = 0 if g == 0 else 256
        kchunks = [jts_all[i:i + 4] for i in range(0, len(jts_all), 4)]
        for ci, jts in enumerate(kchunks):
            col0 = col0_all + ci * 512
            nj = len(jts)
            for ec in range(8):
                bi = 4 + (ec % 2)

                def e_k(e, ec=ec, bi=bi, col0=col0, nj=nj):
                    r = None
                    for kc in range(8):
                        r = e.matmul(bank(bi)[:, 0:nj * 128], w_k[:, kc * 1024 + ec * 128: kc * 1024 + (ec + 1) * 128],
                                     ln1xT[:, kc * 768 + col0: kc * 768 + col0 + nj * 128], start=(kc == 0), stop=(kc == 7))
                    return r
                use("pe", e_k, reads=[bw["w_k"], b_ln1xT], writes=[PB[bi]], tag=f"k{g}_{ci}_{ec}")

                def e_ke(e, ec=ec, bi=bi, jts=jts, nj=nj):
                    r = None
                    ji = 0
                    while ji < nj:
                        sl = jts[ji] % KR
                        run = 1
                        while ji + run < nj and (jts[ji + run] % KR) == sl + run:
                            run += 1
                        r = e.tensor_copy(k_sb[:, ec * (KR * 128) + sl * 128: ec * (KR * 128) + (sl + run) * 128],
                                          bank(bi)[:, ji * 128:(ji + run) * 128])
                        ji += run
                    return r
                use("dve", e_ke, reads=[PB[bi]], writes=[b_k[jt % KR] for jt in jts], tag=f"ke{g}_{ci}_{ec}")
        for ji, jt in enumerate(jts_all):
            lcol = col0_all + ji * 128
            for vs in range(2):
                bi = 4 + vs

                def e_v(e, jt=jt, lcol=lcol, vs=vs, bi=bi):
                    r = None
                    for kc in range(8):
                        r = e.matmul(bank(bi), ln1xT[:, kc * 768 + lcol: kc * 768 + lcol + 128],
                                     w_v[:, kc * 1024 + vs * 512: kc * 1024 + (vs + 1) * 512],
                                     start=(kc == 0), stop=(kc == 7))
                    return r
                use("pe", e_v, reads=[bw["w_v"], b_ln1xT], writes=[PB[bi]], tag=f"v{g}_{jt}_{vs}")
                use("dve", lambda e, jt=jt, vs=vs, bi=bi: e.tensor_copy(
                    v_sb[:, (jt % KR) * 1024 + vs * 512: (jt % KR) * 1024 + (vs + 1) * 512], bank(bi)),
                    reads=[PB[bi]], writes=[b_v[jt % KR]], tag=f"ve{g}_{jt}_{vs}")
        for pc in range(4):
            bi = 4 + (pc % 2)

            def e_gq(e, pc=pc, bi=bi):
                r = None
                for kc in range(8):
                    r = e.matmul(bank(bi), w_gq[:, kc * 512 + pc * 128: kc * 512 + (pc + 1) * 128],
                                 ln1xT[:, kc * 768 + 256: kc * 768 + 768], start=(kc == 0), stop=(kc == 7))
                return r
            use("pe", e_gq, reads=[bw["w_gq"], b_ln1xT], writes=[PB[bi]], tag=f"gq{g}_{pc}")
            use("act", lambda e, pc=pc, bi=bi: e.activation(out=gq_g[:, pc * 512:(pc + 1) * 512],
                                                            in_=bank(bi), func=AF.Copy, bias=0.0, scale=1.0),
                reads=[PB[bi]], writes=[b_gqg], extra=[t_aw], tag=f"gqe{g}_{pc}")

        def e_gate(e):
            r = None
            for c in range(8):
                e.matmul(bank(6)[0:16, :], gblk[:, c * 32:c * 32 + 16],
                         q_g[:, c * 512:(c + 1) * 512], start=(c == 0), stop=(c == 7))
            for c in range(8):
                r = e.matmul(bank(3)[0:16, :], gblk[:, c * 32 + 16:(c + 1) * 32],
                             q_g[:, c * 512:(c + 1) * 512], start=(c == 0), stop=(c == 7))
            return r
        use("pe", e_gate, reads=[bw["gblk"], b_qg], writes=[PB[6], PB[3]], tag=f"gate{g}")

        def e_sig(e):
            e.activation(out=gsig[0:16, 0:512], in_=bank(6)[0:16, :], func=AF.Tanh, bias=0.0, scale=0.5)
            return e.activation(out=gsig[0:16, 512:1024], in_=bank(3)[0:16, :], func=AF.Tanh,
                                bias=0.0, scale=0.5)
        use("act", e_sig, reads=[PB[6], PB[3]], writes=[b_gsig], extra=[t_aw], tag=f"sig{g}")
        use("dve", lambda e: e.tensor_scalar(out=gsig[0:16, :], in0=gsig[0:16, :], scalar1=0.5,
                                             scalar2=0.5, op0=ALU.mult, op1=ALU.add),
            reads=[b_gsig], writes=[b_gsig], tag=f"sigf{g}")

    def s3_local(g, dumps=False):
        mjbuf = bw["mj"]
        for pc in range(8):
            for par in (0, 1):
                h = 2 * pc + par
                rows = slice(par * 64, (par + 1) * 64)
                bb = 0 if par == 0 else 4

                def e_sc(e, pc=pc, rows=rows, bb=bb, g=g):
                    r = None
                    for (dk, q0, qn, bi, boff, pmoff) in GEOM:
                        jt = 4 * g + 2 + dk
                        sl = jt % KR
                        r = e.matmul(bank(bb + bi)[:, boff:boff + qn * 128],
                                     k_sb[rows, pc * (KR * 128) + sl * 128: pc * (KR * 128) + (sl + 1) * 128],
                                     q_g[rows, pc * 512 + q0 * 128: pc * 512 + (q0 + qn) * 128],
                                     start=True, stop=True)
                    return r
                kreads = [b_k[(4 * g + 2 + dk) % KR] for (dk, *_r) in GEOM]
                use("pe", e_sc, reads=kreads + [b_qg],
                    writes=[PB[bb], PB[bb + 1], PB[bb + 2], PB[bb + 3]], tag=f"sc{g}_{h}")
            for par in (0, 1):
                h = 2 * pc + par
                bb = 0 if par == 0 else 4
                tens, off0 = bank_t(bb)

                def e_exp(e, tens=tens, off0=off0):
                    src = bass.AP(tens, off0, [[2048, 128], [512, 4], [1, 384]])
                    return e.activation(out=pmb[:, :], in_=src, func=AF.Exp, bias=0.0, scale=0.125)
                use("act", e_exp, reads=[PB[bb], PB[bb + 1], PB[bb + 2], PB[bb + 3]],
                    writes=[b_pmb[0]], extra=[t_aw], tag=f"exp{g}_{h}")
                use("dve", lambda e: e.tensor_tensor(out=pmb[:, :], in0=pmb[:, :], in1=mj[:, :],
                                                     op=ALU.mult),
                    reads=[b_pmb[0], mjbuf], writes=[b_pmb[0]], tag=f"mk{g}_{h}")
                if dumps and pc == 0 and par == 0:
                    dump("pm", lambda: pmb[:, :], 128, 1536, 3840, reads=[b_pmb[0]])

                def e_ps3(e, par=par):
                    o = par * 512
                    e.tensor_copy(bass.AP(ps3, o, [[1024, 128], [128, 4], [1, 128]]),
                                  bass.AP(pmb, 128, [[1536, 128], [384, 4], [1, 128]]))
                    e.tensor_tensor(out=bass.AP(ps3, o, [[1024, 128], [128, 3], [1, 128]]),
                                    in0=bass.AP(ps3, o, [[1024, 128], [128, 3], [1, 128]]),
                                    in1=bass.AP(pmb, 384, [[1536, 128], [384, 3], [1, 128]]), op=ALU.add)
                    e.tensor_tensor(out=ps3[:, o + 384:o + 512], in0=ps3[:, o + 384:o + 512],
                                    in1=pmb[:, 1408:1536], op=ALU.add)
                    e.tensor_tensor(out=ps3[:, o:o + 128], in0=ps3[:, o:o + 128],
                                    in1=pmb[:, 0:128], op=ALU.add)
                    return e.tensor_tensor(out=bass.AP(ps3, o + 128, [[1024, 128], [128, 3], [1, 128]]),
                                           in0=bass.AP(ps3, o + 128, [[1024, 128], [128, 3], [1, 128]]),
                                           in1=bass.AP(pmb, 256, [[1536, 128], [384, 3], [1, 128]]), op=ALU.add)
                use("gp", e_ps3, reads=[b_pmb[0]], writes=[b_ps3[par]], extra=[t_gw], tag=f"ps3{g}_{h}")

                def e_pv(e, pc=pc, par=par, g=g):
                    r = None
                    h = 2 * pc + par
                    orow = slice(0, 64) if par == 0 else slice(64, 128)
                    for (dk, q0, qn, bi, boff, pmoff) in GEOM:
                        jt = 4 * g + 2 + dk
                        sl = jt % KR
                        r = e.matmul(bank(0)[orow, q0 * 128:(q0 + qn) * 128],
                                     v_sb[:, sl * 1024 + h * 64: sl * 1024 + (h + 1) * 64],
                                     pmb[:, pmoff: pmoff + qn * 128],
                                     start=(dk == -2), stop=(dk == 3))
                    return r
                vreads = [b_v[(4 * g + 2 + dk) % KR] for (dk, *_r) in GEOM]
                use("pe", e_pv, reads=vreads + [b_pmb[0]], writes=[PB[0]], tag=f"pv{g}_{h}")

            def e_den(e):
                e.matmul(bank(1)[0:64, :], ones64_s[:, :], ps3[:, 0:512], start=True, stop=True)
                return e.matmul(bank(1)[64:128, :], ones64_s[:, :], ps3[:, 512:1024], start=True, stop=True)
            use("pe", e_den, reads=[b_ps3[0], b_ps3[1], bw["ones64"]], writes=[PB[1]], tag=f"dn{g}_{pc}")
            use("dve", lambda e: e.reciprocal(rcp[:, :], bank(1)),
                reads=[PB[1]], writes=[b_rcp], tag=f"rc{g}_{pc}")
            if dumps and pc == 0:
                dump("rcp", lambda: rcp[:, :], 128, 512, 8960, reads=[b_rcp])
                dump("ps3d", lambda: ps3[:, :], 128, 1024, 9984, reads=[b_ps3[0], b_ps3[1]])

            def e_glbc(e, pc=pc):
                return e.matmul(bank(2), selg_s[:, pc * 128:(pc + 1) * 128], gsig[0:16, 0:512],
                                start=True, stop=True)
            use("pe", e_glbc, reads=[bw["selg"], b_gsig], writes=[PB[2]], tag=f"glbc{g}_{pc}")
            use("dve", lambda e: e.tensor_tensor(out=sbc[:, :], in0=bank(2), in1=rcp[:, :], op=ALU.mult),
                reads=[PB[2], b_rcp], writes=[b_sbc], sync_same=True, tag=f"sbc{g}_{pc}")
            if dumps and pc == 0:
                dump("sbcd", lambda: sbc[:, :], 128, 512, 9472, reads=[b_sbc])
            use("dve", lambda e, pc=pc: e.tensor_tensor(out=comb[:, pc * 512:(pc + 1) * 512],
                                                        in0=bank(0), in1=sbc[:, :], op=ALU.mult),
                reads=[PB[0], b_sbc], writes=[b_comb[pc]], tag=f"cmb{g}_{pc}")
            if dumps and pc == 0:
                dump("comb0", lambda: comb[:, 0:512], 128, 512, 5376, reads=[b_comb[0]])

    def s3_remote(g):
        use("gp", lambda e, g=g: e.dma_start(out=mr_s[:, :], in_=maskr[:, g * 512:(g + 1) * 512]),
            writes=[bw["mr"]], dma=True, tag=f"ld_mr{g}")
        for pc in range(4):
            def e_rsc(e, pc=pc):
                e.matmul(bank(3)[0:64, :], gkT[0:64, pc * 64:(pc + 1) * 64],
                         gq_g[0:64, pc * 512:(pc + 1) * 512], start=True, stop=True)
                return e.matmul(bank(3)[64:128, :], gkT[64:128, pc * 64:(pc + 1) * 64],
                                gq_g[64:128, pc * 512:(pc + 1) * 512], start=True, stop=True)
            use("pe", e_rsc, reads=[b_gkT, b_gqg], writes=[PB[3]], tag=f"rsc{g}_{pc}")
            use("act", lambda e: e.activation(out=arena[:, AV_PREM:AV_PREM + 512], in_=bank(3),
                                              func=AF.Exp, bias=0.0, scale=0.125),
                reads=[PB[3]], writes=[b_prem], extra=[t_aw], tag=f"rexp{g}_{pc}")
            use("dve", lambda e: e.tensor_tensor(out=arena[:, AV_PM:AV_PM + 512],
                                                 in0=arena[:, AV_PREM:AV_PREM + 512],
                                                 in1=mr_s[:, :], op=ALU.mult),
                reads=[b_prem, bw["mr"]], writes=[b_pmrem], tag=f"rmk{g}_{pc}")

            def e_rdn(e):
                e.matmul(bank(2)[0:64, :], ones64_s[0:64, :], arena[0:64, AV_PM:AV_PM + 512],
                         start=True, stop=True)
                return e.matmul(bank(2)[64:128, :], ones64_s[64:128, :], arena[64:128, AV_PM:AV_PM + 512],
                                start=True, stop=True)
            use("pe", e_rdn, reads=[b_pmrem, bw["ones64"]], writes=[PB[2]], tag=f"rdn{g}_{pc}")
            use("dve", lambda e: e.reciprocal(arena[:, AV_RECR:AV_RECR + 1024].bitcast(F32), bank(2)),
                reads=[PB[2]], writes=[b_recr], tag=f"rrc{g}_{pc}")

            def e_rpv(e, pc=pc):
                e.matmul(bank(3)[0:64, :], gv_l[0:64, (2 * pc) * 64:(2 * pc + 1) * 64],
                         arena[0:64, AV_PM:AV_PM + 512], start=True, stop=True)
                return e.matmul(bank(3)[64:128, :], gv_l[64:128, (2 * pc + 1) * 64:(2 * pc + 2) * 64],
                                arena[64:128, AV_PM:AV_PM + 512], start=True, stop=True)
            use("pe", e_rpv, reads=[b_gvl, b_pmrem], writes=[PB[3]], tag=f"rpv{g}_{pc}")
            use("dve", lambda e, pc=pc: e.tensor_tensor(
                out=arena[:, AV_REML + pc * 512:AV_REML + (pc + 1) * 512], in0=bank(3),
                in1=arena[:, AV_RECR:AV_RECR + 1024].bitcast(F32), op=ALU.mult),
                reads=[PB[3], b_recr], writes=[b_reml], sync_same=True, tag=f"rle{g}_{pc}")
        for oc in range(8):
            def e_ro(e, oc=oc):
                r = None
                for pc in range(4):
                    r = e.matmul(bank(3), r_o[:, pc * 1024 + oc * 128: pc * 1024 + (oc + 1) * 128],
                                 arena[:, AV_REML + pc * 512:AV_REML + (pc + 1) * 512],
                                 start=(pc == 0), stop=(pc == 3))
                return r
            use("pe", e_ro, reads=[bw["r_o"], b_reml], writes=[PB[3]], tag=f"ro{g}_{oc}")

            def e_grbc(e, oc=oc):
                return e.matmul(bank(2), selg_s[:, oc * 128:(oc + 1) * 128], gsig[0:16, 512:1024],
                                start=True, stop=True)
            use("pe", e_grbc, reads=[bw["selg"], b_gsig], writes=[PB[2]], tag=f"grbc{g}_{oc}")
            use("dve", lambda e: e.tensor_copy(grbc_s[:, :], bank(2)),
                reads=[PB[2]], writes=[b_grbc], tag=f"gre{g}_{oc}")
            use("dve", lambda e: e.tensor_tensor(out=t1_s[:, :], in0=bank(3), in1=grbc_s[:, :],
                                                 op=ALU.mult),
                reads=[PB[3], b_grbc], writes=[b_t1], tag=f"t1{g}_{oc}")
            use("gp", lambda e, oc=oc: e.tensor_tensor(out=comb[:, oc * 512:(oc + 1) * 512],
                                                       in0=comb[:, oc * 512:(oc + 1) * 512],
                                                       in1=t1_s[:, :], op=ALU.add),
                reads=[b_t1, b_comb[oc]], writes=[b_comb[oc]], extra=[t_gw], tag=f"cadd{g}_{oc}")

    def ln_tile(stage, src_is_psum, src_aps, src_bufs, rst_col, extra=()):
        stt = {"s1": st_s1, "s5": st_s5, "s7": st_s7}[stage]
        rt = {"s1": rst_s1, "s5": rst_s5, "s7": rst_s7}[stage]
        sbuf_b, rb = b_stat[stage], b_rst[stage]
        c = rst_col

        def e_bn(e):
            e.bn_stats(stt[:, 0:6], src_aps[0])
            return e.bn_stats(stt[:, 6:12], src_aps[1])
        t1 = use("dve", e_bn, reads=list(src_bufs), writes=[sbuf_b], extra=list(extra), tag=f"bn_{stage}")
        t2 = use("dve", lambda e: e.bn_aggr(stt[:, 12:14], stt[:, 0:12]),
                 reads=[sbuf_b], writes=[sbuf_b], extra=[t1], sync_same=True, tag=f"ag_{stage}")

        def e_rs(e):
            e.activation(out=rt[:, c:c + 1], in_=stt[:, 13:14], func=AF.Ln, bias=epsv[:, :], scale=1.0)
            return e.activation(out=rt[:, c:c + 1], in_=rt[:, c:c + 1], func=AF.Exp, bias=0.0, scale=-0.5)
        t3 = use("act", e_rs, reads=[sbuf_b, bw["eps"]], writes=[rb], extra=[t_aw], tag=f"rs_{stage}")
        t4 = use("dve", lambda e: e.tensor_scalar(out=rt[:, c + 1:c + 2], in0=stt[:, 12:13],
                                                  scalar1=rt[:, c:c + 1], scalar2=-1.0,
                                                  op0=ALU.mult, op1=ALU.mult),
                 reads=[sbuf_b, rb], writes=[rb], extra=[t3], sync_same=True, tag=f"nm_{stage}")
        return t4, rt, c

    def s45_group(g, dumps=False):
        for ts in range(4):
            xslot = ts % 2
            xap = xa[:, xslot * 1024:(xslot + 1) * 1024]
            row = g * G + ts * 128
            t_xb = use("sp", lambda e, xap=xap, row=row: e.dma_start(out=xap, in_=x_own[row:row + 128, :]),
                       writes=[b_xa[xslot]], dma=True, tag=f"xr{g}_{ts}")
            for dh in range(2):
                bi = 4 + dh

                def e_wo(e, ts=ts, dh=dh, bi=bi):
                    r = None
                    for pc in range(8):
                        r = e.matmul(bank(bi), comb[:, pc * 512 + ts * 128: pc * 512 + (ts + 1) * 128],
                                     w_o[:, pc * 1024 + dh * 512: pc * 1024 + (dh + 1) * 512],
                                     start=(pc == 0), stop=(pc == 7))
                    return r
                use("pe", e_wo, reads=[bw["w_o"]] + [b_comb[pc] for pc in range(8)],
                    writes=[PB[bi]], tag=f"wo{g}_{ts}_{dh}")
            t4, rt, c = ln_tile("s5", True, [bank(4), bank(5)], [PB[4], PB[5]], 0)
            if dumps and ts == 0:
                use("dve", lambda e: e.tensor_copy(rcp[:, :], bank(4)), reads=[PB[4]],
                    writes=[b_rcp], tag="dbgcp")
                dump("attnout", lambda: rcp[:, :], 128, 512, 12032, reads=[b_rcp])
                dump("rst5", lambda: rst_s5[:, :], 128, 8, 12544, reads=[b_rst["s5"]])
            lsl = ts % 3
            lap = lnst[:, lsl * 1024:(lsl + 1) * 1024]

            def e_ap(e, lap=lap):
                e.tensor_scalar(out=lap[:, 0:512], in0=bank(4), scalar1=rt[:, c:c + 1],
                                scalar2=rt[:, c + 1:c + 2], op0=ALU.mult, op1=ALU.add)
                return e.tensor_scalar(out=lap[:, 512:1024], in0=bank(5), scalar1=rt[:, c:c + 1],
                                       scalar2=rt[:, c + 1:c + 2], op0=ALU.mult, op1=ALU.add)
            t_ap = use("dve", e_ap, reads=[PB[4], PB[5], b_rst["s5"]], writes=[b_lnst[lsl]],
                       extra=[t4], sync_same=True, tag=f"apA{g}_{ts}")
            if dumps and ts == 0:
                dump("lna", lambda lap=lap: lap, 128, 1024, 11008, reads=[b_lnst[lsl]])
            x1slot = ts % 2
            x1ap = x1_sb[:, x1slot * 1024:(x1slot + 1) * 1024]
            t_x1 = use("gp", lambda e, xap=xap, x1ap=x1ap, lap=lap: e.tensor_tensor(
                out=x1ap, in0=xap, in1=lap, op=ALU.add),
                reads=[b_xa[xslot], b_lnst[lsl]], writes=[b_x1[x1slot]], extra=[t_gw, t_xb], tag=f"x1{g}_{ts}")
            ti = g * 4 + ts
            use("sp", lambda e, x1ap=x1ap, row=row: e.dma_start(out=scrx1[row:row + 128, :], in_=x1ap),
                reads=[b_x1[x1slot]], writes=[b_scrx1[ti]], dma=True, tag=f"wx1_{ti}")
            if dumps and ts == 0:
                dump("x1", lambda x1ap=x1ap: x1ap, 128, 1024, 6912, reads=[b_x1[0]])
            t4b, rt2, c2 = ln_tile("s5", False, [x1ap[:, 0:512], x1ap[:, 512:1024]], [b_x1[x1slot]], 2)

            def e_ap2(e, lap=lap, x1ap=x1ap):
                return e.tensor_scalar(out=lap, in0=x1ap, scalar1=rt2[:, c2:c2 + 1],
                                       scalar2=rt2[:, c2 + 1:c2 + 2], op0=ALU.mult, op1=ALU.add)
            use("dve", e_ap2, reads=[b_x1[x1slot], b_rst["s5"]], writes=[b_lnst[lsl]],
                extra=[t4b], sync_same=True, tag=f"ap2{g}_{ts}")
            use("sp", lambda e, lap=lap, row=row: e.dma_start(out=scr2[row:row + 128, :], in_=lap),
                reads=[b_lnst[lsl]], writes=[b_scr2[ti]], dma=True, tag=f"wscr2_{ti}")

    def b3_transpose(g):
        need = [b_scr2[g * 4 + i].w for i in range(4)]

        def e(e_):
            return e_.dma_start_transpose(out=ln2xT[:, :].rearrange("p (c t) -> p c t", c=8),
                                          in_=scr2[g * G:(g + 1) * G, :])
        tok = use("sp", e, writes=[b_ln2xT], dma=True, tag=f"B3_{g}",
                  extra=[t for t in need if t] + rec.dma_quiesce())
        rec.last_dmat = tok
        return tok

    def s6_group(g):
        def e_sT(e):
            r = None
            for kc in range(8):
                r = e.matmul(bank(6)[0:64, :], w_in[:, kc * 64:(kc + 1) * 64],
                             ln2xT[:, kc * 512:(kc + 1) * 512], start=(kc == 0), stop=(kc == 7))
            return r
        use("pe", e_sT, reads=[bw["w_in"], b_ln2xT], writes=[PB[6]], tag=f"sT{g}")
        use("dve", lambda e: e.tensor_scalar(out=sT_v[0:64, :], in0=bank(6)[0:64, :],
                                             scalar1=bin_s[:, :], scalar2=None, op0=ALU.add),
            reads=[PB[6], bw["bin"]], writes=[b_sT], tag=f"sTe{g}")
        for k in range(8):
            bi = 4 + (k % 2)
            hs = k % 2

            def e_rep(e, k=k, bi=bi):
                return e.matmul(bank(bi), rrep_s[:, k * 128:(k + 1) * 128], sT_v[0:64, :],
                                start=True, stop=True)
            use("pe", e_rep, reads=[bw["rrep"], b_sT], writes=[PB[bi]], tag=f"rep{g}_{k}")
            use("act", lambda e, k=k, bi=bi, hs=hs: e.activation(
                out=h_v[:, hs * 512:(hs + 1) * 512], in_=bank(bi), func=AF.Relu,
                bias=b1s[:, k:k + 1], scale=w1s[:, k:k + 1]),
                reads=[PB[bi], bw["w1s"], bw["b1s"]], writes=[b_h[hs]], extra=[t_aw], tag=f"relu{g}_{k}")

            def e_bd(e, k=k, hs=hs):
                return e.matmul(bank(7)[0:64, :], bdh_s[:, k * 64:(k + 1) * 64],
                                h_v[:, hs * 512:(hs + 1) * 512], start=(k == 0), stop=(k == 7))
            use("pe", e_bd, reads=[bw["bdh"], b_h[hs]], writes=[PB[7]], tag=f"bd{g}_{k}")
        use("dve", lambda e: e.tensor_scalar(out=gm_v[0:64, :], in0=bank(7)[0:64, :],
                                             scalar1=b2_s[:, :], scalar2=None, op0=ALU.add),
            reads=[PB[7], bw["b2"]], writes=[b_gm], tag=f"gme{g}")
        for ts in range(4):
            row = g * G + ts * 128
            x1slot = ts % 2
            x1ap = x1_sb[:, x1slot * 1024:(x1slot + 1) * 1024]
            ti = g * 4 + ts
            t_x1r = use("sp", lambda e, x1ap=x1ap, row=row: e.dma_start(out=x1ap, in_=scrx1[row:row + 128, :]),
                        reads=[b_scrx1[ti]], writes=[b_x1[x1slot]], dma=True, tag=f"x1r{g}_{ts}")
            for dh in range(2):
                bi = 4 + dh

                def e_f(e, ts=ts, dh=dh, bi=bi):
                    e.matmul(bank(bi), gm_v[0:64, ts * 128:(ts + 1) * 128],
                             w_of[0:64, dh * 512:(dh + 1) * 512], start=True, stop=False)
                    return e.matmul(bank(bi), ones1_s[:, :], bout_v[0:1, dh * 512:(dh + 1) * 512],
                                    start=False, stop=True)
                use("pe", e_f, reads=[b_gm, bw["w_of"], bw["ones1"], b_union],
                    writes=[PB[bi]], tag=f"f{g}_{ts}_{dh}")
            t4, rt, c = ln_tile("s7", True, [bank(4), bank(5)], [PB[4], PB[5]], 0)
            lsl = ts % 3
            lap = lnst[:, lsl * 1024:(lsl + 1) * 1024]

            def e_ap(e, lap=lap):
                e.tensor_scalar(out=lap[:, 0:512], in0=bank(4), scalar1=rt[:, c:c + 1],
                                scalar2=rt[:, c + 1:c + 2], op0=ALU.mult, op1=ALU.add)
                return e.tensor_scalar(out=lap[:, 512:1024], in0=bank(5), scalar1=rt[:, c:c + 1],
                                       scalar2=rt[:, c + 1:c + 2], op0=ALU.mult, op1=ALU.add)
            t_ap = use("dve", e_ap, reads=[PB[4], PB[5], b_rst["s7"]], writes=[b_lnst[lsl]],
                       extra=[t4], sync_same=True, tag=f"apF{g}_{ts}")
            yslot = ts % 2
            yap = xa[:, yslot * 1024:(yslot + 1) * 1024]
            use("gp", lambda e, x1ap=x1ap, yap=yap, lap=lap: e.tensor_tensor(
                out=yap, in0=x1ap, in1=lap, op=ALU.add),
                reads=[b_x1[x1slot], b_lnst[lsl]], writes=[b_xa[yslot]], extra=[t_gw, t_x1r], tag=f"yt{g}_{ts}")
            use("sp", lambda e, yap=yap, row=row: e.dma_start(out=y[row:row + 128, :], in_=yap),
                reads=[b_xa[yslot]], dma=True, tag=f"wy_{ti}")

    for g in range(NGRP):
        if g > 0:
            b1_transpose(g)
        s2_group(g)
        if g == 0:
            dump("qg", lambda: q_g[:, 0:1024], 128, 1024, 1792, reads=[b_qg])
            dump("gate", lambda: gsig[0:16, 0:1024], 16, 1024, 2816, reads=[b_gsig])
        s3_local(g, dumps=(g == 0))
        if g == 0:
            use("gp", lambda e: e.dma_start(out=mj[:, :], in_=maskj[:, :]),
                writes=[bw["mj"]], dma=True, tag="ld_mj")
        s3_remote(g)
        if g == 0:
            dump("reml", lambda: arena[:, AV_REML:AV_REML + 512], 128, 512, 6400, reads=[b_reml])
            dump("combF", lambda: comb[:, 0:512], 128, 512, 5888, reads=[b_comb[0]])
        s45_group(g, dumps=(g == 0))
        b3_transpose(g)
        s6_group(g)

    final_waits = [(s, rec.cnt[s]) for s in Rec.DSEM.values()]
    comp_waits = [(s, rec.cnt[s]) for s in Rec.CSEM.values() if rec.cnt[s] > 0 and s != "s_sp"]

    engmap = {"pe": "tensor", "act": "scalar", "dve": "vector", "gp": "gpsimd", "sp": "sync"}
    with nc.Block() as block:
        for q in Rec.QUEUES:
            def mk(q):
                def body(e):
                    hw = {}
                    for (emit, deps, sem, inc, tag) in rec.ops[q]:
                        for (s, v) in deps:
                            if hw.get(s, 0) >= v:
                                continue
                            e.wait_ge(sems[s], v)
                            hw[s] = v
                        inst = emit(e)
                        if isinstance(inst, tuple):
                            inst = inst[-1]
                        inst.then_inc(sems[sem], inc)
                    if q == "sp":
                        for (s, v) in comp_waits + final_waits:
                            if v > 0 and hw.get(s, 0) < v:
                                e.wait_ge(sems[s], v)
                return body
            getattr(block, engmap[q])(mk(q))
    st.close()
    return nc


def _bf(a):
    return np.asarray(a, dtype=np.float32)


def build_consts(W1, b1, W2, glw, grw):
    c = {}
    gateblk8 = np.zeros((128, 8 * 32), np.float32)
    for pc in range(8):
        for par in range(2):
            h = 2 * pc + par
            rows = slice(par * 64, (par + 1) * 64)
            gateblk8[rows, pc * 32 + h] = glw[h]
            gateblk8[rows, pc * 32 + 16 + h] = grw[h]
    c["gateblk8"] = gateblk8
    selgm = np.zeros((16, 8 * 128), np.float32)
    for pc in range(8):
        for par in range(2):
            selgm[2 * pc + par, pc * 128 + par * 64: pc * 128 + (par + 1) * 64] = 1.0
    c["selg"] = selgm
    rrep = np.zeros((64, 8 * 128), np.float32)
    for k in range(8):
        for mm in range(8):
            m = 8 * k + mm
            for i in range(16):
                rrep[m, k * 128 + mm * 16 + i] = 1.0
    c["rrep"] = rrep
    bdh = np.zeros((128, 8 * 64), np.float32)
    w1sc = np.zeros((128, 8), np.float32)
    b1sc = np.zeros((128, 8), np.float32)
    for k in range(8):
        for mm in range(8):
            m = 8 * k + mm
            for i in range(16):
                bdh[mm * 16 + i, k * 64 + m] = W2[m, i]
                w1sc[mm * 16 + i, k] = W1[m, i]
                b1sc[mm * 16 + i, k] = b1[m, i]
    c["bdh"] = bdh
    c["w1scat"] = w1sc
    c["b1scat"] = b1sc
    jj = np.arange(128)[:, None]
    tt = np.arange(128)[None, :]
    Lm = (jj > tt).astype(np.float32)
    Um = (jj <= tt).astype(np.float32)
    Fm = np.ones((128, 128), np.float32)
    mjc = np.concatenate([Lm, Fm, Lm, Um, Fm, Lm, Um, Fm, Lm, Um, Fm, Um], axis=1)
    c["maskj"] = mjc
    mj0 = mjc.copy()
    mj0[:, 0:384] = 0.0
    c["maskj0_first"] = mj0
    c["maskj0_second"] = mjc
    tglob_first = np.arange(T)
    slot = np.arange(64)
    def _mr(second):
        m = np.zeros((64, T), np.float32)
        for s in range(64):
            if second:
                valid = (s < 32) | (s - 32 <= (np.arange(T) // 64))
            else:
                valid = (s >= 32) & (s - 32 <= (np.arange(T) // 64))
            m[s] = valid.astype(np.float32)
        return np.concatenate([m, m], axis=0)
    c["maskr_first"] = _mr(False)
    c["maskr_second"] = _mr(True)
    pm = np.zeros((128, 16 * 32), np.float32)
    for i in range(16):
        for t in range(128):
            pm[t, i * 32 + 2 * i + t // 64] = 1.0 / 64.0
    c["pm32"] = pm
    c["identc"] = np.eye(64, dtype=np.float32)
    c["onesr"] = np.ones((1, 128), np.float32)
    c["ones64c"] = np.ones((128, 64), np.float32)
    return c


_NC_CACHE = {}


def kernel(L0_attn_x, L0_attn_wq, L0_attn_wk, L0_attn_wv, L0_attn_wgq, L0_attn_wgk,
           L0_attn_wgv, L0_attn_r_out_w, L0_attn_gate_local_w, L0_attn_gate_remote_w,
           L0_attn_w_out, L0_ffn_Win, L0_ffn_bin_, L0_ffn_W1, L0_ffn_b1, L0_ffn_W2,
           L0_ffn_b2, L0_ffn_Wout, L0_ffn_bout):
    x = np.ascontiguousarray(np.asarray(L0_attn_x, dtype=np.float32))
    Bx, S, Dm = x.shape
    consts = build_consts(_bf(L0_ffn_W1), _bf(L0_ffn_b1), _bf(L0_ffn_W2),
                          _bf(L0_attn_gate_local_w), _bf(L0_attn_gate_remote_w))
    shared = {
        "wq": _bf(L0_attn_wq), "wk": _bf(L0_attn_wk), "wv": _bf(L0_attn_wv),
        "wgq": _bf(L0_attn_wgq), "wgk": _bf(L0_attn_wgk), "wgv": _bf(L0_attn_wgv),
        "r_out": _bf(L0_attn_r_out_w), "w_out": _bf(L0_attn_w_out),
        "win": _bf(L0_ffn_Win), "binv": _bf(L0_ffn_bin_).reshape(64, 1),
        "woutf": _bf(L0_ffn_Wout), "boutv": _bf(L0_ffn_bout).reshape(1, D),
        "b2v": _bf(L0_ffn_b2).reshape(64, 1),
        "gateblk8": consts["gateblk8"], "selg": consts["selg"], "rrep": consts["rrep"],
        "bdh": consts["bdh"], "w1scat": consts["w1scat"], "b1scat": consts["b1scat"],
        "maskj": consts["maskj"], "pm32": consts["pm32"], "identc": consts["identc"],
        "onesr": consts["onesr"], "ones64c": consts["ones64c"],
    }
    in_maps = []
    zeros_h = np.zeros((HALO, D), np.float32)
    zeros_o = np.zeros((T, D), np.float32)
    for core in range(8):
        b = core // 2
        hf = core % 2
        m = dict(shared)
        m["x_own"] = np.ascontiguousarray(x[b, hf * T:(hf + 1) * T, :])
        m["x_halo"] = np.ascontiguousarray(x[b, T - HALO:T, :]) if hf == 1 else zeros_h
        m["x_other"] = np.ascontiguousarray(x[b, 0:T, :]) if hf == 1 else zeros_o
        m["maskj0"] = consts["maskj0_second"] if hf == 1 else consts["maskj0_first"]
        m["maskr"] = consts["maskr_second"] if hf == 1 else consts["maskr_first"]
        in_maps.append(m)

    key = "main"
    if key not in _NC_CACHE:
        _NC_CACHE[key] = build_nc()
    nc = _NC_CACHE[key]
    res = run_bass_kernel_spmd(nc, in_maps, core_ids=list(range(8)))
    out = np.empty((Bx, S, Dm), np.float32)
    for core in range(8):
        b = core // 2
        hf = core % 2
        out[b, hf * T:(hf + 1) * T, :] = res.results[core]["y"]
    return out


if __name__ == "__main__":
    pass


# version 06
# speedup vs baseline: 1.1271x; 1.0890x over previous
import os
import numpy as np
import ml_dtypes
import concourse.bass as bass
import concourse.mybir as mybir
from concourse.bass_utils import run_bass_kernel_spmd
from contextlib import ExitStack

F32 = mybir.dt.float32
BF16 = mybir.dt.bfloat16
AF = mybir.ActivationFunctionType
ALU = mybir.AluOpType

T = 2048
HALO = 256
D = 1024
NH = 16
G = 512
NG = T // G
NT = T // 128
NJT = (T + HALO) // 128
KR = 6
EPS = 1e-5
NGRP = int(os.environ.get("KNGRP", str(NG)))


class Rec:
    QUEUES = ("pe", "act", "dve", "gp", "sp")
    CSEM = {"pe": "s_pe", "act": "s_act", "dve": "s_dve", "gp": "s_gp", "sp": "s_sp"}
    DSEM = {"sp": "d_sp", "act": "d_act", "gp": "d_gp"}

    def __init__(self):
        self.ops = {q: [] for q in self.QUEUES}
        self.cnt = {s: 0 for s in list(self.CSEM.values()) + list(self.DSEM.values())}
        self.last_dmat = None

    def add(self, q, emit, deps=(), dma=False, tag="", keep_same=False):
        sem = self.DSEM[q] if dma else self.CSEM[q]
        inc = 16 if dma else 1
        dd = {}
        for t in deps:
            if t is None:
                continue
            s, v = t
            if (not dma) and (not keep_same) and s == self.CSEM[q]:
                continue
            dd[s] = max(dd.get(s, 0), v)
        if dma and self.last_dmat is not None:
            s, v = self.last_dmat
            dd[s] = max(dd.get(s, 0), v)
        self.cnt[sem] += inc
        self.ops[q].append((emit, sorted(dd.items()), sem, inc, tag))
        return (sem, self.cnt[sem])

    def dma_quiesce(self):
        return [(s, self.cnt[s]) for s in self.DSEM.values() if self.cnt[s] > 0]


class Buf:
    __slots__ = ("w", "r")

    def __init__(self):
        self.w = None
        self.r = []


def mkuse(rec):
    def use(q, emit, reads=(), writes=(), dma=False, tag="", extra=(), sync_same=False):
        deps = list(extra)
        for b in reads:
            deps.append(b.w)
        for b in writes:
            deps.append(b.w)
            deps.extend(b.r)
        tok = rec.add(q, emit, deps, dma=dma, tag=tag, keep_same=sync_same)
        for b in reads:
            b.r.append(tok)
        for b in writes:
            b.w = tok
            b.r = []
        return tok
    return use


def build_nc(debug_keys=()):
    nc = bass.Bass("TRN2", target_bir_lowering=False)

    def din(name, shape, dtype=F32):
        return nc.dram_tensor(name, shape, dtype, kind="ExternalInput").ap()

    x_own = din("x_own", [T, D])
    x_halo = din("x_halo", [HALO, D])
    x_other = din("x_other", [T, D])
    wq = din("wq", [D, D]); wk = din("wk", [D, D]); wv = din("wv", [D, D])
    wgq = din("wgq", [D, 512]); wgk = din("wgk", [D, 512]); wgv = din("wgv", [D, 512])
    r_out = din("r_out", [512, D]); w_out = din("w_out", [D, D])
    win = din("win", [D, 64]); binv = din("binv", [64, 1])
    woutf = din("woutf", [64, D]); boutv = din("boutv", [1, D]); b2v = din("b2v", [64, 1])
    gateblk8 = din("gateblk8", [128, 8 * 32])
    selg = din("selg", [16, 8 * 128])
    rrep = din("rrep", [64, 8 * 128])
    bdh = din("bdh", [128, 8 * 64])
    w1scat = din("w1scat", [128, 8]); b1scat = din("b1scat", [128, 8])
    maskj = din("maskj", [128, 1536]); maskj0 = din("maskj0", [128, 1536])
    maskr = din("maskr", [128, 2048])
    pm32 = din("pm32", [128, 16 * 32])
    identc = din("identc", [64, 64])
    onesr = din("onesr", [1, 128])
    ones64c = din("ones64c", [128, 64])

    y = nc.dram_tensor("y", [T, D], F32, kind="ExternalOutput").ap()
    dbg = None
    if debug_keys:
        dbg = nc.dram_tensor("dbg", [128, 16384], F32, kind="ExternalOutput").ap()
    scr1 = nc.dram_tensor("scr1", [T + HALO, D], BF16, kind="Internal").ap()
    scr2 = nc.dram_tensor("scr2", [T, D], BF16, kind="Internal").ap()
    scrx1 = nc.dram_tensor("scrx1", [T, D], F32, kind="Internal").ap()

    rec = Rec()
    use = mkuse(rec)
    st = ExitStack()
    en = st.enter_context

    def sb(name, shape, dtype):
        return en(nc.sbuf_tensor(name, shape, dtype))

    w_q = sb("w_q", [128, 8 * 1024], BF16)
    w_k = sb("w_k", [128, 8 * 1024], BF16)
    w_v = sb("w_v", [128, 8 * 1024], BF16)
    w_o = sb("w_o", [128, 8 * 1024], BF16)
    w_gq = sb("w_gq", [128, 8 * 512], BF16)
    r_o = sb("r_o", [128, 4 * 1024], BF16)
    w_in = sb("w_in", [128, 8 * 64], BF16)
    w_of = sb("w_of", [64, 1024], BF16)
    rrep_s = sb("rrep_s", [64, 8 * 128], BF16)
    bdh_s = sb("bdh_s", [128, 8 * 64], BF16)
    gblk = sb("gblk", [128, 8 * 32], BF16)
    selg_s = sb("selg_s", [16, 8 * 128], BF16)
    mj = sb("mj", [128, 1536], BF16)
    mr_s = sb("mr_s", [128, 512], BF16)
    ident_s = sb("ident_s", [64, 64], BF16)
    ones1_s = sb("ones1_s", [1, 128], BF16)
    ones64_s = sb("ones64_s", [128, 64], BF16)
    w1s = sb("w1s", [128, 8], F32)
    b1s = sb("b1s", [128, 8], F32)
    bin_s = sb("bin_s", [64, 1], F32)
    b2_s = sb("b2_s", [64, 1], F32)
    epsv = sb("epsv", [128, 1], F32)

    arena = sb("arena", [128, 4096], BF16)
    grbc_s = sb("grbc_s", [128, 512], BF16)
    t1_s = sb("t1_s", [128, 512], BF16)
    xa = sb("xa", [128, 2 * 1024], F32)
    lnst = sb("lnst", [128, 3 * 1024], BF16)
    ln1xT = sb("ln1xT", [128, 8 * 768], BF16)
    q_g = sb("q_g", [128, 8 * 512], BF16)
    ln2xT = sb("ln2xT", [128, 8 * 512], BF16)
    gq_g = sb("gq_g", [128, 4 * 512], BF16)
    k_sb = sb("k_sb", [128, 8 * KR * 128], BF16)
    v_sb = sb("v_sb", [128, KR * 1024], BF16)
    pmb = sb("pmb", [128, 1536], BF16)
    sbc = sb("sbc", [128, 512], BF16)
    rcp = sb("rcp", [128, 512], F32)
    comb = sb("comb", [128, 8 * 512], BF16)
    gsig = sb("gsig", [16, 1024], BF16)
    union = sb("union", [128, 3072], BF16)
    pooled = union[:, 0:1024]
    pooledT = union[:, 1024:1536]
    pm32_v = union[:, 1536:2048]
    sT_v = union[:, 0:512]
    h_v = union[:, 512:1536]
    gm_v = union[:, 1536:2048]
    bout_v = union[:, 2048:3072]
    gkT = sb("gkT", [128, 4 * 64], BF16)
    gv_l = sb("gv_l", [128, 512], BF16)
    x1_sb = sb("x1_sb", [128, 2 * 1024], F32)

    st_s1 = sb("st_s1", [128, 16], F32)
    st_s5 = sb("st_s5", [128, 16], F32)
    st_s7 = sb("st_s7", [128, 16], F32)
    rst_s1 = sb("rst_s1", [128, 4], F32)
    rst_s5 = sb("rst_s5", [128, 8], F32)
    rst_s7 = sb("rst_s7", [128, 4], F32)
    wrm = rcp
    wrmb = sbc

    psA = en(nc.psum_tensor("psA", [128, 2048], F32))
    psB = en(nc.psum_tensor("psB", [128, 2048], F32))

    def bank(i):
        t = psA if i < 4 else psB
        return t[:, (i % 4) * 512:((i % 4) + 1) * 512]

    def bank_t(i):
        return (psA if i < 4 else psB, (i % 4) * 512)

    sems = {}
    for s in list(Rec.CSEM.values()) + list(Rec.DSEM.values()):
        sems[s] = en(nc.semaphore(s))

    PB = [Buf() for _ in range(8)]
    B = Buf
    bw = {n: B() for n in ["w_q", "w_k", "w_v", "w_o", "w_gq", "r_o", "w_in", "w_of",
                           "rrep", "bdh", "gblk", "selg", "mj", "mj0", "mr",
                           "ident", "ones1", "ones64", "w1s", "b1s", "bin", "b2",
                           "eps"]}
    b_xa = [B(), B()]
    b_lnst = [B() for _ in range(3)]
    b_ln1xT = B(); b_qg = B(); b_gqg = B()
    b_ln2xT = B()
    b_k = [B() for _ in range(KR)]
    b_v = [B() for _ in range(KR)]
    b_pmb = [B()]
    b_sbc = B(); b_rcp = B()
    b_comb = [B() for _ in range(8)]
    b_gsig = B()
    b_union = B()
    b_pooled = b_union; b_pooledT = b_union; b_gkT = B(); b_gvl = B()
    b_x1 = [B(), B()]
    b_scrx1 = [B() for _ in range(NT)]
    b_sT = b_union; b_h = [b_union, b_union]; b_gm = b_union
    b_stat = {"s1": B(), "s5": B(), "s7": B()}
    b_rst = {"s1": B(), "s5": B(), "s7": B()}
    b_scr1 = [B() for _ in range(NJT)]
    b_scr2 = [B() for _ in range(NT)]
    b_arena = B()
    b_prem = B(); b_pmrem = B(); b_reml = B(); b_grbc = B(); b_t1 = B(); b_recr = B()
    b_dbg = B()

    AV_PREM, AV_PM, AV_RECR, AV_REML = 0, 512, 1024, 2048

    dbg_done = []

    def dump(key, ap_fn, rows, cols, col0, reads=()):
        if dbg is None:
            return

        def e(e_):
            return e_.dma_start(out=dbg[0:rows, col0:col0 + cols], in_=ap_fn())
        use("gp", e, reads=list(reads), dma=True, tag=f"dbg_{key}")
        dbg_done.append(key)

    def w_dve(e):
        e.memset(wrm[:, :], 1.0)
        e.memset(wrmb[:, 256:512], 0.0)
        e.memset(epsv[:, :], EPS)
        r = None
        for _ in range(2):
            e.bn_stats(st_s1[:, 0:6], wrm[:, 0:512])
            e.bn_stats(st_s1[:, 6:12], wrm[:, 0:512])
            e.bn_aggr(st_s1[:, 12:14], st_s1[:, 0:12])
            e.reciprocal(wrm[:, 0:1], wrm[:, 1:2])
            e.reciprocal(wrm[:, 0:512], wrm[:, 0:512])
            e.memset(wrm[:, 0:512], 1.0)
            r = e.tensor_scalar(out=wrm[:, 2:3], in0=wrm[:, 3:4], scalar1=wrm[:, 4:5],
                                scalar2=wrm[:, 5:6], op0=ALU.mult, op1=ALU.add)
        return r
    t_dw = use("dve", w_dve, writes=[b_rcp, b_sbc, b_stat["s1"], bw["eps"]], tag="warm_dve")

    def w_pe(e):
        return e.matmul(psA[:, 0:512], wrmb[0:1, 256:384], wrmb[0:1, 0:512], start=True, stop=True)
    t_pw = use("pe", w_pe, reads=[b_sbc], writes=[PB[0]], tag="warm_pe")

    def w_dve2(e):
        e.bn_stats(st_s1[:, 0:6], psA[:, 0:512])
        e.bn_stats(st_s1[:, 6:12], psA[:, 0:512])
        e.bn_aggr(st_s1[:, 12:14], st_s1[:, 0:12])
        return e.reciprocal(wrm[:, 0:512], wrm[:, 0:512])
    t_dw2 = use("dve", w_dve2, reads=[PB[0]], writes=[b_rcp, b_stat["s1"]], tag="warm_dve2")

    def w_act(e):
        r = None
        for _ in range(2):
            e.activation(out=wrm[:, 6:7], in_=wrm[:, 7:8], func=AF.Ln, bias=epsv[:, :], scale=1.0)
            e.activation(out=wrm[:, 6:7], in_=wrm[:, 7:8], func=AF.Exp, bias=0.0, scale=-0.5)
            e.activation(out=wrmb[:, 0:64], in_=psA[:, 0:64], func=AF.Exp, bias=0.0, scale=0.125)
            e.activation(out=wrmb[:, 0:64], in_=wrm[:, 72:136], func=AF.Tanh, bias=0.0, scale=0.5)
            e.activation(out=wrmb[:, 0:64], in_=psA[:, 0:64], func=AF.Relu, bias=wrm[:, 1:2],
                         scale=wrm[:, 2:3])
            r = e.activation(out=wrmb[:, 0:64], in_=psA[:, 0:64], func=AF.Copy, bias=0.0, scale=1.0)
        return r
    t_aw = use("act", w_act, reads=[b_rcp, bw["eps"], PB[0]], writes=[b_rcp, b_sbc], tag="warm_act")

    def w_gp(e):
        e.memset(wrmb[:, 256:512], 1.0)
        e.tensor_tensor(out=wrmb[:, 0:128], in0=wrmb[:, 256:384], in1=wrmb[:, 384:512], op=ALU.add)
        e.tensor_copy(wrmb[:, 0:128], wrmb[:, 256:384])
        e.tensor_tensor(out=wrm[:, 128:256], in0=wrm[:, 256:384], in1=wrmb[:, 384:512], op=ALU.add)
        r = e.tensor_scalar(out=wrmb[:, 0:256], in0=wrm[:, 128:384], scalar1=wrm[:, 0:1],
                            scalar2=wrm[:, 1:2], op0=ALU.mult, op1=ALU.add)
        return r
    t_gw = use("gp", w_gp, reads=[b_rcp], writes=[b_rcp, b_sbc], extra=[t_dw], tag="warm_gp")

    def wload(q, dst, src, bufs, tag):
        return use(q, lambda e: e.dma_start(out=dst, in_=src), writes=bufs, dma=True, tag=tag)

    pool_first = {"flag": True}

    def s1_tile(kind, i, xi):
        slot = xi % 2
        lsl = xi % 3
        xap = xa[:, slot * 1024:(slot + 1) * 1024]
        lap = lnst[:, lsl * 1024:(lsl + 1) * 1024]
        src = {"halo": x_halo, "own": x_own, "other": x_other}[kind]
        row = i * 128
        t_ld = use("sp", lambda e, xap=xap, src=src, row=row: e.dma_start(out=xap, in_=src[row:row + 128, :]),
                   writes=[b_xa[slot]], dma=True, tag=f"x_{kind}{i}")

        def e_bn(e, xap=xap):
            e.bn_stats(st_s1[:, 0:6], xap[:, 0:512])
            return e.bn_stats(st_s1[:, 6:12], xap[:, 512:1024])
        t_bn = use("dve", e_bn, reads=[b_xa[slot]], writes=[b_stat["s1"]], extra=[t_dw2], tag=f"bn1_{kind}{i}")
        t_ag = use("dve", lambda e: e.bn_aggr(st_s1[:, 12:14], st_s1[:, 0:12]),
                   reads=[b_stat["s1"]], writes=[b_stat["s1"]], extra=[t_bn], sync_same=True, tag=f"ag1_{kind}{i}")

        def e_rs(e):
            e.activation(out=rst_s1[:, 0:1], in_=st_s1[:, 13:14], func=AF.Ln, bias=epsv[:, :], scale=1.0)
            return e.activation(out=rst_s1[:, 0:1], in_=rst_s1[:, 0:1], func=AF.Exp, bias=0.0, scale=-0.5)
        t_rs = use("act", e_rs, reads=[b_stat["s1"], bw["eps"]], writes=[b_rst["s1"]], extra=[t_aw], tag=f"rs1_{kind}{i}")
        t_nm = use("dve", lambda e: e.tensor_scalar(out=rst_s1[:, 1:2], in0=st_s1[:, 12:13],
                                                    scalar1=rst_s1[:, 0:1], scalar2=-1.0,
                                                    op0=ALU.mult, op1=ALU.mult),
                   reads=[b_stat["s1"], b_rst["s1"]], writes=[b_rst["s1"]], extra=[t_rs], sync_same=True,
                   tag=f"nm1_{kind}{i}")
        t_ap = use("gp", lambda e, xap=xap, lap=lap: e.tensor_scalar(
            out=lap, in0=xap, scalar1=rst_s1[:, 0:1], scalar2=rst_s1[:, 1:2], op0=ALU.mult, op1=ALU.add),
            reads=[b_xa[slot], b_rst["s1"]], writes=[b_lnst[lsl]], extra=[t_nm, t_gw], tag=f"ap1_{kind}{i}")
        if kind != "other":
            jt = i if kind == "halo" else i + 2
            use("sp", lambda e, lap=lap, jt=jt: e.dma_start(out=scr1[jt * 128:(jt + 1) * 128, :], in_=lap),
                reads=[b_lnst[lsl]], writes=[b_scr1[jt]], dma=True, tag=f"w_scr1_{jt}")
        if kind != "halo":
            orow = slice(32, 64) if kind == "own" else slice(0, 32)
            first = pool_first["flag"]
            pool_first["flag"] = False
            last = (kind == "other" and i == NT - 1)

            def e_pool2(e, lap=lap, i=i, orow=orow, kind=kind):
                r = None
                for ds in range(2):
                    r = e.matmul(bank(6 + ds)[orow, :], pm32_v[:, i * 32:(i + 1) * 32],
                                 lap[:, ds * 512:(ds + 1) * 512],
                                 start=(i == 0), stop=(i == NT - 1))
                return r
            use("pe", e_pool2, reads=[b_lnst[lsl], b_union], writes=[PB[6], PB[7]],
                extra=[t_ap], tag=f"pool_{kind}{i}")

    wload("gp", pm32_v[:, :], pm32[:, :], [b_union], "ld_pm32")
    for i in range(2):
        s1_tile("halo", i, i)
    for i in range(0, 4):
        s1_tile("own", i, 2 + i)


    def b1_transpose(g):
        need = [b_scr1[j].w for j in range(4 * g, 4 * g + 6)]

        def e(e_):
            return e_.dma_start_transpose(out=ln1xT[:, :].rearrange("p (c t) -> p c t", c=8),
                                          in_=scr1[g * G:g * G + G + HALO, :])
        tok = use("sp", e, writes=[b_ln1xT], dma=True, tag=f"B1_{g}",
                  extra=[t for t in need if t] + rec.dma_quiesce())
        rec.last_dmat = tok
        return tok

    b1_transpose(0)
    wload("gp", w_q[:, :].rearrange("p (c e) -> p c e", c=8), wq.rearrange("(c p) e -> p c e", p=128), [bw["w_q"]], "ld_wq")
    wload("gp", w_k[:, :].rearrange("p (c e) -> p c e", c=8), wk.rearrange("(c p) e -> p c e", p=128), [bw["w_k"]], "ld_wk")
    wload("gp", w_v[:, :].rearrange("p (c e) -> p c e", c=8), wv.rearrange("(c p) e -> p c e", p=128), [bw["w_v"]], "ld_wv")
    wload("gp", w_o[:, :].rearrange("p (c e) -> p c e", c=8), w_out.rearrange("(c p) e -> p c e", p=128), [bw["w_o"]], "ld_wo")
    wload("gp", w_gq[:, :].rearrange("p (c e) -> p c e", c=8), wgq.rearrange("(c p) e -> p c e", p=128), [bw["w_gq"]], "ld_wgq")
    wload("gp", arena[:, 0:4096].rearrange("p (c e) -> p c e", c=8), wgk.rearrange("(c p) e -> p c e", p=128), [b_arena], "ld_wgk")
    wload("gp", r_o[:, :].rearrange("p (c e) -> p c e", c=4), r_out.rearrange("(c p) e -> p c e", p=128), [bw["r_o"]], "ld_ro")
    wload("gp", w_in[:, :].rearrange("p (c e) -> p c e", c=8), win.rearrange("(c p) e -> p c e", p=128), [bw["w_in"]], "ld_win")
    wload("gp", w_of[:, :], woutf[:, :], [bw["w_of"]], "ld_wof")
    wload("gp", rrep_s[:, :], rrep[:, :], [bw["rrep"]], "ld_rrep")
    wload("gp", bdh_s[:, :], bdh[:, :], [bw["bdh"]], "ld_bdh")
    wload("gp", gblk[:, :], gateblk8[:, :], [bw["gblk"]], "ld_gblk")
    wload("gp", selg_s[:, :], selg[:, :], [bw["selg"]], "ld_selg")
    wload("gp", mj[:, :], maskj0[:, :], [bw["mj"]], "ld_mj0")
    wload("gp", ident_s[:, :], identc[:, :], [bw["ident"]], "ld_ident")
    wload("gp", ones1_s[:, :], onesr[:, :], [bw["ones1"]], "ld_ones1")
    wload("gp", ones64_s[:, :], ones64c[:, :], [bw["ones64"]], "ld_ones64")
    wload("gp", bout_v[0:1, :], boutv[:, :], [b_union], "ld_bout")
    wload("sp", w1s[:, :], w1scat[:, :], [bw["w1s"]], "ld_w1s")
    wload("sp", b1s[:, :], b1scat[:, :], [bw["b1s"]], "ld_b1s")
    wload("sp", bin_s[:, :], binv[:, :], [bw["bin"]], "ld_bin")
    wload("sp", b2_s[:, :], b2v[:, :], [bw["b2"]], "ld_b2")
    for i in range(4, NT):
        s1_tile("own", i, 2 + i)
    for i in range(NT):
        s1_tile("other", i, 2 + NT + i)

    use("dve", lambda e: e.tensor_copy(pooled[0:64, 0:512], bank(6)[0:64, :]),
        reads=[PB[6]], writes=[b_pooled], tag="pool_ev0")
    use("dve", lambda e: e.tensor_copy(pooled[0:64, 512:1024], bank(7)[0:64, :]),
        reads=[PB[7]], writes=[b_pooled], tag="pool_ev1")
    for c in range(8):
        bi = 6 + (c % 2)

        def e_tp(e, c=c, bi=bi):
            return e.transpose(bank(bi).bitcast(BF16)[:, 0:64], pooled[0:64, c * 128:(c + 1) * 128],
                               ident_s[:, :])
        use("pe", e_tp, reads=[b_pooled, bw["ident"]], writes=[PB[bi]], tag=f"poolT{c}")
        use("dve", lambda e, c=c, bi=bi: e.tensor_copy(pooledT[:, c * 64:(c + 1) * 64],
                                                       bank(bi).bitcast(BF16)[:, 0:64]),
            reads=[PB[bi]], writes=[b_pooledT], tag=f"poolTe{c}")
    for gc in range(4):
        def e_gk(e, gc=gc):
            r = None
            for kc in range(8):
                r = e.matmul(bank(6)[:, 0:64], arena[:, kc * 512 + gc * 128: kc * 512 + gc * 128 + 128],
                             pooledT[:, kc * 64:(kc + 1) * 64], start=(kc == 0), stop=(kc == 7))
            return r
        use("pe", e_gk, reads=[b_arena, b_pooledT], writes=[PB[6]], tag=f"gk{gc}")
        use("dve", lambda e, gc=gc: e.tensor_copy(gkT[:, gc * 64:(gc + 1) * 64], bank(6)[:, 0:64]),
            reads=[PB[6]], writes=[b_gkT], tag=f"gke{gc}")

    wload("gp", arena[:, 0:4096].rearrange("p (c e) -> p c e", c=8),
          wgv.rearrange("(c p) e -> p c e", p=128), [b_arena], "ld_wgv")

    def e_gv(e):
        r = None
        for kc in range(8):
            r = e.matmul(bank(7)[0:64, :], pooledT[:, kc * 64:(kc + 1) * 64],
                         arena[:, kc * 512:(kc + 1) * 512], start=(kc == 0), stop=(kc == 7))
        return r
    use("pe", e_gv, reads=[b_arena, b_pooledT], writes=[PB[7]], tag="gv")
    use("dve", lambda e: e.tensor_copy(gv_l[0:64, :], bank(7)[0:64, :]),
        reads=[PB[7]], writes=[b_gvl], tag="gve")
    use("pe", lambda e: e.matmul(bank(7)[64:128, :], ident_s[:, :], gv_l[0:64, :], start=True, stop=True),
        reads=[b_gvl, bw["ident"]], writes=[PB[7]], tag="gv64")
    use("dve", lambda e: e.tensor_copy(gv_l[64:128, :], bank(7)[64:128, :]),
        reads=[PB[7]], writes=[b_gvl], tag="gv64e")
    dump("pooled", lambda: pooled[0:64, :], 64, 1024, 7936, reads=[b_union])

    GEOM = [(-2, 0, 1, 0, 0, 0), (-1, 0, 2, 0, 128, 128), (0, 0, 3, 1, 0, 384),
            (1, 1, 3, 2, 0, 768), (2, 2, 2, 3, 0, 1152), (3, 3, 1, 3, 256, 1408)]

    def s2_group(g):
        for ec in range(8):
            bi = 4 + (ec % 2)

            def e_q(e, ec=ec, bi=bi):
                r = None
                for kc in range(8):
                    r = e.matmul(bank(bi), w_q[:, kc * 1024 + ec * 128: kc * 1024 + (ec + 1) * 128],
                                 ln1xT[:, kc * 768 + 256: kc * 768 + 768], start=(kc == 0), stop=(kc == 7))
                return r
            use("pe", e_q, reads=[bw["w_q"], b_ln1xT], writes=[PB[bi]], tag=f"q{g}_{ec}")
            use("dve", lambda e, ec=ec, bi=bi: e.tensor_copy(q_g[:, ec * 512:(ec + 1) * 512], bank(bi)),
                reads=[PB[bi]], writes=[b_qg], tag=f"qe{g}_{ec}")
        jts_all = list(range(0, 6)) if g == 0 else list(range(4 * g + 2, 4 * g + 6))
        col0_all = 0 if g == 0 else 256
        kchunks = [jts_all[i:i + 4] for i in range(0, len(jts_all), 4)]
        for ci, jts in enumerate(kchunks):
            col0 = col0_all + ci * 512
            nj = len(jts)
            for ec in range(8):
                bi = 4 + (ec % 2)

                def e_k(e, ec=ec, bi=bi, col0=col0, nj=nj):
                    r = None
                    for kc in range(8):
                        r = e.matmul(bank(bi)[:, 0:nj * 128], w_k[:, kc * 1024 + ec * 128: kc * 1024 + (ec + 1) * 128],
                                     ln1xT[:, kc * 768 + col0: kc * 768 + col0 + nj * 128], start=(kc == 0), stop=(kc == 7))
                    return r
                use("pe", e_k, reads=[bw["w_k"], b_ln1xT], writes=[PB[bi]], tag=f"k{g}_{ci}_{ec}")

                def e_ke(e, ec=ec, bi=bi, jts=jts, nj=nj):
                    r = None
                    ji = 0
                    while ji < nj:
                        sl = jts[ji] % KR
                        run = 1
                        while ji + run < nj and (jts[ji + run] % KR) == sl + run:
                            run += 1
                        r = e.tensor_copy(k_sb[:, ec * (KR * 128) + sl * 128: ec * (KR * 128) + (sl + run) * 128],
                                          bank(bi)[:, ji * 128:(ji + run) * 128])
                        ji += run
                    return r
                use("dve", e_ke, reads=[PB[bi]], writes=[b_k[jt % KR] for jt in jts], tag=f"ke{g}_{ci}_{ec}")
        for ji, jt in enumerate(jts_all):
            lcol = col0_all + ji * 128
            for vs in range(2):
                bi = 4 + vs

                def e_v(e, jt=jt, lcol=lcol, vs=vs, bi=bi):
                    r = None
                    for kc in range(8):
                        r = e.matmul(bank(bi), ln1xT[:, kc * 768 + lcol: kc * 768 + lcol + 128],
                                     w_v[:, kc * 1024 + vs * 512: kc * 1024 + (vs + 1) * 512],
                                     start=(kc == 0), stop=(kc == 7))
                    return r
                use("pe", e_v, reads=[bw["w_v"], b_ln1xT], writes=[PB[bi]], tag=f"v{g}_{jt}_{vs}")
                use("dve", lambda e, jt=jt, vs=vs, bi=bi: e.tensor_copy(
                    v_sb[:, (jt % KR) * 1024 + vs * 512: (jt % KR) * 1024 + (vs + 1) * 512], bank(bi)),
                    reads=[PB[bi]], writes=[b_v[jt % KR]], tag=f"ve{g}_{jt}_{vs}")
        for pc in range(4):
            bi = 4 + (pc % 2)

            def e_gq(e, pc=pc, bi=bi):
                r = None
                for kc in range(8):
                    r = e.matmul(bank(bi), w_gq[:, kc * 512 + pc * 128: kc * 512 + (pc + 1) * 128],
                                 ln1xT[:, kc * 768 + 256: kc * 768 + 768], start=(kc == 0), stop=(kc == 7))
                return r
            use("pe", e_gq, reads=[bw["w_gq"], b_ln1xT], writes=[PB[bi]], tag=f"gq{g}_{pc}")
            use("act", lambda e, pc=pc, bi=bi: e.activation(out=gq_g[:, pc * 512:(pc + 1) * 512],
                                                            in_=bank(bi), func=AF.Copy, bias=0.0, scale=1.0),
                reads=[PB[bi]], writes=[b_gqg], extra=[t_aw], tag=f"gqe{g}_{pc}")

        def e_gate(e):
            r = None
            for c in range(8):
                e.matmul(bank(6)[0:16, :], gblk[:, c * 32:c * 32 + 16],
                         q_g[:, c * 512:(c + 1) * 512], start=(c == 0), stop=(c == 7))
            for c in range(8):
                r = e.matmul(bank(3)[0:16, :], gblk[:, c * 32 + 16:(c + 1) * 32],
                             q_g[:, c * 512:(c + 1) * 512], start=(c == 0), stop=(c == 7))
            return r
        use("pe", e_gate, reads=[bw["gblk"], b_qg], writes=[PB[6], PB[3]], tag=f"gate{g}")

        def e_sig(e):
            e.activation(out=gsig[0:16, 0:512], in_=bank(6)[0:16, :], func=AF.Tanh, bias=0.0, scale=0.5)
            return e.activation(out=gsig[0:16, 512:1024], in_=bank(3)[0:16, :], func=AF.Tanh,
                                bias=0.0, scale=0.5)
        use("act", e_sig, reads=[PB[6], PB[3]], writes=[b_gsig], extra=[t_aw], tag=f"sig{g}")
        use("dve", lambda e: e.tensor_scalar(out=gsig[0:16, :], in0=gsig[0:16, :], scalar1=0.5,
                                             scalar2=0.5, op0=ALU.mult, op1=ALU.add),
            reads=[b_gsig], writes=[b_gsig], tag=f"sigf{g}")

    GEOM2 = [(-1, 0, 2, 0, 0), (2, 2, 2, 0, 256), (0, 0, 3, 1, 0), (-2, 0, 1, 1, 384),
             (1, 1, 3, 2, 0), (3, 3, 1, 2, 384)]

    def s3_local(g, dumps=False):
        mjbuf = bw["mj"]

        def rec_scores(h):
            par = h % 2
            pc = h // 2
            rows = slice(par * 64, (par + 1) * 64)
            bb = 0 if par == 0 else 4

            def e_sc(e, pc=pc, rows=rows, bb=bb, g=g):
                r = None
                for (dk, q0, qn, bi, boff) in GEOM2:
                    jt = 4 * g + 2 + dk
                    sl = jt % KR
                    r = e.matmul(bank(bb + bi)[:, boff:boff + qn * 128],
                                 k_sb[rows, pc * (KR * 128) + sl * 128: pc * (KR * 128) + (sl + 1) * 128],
                                 q_g[rows, pc * 512 + q0 * 128: pc * 512 + (q0 + qn) * 128],
                                 start=True, stop=True)
                return r
            kreads = [b_k[(4 * g + 2 + dk) % KR] for (dk, *_r) in GEOM2]
            use("pe", e_sc, reads=kreads + [b_qg],
                writes=[PB[bb], PB[bb + 1], PB[bb + 2]], tag=f"sc{g}_{h}")

        def rec_consume(h):
            par = h % 2
            pc = h // 2
            bb = 0 if par == 0 else 4
            tens = psA if par == 0 else psB

            def e_exp(e, tens=tens):
                return e.activation(out=pmb[:, :], in_=tens[:, 0:1536], func=AF.Exp,
                                    bias=0.0, scale=0.125)
            use("act", e_exp, reads=[PB[bb], PB[bb + 1], PB[bb + 2]],
                writes=[b_pmb[0]], extra=[t_aw], tag=f"exp{g}_{h}")
            use("dve", lambda e: e.tensor_tensor(out=pmb[:, :], in0=pmb[:, :], in1=mj[:, :],
                                                 op=ALU.mult),
                reads=[b_pmb[0], mjbuf], writes=[b_pmb[0]], tag=f"mk{g}_{h}")

            def e_pv(e, pc=pc, par=par, g=g):
                r = None
                h = 2 * pc + par
                orow = slice(0, 64) if par == 0 else slice(64, 128)
                first = True
                for (dk, q0, qn, bi, boff) in GEOM2:
                    jt = 4 * g + 2 + dk
                    sl = jt % KR
                    r = e.matmul(bank(3)[orow, q0 * 128:(q0 + qn) * 128],
                                 v_sb[:, sl * 1024 + h * 64: sl * 1024 + (h + 1) * 64],
                                 pmb[:, bi * 512 + boff: bi * 512 + boff + qn * 128],
                                 start=first, stop=(dk == 3))
                    first = False
                return r
            vreads = [b_v[(4 * g + 2 + dk) % KR] for (dk, *_r) in GEOM2]
            use("pe", e_pv, reads=vreads + [b_pmb[0]], writes=[PB[3]], tag=f"pv{g}_{h}")

            def e_dn(e, par=par):
                r = None
                orow = slice(0, 64) if par == 0 else slice(64, 128)
                first = True
                for (dk, q0, qn, bi, boff) in GEOM2:
                    r = e.matmul(bank(7)[orow, q0 * 128:(q0 + qn) * 128],
                                 ones64_s[:, :],
                                 pmb[:, bi * 512 + boff: bi * 512 + boff + qn * 128],
                                 start=first, stop=(dk == 3))
                    first = False
                return r
            use("pe", e_dn, reads=[b_pmb[0], bw["ones64"]], writes=[PB[7]], tag=f"dn{g}_{h}")

        def rec_pair(pc):
            use("dve", lambda e: e.reciprocal(rcp[:, :], bank(7)),
                reads=[PB[7]], writes=[b_rcp], tag=f"rc{g}_{pc}")

            def e_glbc(e, pc=pc):
                return e.matmul(bank(7), selg_s[:, pc * 128:(pc + 1) * 128], gsig[0:16, 0:512],
                                start=True, stop=True)
            use("pe", e_glbc, reads=[bw["selg"], b_gsig, b_rcp], writes=[PB[7]], tag=f"glbc{g}_{pc}")
            use("dve", lambda e: e.tensor_tensor(out=sbc[:, :], in0=bank(7), in1=rcp[:, :], op=ALU.mult),
                reads=[PB[7], b_rcp], writes=[b_sbc], sync_same=True, tag=f"sbc{g}_{pc}")
            use("dve", lambda e, pc=pc: e.tensor_tensor(out=comb[:, pc * 512:(pc + 1) * 512],
                                                        in0=bank(3), in1=sbc[:, :], op=ALU.mult),
                reads=[PB[3], b_sbc], writes=[b_comb[pc]], tag=f"cmb{g}_{pc}")

        for h in range(16):
            rec_scores(h)
            if h >= 1:
                rec_consume(h - 1)
                if (h - 1) % 2 == 1:
                    rec_pair((h - 1) // 2)
        rec_consume(15)
        rec_pair(7)

    def s3_remote(g):
        use("gp", lambda e, g=g: e.dma_start(out=mr_s[:, :], in_=maskr[:, g * 512:(g + 1) * 512]),
            writes=[bw["mr"]], dma=True, tag=f"ld_mr{g}")
        for pc in range(4):
            def e_rsc(e, pc=pc):
                e.matmul(bank(3)[0:64, :], gkT[0:64, pc * 64:(pc + 1) * 64],
                         gq_g[0:64, pc * 512:(pc + 1) * 512], start=True, stop=True)
                return e.matmul(bank(3)[64:128, :], gkT[64:128, pc * 64:(pc + 1) * 64],
                                gq_g[64:128, pc * 512:(pc + 1) * 512], start=True, stop=True)
            use("pe", e_rsc, reads=[b_gkT, b_gqg], writes=[PB[3]], tag=f"rsc{g}_{pc}")
            use("act", lambda e: e.activation(out=arena[:, AV_PREM:AV_PREM + 512], in_=bank(3),
                                              func=AF.Exp, bias=0.0, scale=0.125),
                reads=[PB[3]], writes=[b_prem], extra=[t_aw], tag=f"rexp{g}_{pc}")
            use("dve", lambda e: e.tensor_tensor(out=arena[:, AV_PM:AV_PM + 512],
                                                 in0=arena[:, AV_PREM:AV_PREM + 512],
                                                 in1=mr_s[:, :], op=ALU.mult),
                reads=[b_prem, bw["mr"]], writes=[b_pmrem], tag=f"rmk{g}_{pc}")

            def e_rdn(e):
                e.matmul(bank(2)[0:64, :], ones64_s[0:64, :], arena[0:64, AV_PM:AV_PM + 512],
                         start=True, stop=True)
                return e.matmul(bank(2)[64:128, :], ones64_s[64:128, :], arena[64:128, AV_PM:AV_PM + 512],
                                start=True, stop=True)
            use("pe", e_rdn, reads=[b_pmrem, bw["ones64"]], writes=[PB[2]], tag=f"rdn{g}_{pc}")
            use("dve", lambda e: e.reciprocal(arena[:, AV_RECR:AV_RECR + 1024].bitcast(F32), bank(2)),
                reads=[PB[2]], writes=[b_recr], tag=f"rrc{g}_{pc}")

            def e_rpv(e, pc=pc):
                e.matmul(bank(3)[0:64, :], gv_l[0:64, (2 * pc) * 64:(2 * pc + 1) * 64],
                         arena[0:64, AV_PM:AV_PM + 512], start=True, stop=True)
                return e.matmul(bank(3)[64:128, :], gv_l[64:128, (2 * pc + 1) * 64:(2 * pc + 2) * 64],
                                arena[64:128, AV_PM:AV_PM + 512], start=True, stop=True)
            use("pe", e_rpv, reads=[b_gvl, b_pmrem], writes=[PB[3]], tag=f"rpv{g}_{pc}")
            use("dve", lambda e, pc=pc: e.tensor_tensor(
                out=arena[:, AV_REML + pc * 512:AV_REML + (pc + 1) * 512], in0=bank(3),
                in1=arena[:, AV_RECR:AV_RECR + 1024].bitcast(F32), op=ALU.mult),
                reads=[PB[3], b_recr], writes=[b_reml], sync_same=True, tag=f"rle{g}_{pc}")
        for oc in range(8):
            def e_ro(e, oc=oc):
                r = None
                for pc in range(4):
                    r = e.matmul(bank(3), r_o[:, pc * 1024 + oc * 128: pc * 1024 + (oc + 1) * 128],
                                 arena[:, AV_REML + pc * 512:AV_REML + (pc + 1) * 512],
                                 start=(pc == 0), stop=(pc == 3))
                return r
            use("pe", e_ro, reads=[bw["r_o"], b_reml], writes=[PB[3]], tag=f"ro{g}_{oc}")

            def e_grbc(e, oc=oc):
                return e.matmul(bank(2), selg_s[:, oc * 128:(oc + 1) * 128], gsig[0:16, 512:1024],
                                start=True, stop=True)
            use("pe", e_grbc, reads=[bw["selg"], b_gsig], writes=[PB[2]], tag=f"grbc{g}_{oc}")
            use("dve", lambda e: e.tensor_copy(grbc_s[:, :], bank(2)),
                reads=[PB[2]], writes=[b_grbc], tag=f"gre{g}_{oc}")
            use("dve", lambda e: e.tensor_tensor(out=t1_s[:, :], in0=bank(3), in1=grbc_s[:, :],
                                                 op=ALU.mult),
                reads=[PB[3], b_grbc], writes=[b_t1], tag=f"t1{g}_{oc}")
            use("gp", lambda e, oc=oc: e.tensor_tensor(out=comb[:, oc * 512:(oc + 1) * 512],
                                                       in0=comb[:, oc * 512:(oc + 1) * 512],
                                                       in1=t1_s[:, :], op=ALU.add),
                reads=[b_t1, b_comb[oc]], writes=[b_comb[oc]], extra=[t_gw], tag=f"cadd{g}_{oc}")

    def ln_tile(stage, src_is_psum, src_aps, src_bufs, rst_col, extra=()):
        stt = {"s1": st_s1, "s5": st_s5, "s7": st_s7}[stage]
        rt = {"s1": rst_s1, "s5": rst_s5, "s7": rst_s7}[stage]
        sbuf_b, rb = b_stat[stage], b_rst[stage]
        c = rst_col

        def e_bn(e):
            e.bn_stats(stt[:, 0:6], src_aps[0])
            return e.bn_stats(stt[:, 6:12], src_aps[1])
        t1 = use("dve", e_bn, reads=list(src_bufs), writes=[sbuf_b], extra=list(extra), tag=f"bn_{stage}")
        t2 = use("dve", lambda e: e.bn_aggr(stt[:, 12:14], stt[:, 0:12]),
                 reads=[sbuf_b], writes=[sbuf_b], extra=[t1], sync_same=True, tag=f"ag_{stage}")

        def e_rs(e):
            e.activation(out=rt[:, c:c + 1], in_=stt[:, 13:14], func=AF.Ln, bias=epsv[:, :], scale=1.0)
            return e.activation(out=rt[:, c:c + 1], in_=rt[:, c:c + 1], func=AF.Exp, bias=0.0, scale=-0.5)
        t3 = use("act", e_rs, reads=[sbuf_b, bw["eps"]], writes=[rb], extra=[t_aw], tag=f"rs_{stage}")
        t4 = use("dve", lambda e: e.tensor_scalar(out=rt[:, c + 1:c + 2], in0=stt[:, 12:13],
                                                  scalar1=rt[:, c:c + 1], scalar2=-1.0,
                                                  op0=ALU.mult, op1=ALU.mult),
                 reads=[sbuf_b, rb], writes=[rb], extra=[t3], sync_same=True, tag=f"nm_{stage}")
        return t4, rt, c

    def s45_group(g, dumps=False):
        for ts in range(4):
            xslot = ts % 2
            xap = xa[:, xslot * 1024:(xslot + 1) * 1024]
            row = g * G + ts * 128
            t_xb = use("sp", lambda e, xap=xap, row=row: e.dma_start(out=xap, in_=x_own[row:row + 128, :]),
                       writes=[b_xa[xslot]], dma=True, tag=f"xr{g}_{ts}")
            for dh in range(2):
                bi = 4 + dh

                def e_wo(e, ts=ts, dh=dh, bi=bi):
                    r = None
                    for pc in range(8):
                        r = e.matmul(bank(bi), comb[:, pc * 512 + ts * 128: pc * 512 + (ts + 1) * 128],
                                     w_o[:, pc * 1024 + dh * 512: pc * 1024 + (dh + 1) * 512],
                                     start=(pc == 0), stop=(pc == 7))
                    return r
                use("pe", e_wo, reads=[bw["w_o"]] + [b_comb[pc] for pc in range(8)],
                    writes=[PB[bi]], tag=f"wo{g}_{ts}_{dh}")
            t4, rt, c = ln_tile("s5", True, [bank(4), bank(5)], [PB[4], PB[5]], 0)
            if dumps and ts == 0:
                use("dve", lambda e: e.tensor_copy(rcp[:, :], bank(4)), reads=[PB[4]],
                    writes=[b_rcp], tag="dbgcp")
                dump("attnout", lambda: rcp[:, :], 128, 512, 12032, reads=[b_rcp])
                dump("rst5", lambda: rst_s5[:, :], 128, 8, 12544, reads=[b_rst["s5"]])
            lsl = ts % 3
            lap = lnst[:, lsl * 1024:(lsl + 1) * 1024]

            def e_ap(e, lap=lap):
                e.tensor_scalar(out=lap[:, 0:512], in0=bank(4), scalar1=rt[:, c:c + 1],
                                scalar2=rt[:, c + 1:c + 2], op0=ALU.mult, op1=ALU.add)
                return e.tensor_scalar(out=lap[:, 512:1024], in0=bank(5), scalar1=rt[:, c:c + 1],
                                       scalar2=rt[:, c + 1:c + 2], op0=ALU.mult, op1=ALU.add)
            t_ap = use("dve", e_ap, reads=[PB[4], PB[5], b_rst["s5"]], writes=[b_lnst[lsl]],
                       extra=[t4], sync_same=True, tag=f"apA{g}_{ts}")
            if dumps and ts == 0:
                dump("lna", lambda lap=lap: lap, 128, 1024, 11008, reads=[b_lnst[lsl]])
            x1slot = ts % 2
            x1ap = x1_sb[:, x1slot * 1024:(x1slot + 1) * 1024]
            t_x1 = use("gp", lambda e, xap=xap, x1ap=x1ap, lap=lap: e.tensor_tensor(
                out=x1ap, in0=xap, in1=lap, op=ALU.add),
                reads=[b_xa[xslot], b_lnst[lsl]], writes=[b_x1[x1slot]], extra=[t_gw, t_xb], tag=f"x1{g}_{ts}")
            ti = g * 4 + ts
            use("sp", lambda e, x1ap=x1ap, row=row: e.dma_start(out=scrx1[row:row + 128, :], in_=x1ap),
                reads=[b_x1[x1slot]], writes=[b_scrx1[ti]], dma=True, tag=f"wx1_{ti}")
            if dumps and ts == 0:
                dump("x1", lambda x1ap=x1ap: x1ap, 128, 1024, 6912, reads=[b_x1[0]])
            t4b, rt2, c2 = ln_tile("s5", False, [x1ap[:, 0:512], x1ap[:, 512:1024]], [b_x1[x1slot]], 2)

            def e_ap2(e, lap=lap, x1ap=x1ap):
                return e.tensor_scalar(out=lap, in0=x1ap, scalar1=rt2[:, c2:c2 + 1],
                                       scalar2=rt2[:, c2 + 1:c2 + 2], op0=ALU.mult, op1=ALU.add)
            use("dve", e_ap2, reads=[b_x1[x1slot], b_rst["s5"]], writes=[b_lnst[lsl]],
                extra=[t4b], sync_same=True, tag=f"ap2{g}_{ts}")
            use("sp", lambda e, lap=lap, row=row: e.dma_start(out=scr2[row:row + 128, :], in_=lap),
                reads=[b_lnst[lsl]], writes=[b_scr2[ti]], dma=True, tag=f"wscr2_{ti}")

    def b3_transpose(g):
        need = [b_scr2[g * 4 + i].w for i in range(4)]

        def e(e_):
            return e_.dma_start_transpose(out=ln2xT[:, :].rearrange("p (c t) -> p c t", c=8),
                                          in_=scr2[g * G:(g + 1) * G, :])
        tok = use("sp", e, writes=[b_ln2xT], dma=True, tag=f"B3_{g}",
                  extra=[t for t in need if t] + rec.dma_quiesce())
        rec.last_dmat = tok
        return tok

    def s6_group(g):
        def e_sT(e):
            r = None
            for kc in range(8):
                r = e.matmul(bank(6)[0:64, :], w_in[:, kc * 64:(kc + 1) * 64],
                             ln2xT[:, kc * 512:(kc + 1) * 512], start=(kc == 0), stop=(kc == 7))
            return r
        use("pe", e_sT, reads=[bw["w_in"], b_ln2xT], writes=[PB[6]], tag=f"sT{g}")
        use("dve", lambda e: e.tensor_scalar(out=sT_v[0:64, :], in0=bank(6)[0:64, :],
                                             scalar1=bin_s[:, :], scalar2=None, op0=ALU.add),
            reads=[PB[6], bw["bin"]], writes=[b_sT], tag=f"sTe{g}")
        for k in range(8):
            bi = 4 + (k % 2)
            hs = k % 2

            def e_rep(e, k=k, bi=bi):
                return e.matmul(bank(bi), rrep_s[:, k * 128:(k + 1) * 128], sT_v[0:64, :],
                                start=True, stop=True)
            use("pe", e_rep, reads=[bw["rrep"], b_sT], writes=[PB[bi]], tag=f"rep{g}_{k}")
            use("act", lambda e, k=k, bi=bi, hs=hs: e.activation(
                out=h_v[:, hs * 512:(hs + 1) * 512], in_=bank(bi), func=AF.Relu,
                bias=b1s[:, k:k + 1], scale=w1s[:, k:k + 1]),
                reads=[PB[bi], bw["w1s"], bw["b1s"]], writes=[b_h[hs]], extra=[t_aw], tag=f"relu{g}_{k}")

            def e_bd(e, k=k, hs=hs):
                return e.matmul(bank(7)[0:64, :], bdh_s[:, k * 64:(k + 1) * 64],
                                h_v[:, hs * 512:(hs + 1) * 512], start=(k == 0), stop=(k == 7))
            use("pe", e_bd, reads=[bw["bdh"], b_h[hs]], writes=[PB[7]], tag=f"bd{g}_{k}")
        use("dve", lambda e: e.tensor_scalar(out=gm_v[0:64, :], in0=bank(7)[0:64, :],
                                             scalar1=b2_s[:, :], scalar2=None, op0=ALU.add),
            reads=[PB[7], bw["b2"]], writes=[b_gm], tag=f"gme{g}")
        for ts in range(4):
            row = g * G + ts * 128
            x1slot = ts % 2
            x1ap = x1_sb[:, x1slot * 1024:(x1slot + 1) * 1024]
            ti = g * 4 + ts
            t_x1r = use("sp", lambda e, x1ap=x1ap, row=row: e.dma_start(out=x1ap, in_=scrx1[row:row + 128, :]),
                        reads=[b_scrx1[ti]], writes=[b_x1[x1slot]], dma=True, tag=f"x1r{g}_{ts}")
            for dh in range(2):
                bi = 4 + dh

                def e_f(e, ts=ts, dh=dh, bi=bi):
                    e.matmul(bank(bi), gm_v[0:64, ts * 128:(ts + 1) * 128],
                             w_of[0:64, dh * 512:(dh + 1) * 512], start=True, stop=False)
                    return e.matmul(bank(bi), ones1_s[:, :], bout_v[0:1, dh * 512:(dh + 1) * 512],
                                    start=False, stop=True)
                use("pe", e_f, reads=[b_gm, bw["w_of"], bw["ones1"], b_union],
                    writes=[PB[bi]], tag=f"f{g}_{ts}_{dh}")
            t4, rt, c = ln_tile("s7", True, [bank(4), bank(5)], [PB[4], PB[5]], 0)
            lsl = ts % 3
            lap = lnst[:, lsl * 1024:(lsl + 1) * 1024]

            def e_ap(e, lap=lap):
                e.tensor_scalar(out=lap[:, 0:512], in0=bank(4), scalar1=rt[:, c:c + 1],
                                scalar2=rt[:, c + 1:c + 2], op0=ALU.mult, op1=ALU.add)
                return e.tensor_scalar(out=lap[:, 512:1024], in0=bank(5), scalar1=rt[:, c:c + 1],
                                       scalar2=rt[:, c + 1:c + 2], op0=ALU.mult, op1=ALU.add)
            t_ap = use("dve", e_ap, reads=[PB[4], PB[5], b_rst["s7"]], writes=[b_lnst[lsl]],
                       extra=[t4], sync_same=True, tag=f"apF{g}_{ts}")
            yslot = ts % 2
            yap = xa[:, yslot * 1024:(yslot + 1) * 1024]
            use("gp", lambda e, x1ap=x1ap, yap=yap, lap=lap: e.tensor_tensor(
                out=yap, in0=x1ap, in1=lap, op=ALU.add),
                reads=[b_x1[x1slot], b_lnst[lsl]], writes=[b_xa[yslot]], extra=[t_gw, t_x1r], tag=f"yt{g}_{ts}")
            use("sp", lambda e, yap=yap, row=row: e.dma_start(out=y[row:row + 128, :], in_=yap),
                reads=[b_xa[yslot]], dma=True, tag=f"wy_{ti}")

    for g in range(NGRP):
        if g > 0:
            b1_transpose(g)
        s2_group(g)
        if g == 0:
            dump("qg", lambda: q_g[:, 0:1024], 128, 1024, 1792, reads=[b_qg])
            dump("gate", lambda: gsig[0:16, 0:1024], 16, 1024, 2816, reads=[b_gsig])
        s3_local(g, dumps=(g == 0))
        if g == 0:
            use("gp", lambda e: e.dma_start(out=mj[:, :], in_=maskj[:, :]),
                writes=[bw["mj"]], dma=True, tag="ld_mj")
        s3_remote(g)
        if g == 0:
            dump("reml", lambda: arena[:, AV_REML:AV_REML + 512], 128, 512, 6400, reads=[b_reml])
            dump("combF", lambda: comb[:, 0:512], 128, 512, 5888, reads=[b_comb[0]])
        s45_group(g, dumps=(g == 0))
        b3_transpose(g)
        s6_group(g)

    final_waits = [(s, rec.cnt[s]) for s in Rec.DSEM.values()]
    comp_waits = [(s, rec.cnt[s]) for s in Rec.CSEM.values() if rec.cnt[s] > 0 and s != "s_sp"]

    engmap = {"pe": "tensor", "act": "scalar", "dve": "vector", "gp": "gpsimd", "sp": "sync"}
    with nc.Block() as block:
        for q in Rec.QUEUES:
            def mk(q):
                def body(e):
                    hw = {}
                    for (emit, deps, sem, inc, tag) in rec.ops[q]:
                        for (s, v) in deps:
                            if hw.get(s, 0) >= v:
                                continue
                            e.wait_ge(sems[s], v)
                            hw[s] = v
                        inst = emit(e)
                        if isinstance(inst, tuple):
                            inst = inst[-1]
                        inst.then_inc(sems[sem], inc)
                    if q == "sp":
                        for (s, v) in comp_waits + final_waits:
                            if v > 0 and hw.get(s, 0) < v:
                                e.wait_ge(sems[s], v)
                return body
            getattr(block, engmap[q])(mk(q))
    st.close()
    return nc


def _bf(a):
    return np.asarray(a, dtype=np.float32)


def build_consts(W1, b1, W2, glw, grw):
    c = {}
    gateblk8 = np.zeros((128, 8 * 32), np.float32)
    for pc in range(8):
        for par in range(2):
            h = 2 * pc + par
            rows = slice(par * 64, (par + 1) * 64)
            gateblk8[rows, pc * 32 + h] = glw[h]
            gateblk8[rows, pc * 32 + 16 + h] = grw[h]
    c["gateblk8"] = gateblk8
    selgm = np.zeros((16, 8 * 128), np.float32)
    for pc in range(8):
        for par in range(2):
            selgm[2 * pc + par, pc * 128 + par * 64: pc * 128 + (par + 1) * 64] = 1.0
    c["selg"] = selgm
    rrep = np.zeros((64, 8 * 128), np.float32)
    for k in range(8):
        for mm in range(8):
            m = 8 * k + mm
            for i in range(16):
                rrep[m, k * 128 + mm * 16 + i] = 1.0
    c["rrep"] = rrep
    bdh = np.zeros((128, 8 * 64), np.float32)
    w1sc = np.zeros((128, 8), np.float32)
    b1sc = np.zeros((128, 8), np.float32)
    for k in range(8):
        for mm in range(8):
            m = 8 * k + mm
            for i in range(16):
                bdh[mm * 16 + i, k * 64 + m] = W2[m, i]
                w1sc[mm * 16 + i, k] = W1[m, i]
                b1sc[mm * 16 + i, k] = b1[m, i]
    c["bdh"] = bdh
    c["w1scat"] = w1sc
    c["b1scat"] = b1sc
    jj = np.arange(128)[:, None]
    tt = np.arange(128)[None, :]
    Lm = (jj > tt).astype(np.float32)
    Um = (jj <= tt).astype(np.float32)
    Fm = np.ones((128, 128), np.float32)
    old = np.concatenate([Lm, Fm, Lm, Um, Fm, Lm, Um, Fm, Lm, Um, Fm, Um], axis=1)
    seg = {"s0": old[:, 0:128], "s1": old[:, 128:384], "s2": old[:, 384:768],
           "s3": old[:, 768:1152], "s4": old[:, 1152:1408], "s5": old[:, 1408:1536]}
    mjc = np.concatenate([seg["s1"], seg["s4"], seg["s2"], seg["s0"], seg["s3"], seg["s5"]], axis=1)
    c["maskj"] = mjc
    old0 = old.copy()
    old0[:, 0:384] = 0.0
    seg0 = {"s0": old0[:, 0:128], "s1": old0[:, 128:384], "s2": old0[:, 384:768],
            "s3": old0[:, 768:1152], "s4": old0[:, 1152:1408], "s5": old0[:, 1408:1536]}
    c["maskj0_first"] = np.concatenate([seg0["s1"], seg0["s4"], seg0["s2"], seg0["s0"],
                                        seg0["s3"], seg0["s5"]], axis=1)
    c["maskj0_second"] = mjc
    tglob_first = np.arange(T)
    slot = np.arange(64)
    def _mr(second):
        m = np.zeros((64, T), np.float32)
        for s in range(64):
            if second:
                valid = (s < 32) | (s - 32 <= (np.arange(T) // 64))
            else:
                valid = (s >= 32) & (s - 32 <= (np.arange(T) // 64))
            m[s] = valid.astype(np.float32)
        return np.concatenate([m, m], axis=0)
    c["maskr_first"] = _mr(False)
    c["maskr_second"] = _mr(True)
    pm = np.zeros((128, 16 * 32), np.float32)
    for i in range(16):
        for t in range(128):
            pm[t, i * 32 + 2 * i + t // 64] = 1.0 / 64.0
    c["pm32"] = pm
    c["identc"] = np.eye(64, dtype=np.float32)
    c["onesr"] = np.ones((1, 128), np.float32)
    c["ones64c"] = np.ones((128, 64), np.float32)
    return c


_NC_CACHE = {}


def kernel(L0_attn_x, L0_attn_wq, L0_attn_wk, L0_attn_wv, L0_attn_wgq, L0_attn_wgk,
           L0_attn_wgv, L0_attn_r_out_w, L0_attn_gate_local_w, L0_attn_gate_remote_w,
           L0_attn_w_out, L0_ffn_Win, L0_ffn_bin_, L0_ffn_W1, L0_ffn_b1, L0_ffn_W2,
           L0_ffn_b2, L0_ffn_Wout, L0_ffn_bout):
    x = np.ascontiguousarray(np.asarray(L0_attn_x, dtype=np.float32))
    Bx, S, Dm = x.shape
    consts = build_consts(_bf(L0_ffn_W1), _bf(L0_ffn_b1), _bf(L0_ffn_W2),
                          _bf(L0_attn_gate_local_w), _bf(L0_attn_gate_remote_w))
    shared = {
        "wq": _bf(L0_attn_wq), "wk": _bf(L0_attn_wk), "wv": _bf(L0_attn_wv),
        "wgq": _bf(L0_attn_wgq), "wgk": _bf(L0_attn_wgk), "wgv": _bf(L0_attn_wgv),
        "r_out": _bf(L0_attn_r_out_w), "w_out": _bf(L0_attn_w_out),
        "win": _bf(L0_ffn_Win), "binv": _bf(L0_ffn_bin_).reshape(64, 1),
        "woutf": _bf(L0_ffn_Wout), "boutv": _bf(L0_ffn_bout).reshape(1, D),
        "b2v": _bf(L0_ffn_b2).reshape(64, 1),
        "gateblk8": consts["gateblk8"], "selg": consts["selg"], "rrep": consts["rrep"],
        "bdh": consts["bdh"], "w1scat": consts["w1scat"], "b1scat": consts["b1scat"],
        "maskj": consts["maskj"], "pm32": consts["pm32"], "identc": consts["identc"],
        "onesr": consts["onesr"], "ones64c": consts["ones64c"],
    }
    in_maps = []
    zeros_h = np.zeros((HALO, D), np.float32)
    zeros_o = np.zeros((T, D), np.float32)
    for core in range(8):
        b = core // 2
        hf = core % 2
        m = dict(shared)
        m["x_own"] = np.ascontiguousarray(x[b, hf * T:(hf + 1) * T, :])
        m["x_halo"] = np.ascontiguousarray(x[b, T - HALO:T, :]) if hf == 1 else zeros_h
        m["x_other"] = np.ascontiguousarray(x[b, 0:T, :]) if hf == 1 else zeros_o
        m["maskj0"] = consts["maskj0_second"] if hf == 1 else consts["maskj0_first"]
        m["maskr"] = consts["maskr_second"] if hf == 1 else consts["maskr_first"]
        in_maps.append(m)

    key = "main"
    if key not in _NC_CACHE:
        _NC_CACHE[key] = build_nc()
    nc = _NC_CACHE[key]
    res = run_bass_kernel_spmd(nc, in_maps, core_ids=list(range(8)))
    out = np.empty((Bx, S, Dm), np.float32)
    for core in range(8):
        b = core // 2
        hf = core % 2
        out[b, hf * T:(hf + 1) * T, :] = res.results[core]["y"]
    return out


if __name__ == "__main__":
    pass


# version 07
# speedup vs baseline: 1.2026x; 1.0670x over previous
import os
import numpy as np
import ml_dtypes
import concourse.bass as bass
import concourse.mybir as mybir
from concourse.bass_utils import run_bass_kernel_spmd
from contextlib import ExitStack

F32 = mybir.dt.float32
BF16 = mybir.dt.bfloat16
AF = mybir.ActivationFunctionType
ALU = mybir.AluOpType

T = 2048
HALO = 256
D = 1024
NH = 16
G = 512
NG = T // G
NT = T // 128
NJT = (T + HALO) // 128
KR = 6
EPS = 1e-5
NGRP = int(os.environ.get("KNGRP", str(NG)))


class Rec:
    QUEUES = ("pe", "act", "dve", "gp", "sp")
    CSEM = {"pe": "s_pe", "act": "s_act", "dve": "s_dve", "gp": "s_gp", "sp": "s_sp"}
    DSEM = {"sp": "d_sp", "act": "d_act", "gp": "d_gp"}

    def __init__(self):
        self.ops = {q: [] for q in self.QUEUES}
        self.cnt = {s: 0 for s in list(self.CSEM.values()) + list(self.DSEM.values())}
        self.last_dmat = None

    def add(self, q, emit, deps=(), dma=False, tag="", keep_same=False):
        sem = self.DSEM[q] if dma else self.CSEM[q]
        inc = 16 if dma else 1
        dd = {}
        for t in deps:
            if t is None:
                continue
            s, v = t
            if (not dma) and (not keep_same) and s == self.CSEM[q]:
                continue
            dd[s] = max(dd.get(s, 0), v)
        if dma and self.last_dmat is not None:
            s, v = self.last_dmat
            dd[s] = max(dd.get(s, 0), v)
        self.cnt[sem] += inc
        self.ops[q].append((emit, sorted(dd.items()), sem, inc, tag))
        return (sem, self.cnt[sem])

    def dma_quiesce(self):
        return [(s, self.cnt[s]) for s in self.DSEM.values() if self.cnt[s] > 0]


class Buf:
    __slots__ = ("w", "r")

    def __init__(self):
        self.w = None
        self.r = []


def mkuse(rec):
    def use(q, emit, reads=(), writes=(), dma=False, tag="", extra=(), sync_same=False):
        deps = list(extra)
        for b in reads:
            deps.append(b.w)
        for b in writes:
            deps.append(b.w)
            deps.extend(b.r)
        tok = rec.add(q, emit, deps, dma=dma, tag=tag, keep_same=sync_same)
        for b in reads:
            b.r.append(tok)
        for b in writes:
            b.w = tok
            b.r = []
        return tok
    return use


def build_nc(debug_keys=()):
    nc = bass.Bass("TRN2", target_bir_lowering=False)

    def din(name, shape, dtype=F32):
        return nc.dram_tensor(name, shape, dtype, kind="ExternalInput").ap()

    x_own = din("x_own", [T, D])
    x_halo = din("x_halo", [HALO, D])
    x_other = din("x_other", [T, D])
    wq = din("wq", [D, D]); wk = din("wk", [D, D]); wv = din("wv", [D, D])
    wgq = din("wgq", [D, 512]); wgk = din("wgk", [D, 512]); wgv = din("wgv", [D, 512])
    r_out = din("r_out", [512, D]); w_out = din("w_out", [D, D])
    win = din("win", [D, 64]); binv = din("binv", [64, 1])
    woutf = din("woutf", [64, D]); boutv = din("boutv", [1, D]); b2v = din("b2v", [64, 1])
    gateblk8 = din("gateblk8", [128, 8 * 32])
    selg = din("selg", [16, 8 * 128])
    rrep = din("rrep", [64, 8 * 128])
    bdh = din("bdh", [128, 8 * 64])
    w1scat = din("w1scat", [128, 8]); b1scat = din("b1scat", [128, 8])
    maskj = din("maskj", [128, 1536]); maskj0 = din("maskj0", [128, 1536])
    maskr = din("maskr", [128, 2048])
    pm32 = din("pm32", [128, 16 * 32])
    identc = din("identc", [64, 64])
    onesr = din("onesr", [1, 128])
    ones64c = din("ones64c", [128, 64])

    y = nc.dram_tensor("y", [T, D], F32, kind="ExternalOutput").ap()
    dbg = None
    if debug_keys:
        dbg = nc.dram_tensor("dbg", [128, 16384], F32, kind="ExternalOutput").ap()
    scr1 = nc.dram_tensor("scr1", [T + HALO, D], BF16, kind="Internal").ap()
    scr2 = nc.dram_tensor("scr2", [T, D], BF16, kind="Internal").ap()
    scrx1 = nc.dram_tensor("scrx1", [T, D], F32, kind="Internal").ap()

    rec = Rec()
    use = mkuse(rec)
    st = ExitStack()
    en = st.enter_context

    def sb(name, shape, dtype):
        return en(nc.sbuf_tensor(name, shape, dtype))

    w_q = sb("w_q", [128, 8 * 1024], BF16)
    w_k = sb("w_k", [128, 8 * 1024], BF16)
    w_v = sb("w_v", [128, 8 * 1024], BF16)
    w_o = sb("w_o", [128, 8 * 1024], BF16)
    w_gq = sb("w_gq", [128, 8 * 512], BF16)
    r_o = sb("r_o", [128, 4 * 1024], BF16)
    w_in = sb("w_in", [128, 8 * 64], BF16)
    w_of = sb("w_of", [64, 1024], BF16)
    rrep_s = sb("rrep_s", [64, 8 * 128], BF16)
    bdh_s = sb("bdh_s", [128, 8 * 64], BF16)
    gblk = sb("gblk", [128, 8 * 32], BF16)
    selg_s = sb("selg_s", [16, 8 * 128], BF16)
    mj = sb("mj", [128, 1536], BF16)
    mr_s = sb("mr_s", [128, 512], BF16)
    ident_s = sb("ident_s", [64, 64], BF16)
    ones1_s = sb("ones1_s", [1, 128], BF16)
    ones64_s = sb("ones64_s", [128, 64], BF16)
    w1s = sb("w1s", [128, 8], F32)
    b1s = sb("b1s", [128, 8], F32)
    bin_s = sb("bin_s", [64, 1], F32)
    b2_s = sb("b2_s", [64, 1], F32)
    epsv = sb("epsv", [128, 1], F32)

    arena = sb("arena", [128, 4096], BF16)
    grbc_s = sb("grbc_s", [128, 512], BF16)
    t1_s = sb("t1_s", [128, 512], BF16)
    xa = sb("xa", [128, 2 * 1024], F32)
    lnst = sb("lnst", [128, 3 * 1024], BF16)
    ln1xT = sb("ln1xT", [128, 8 * 768], BF16)
    q_g = sb("q_g", [128, 8 * 512], BF16)
    ln2xT = sb("ln2xT", [128, 8 * 512], BF16)
    gq_g = sb("gq_g", [128, 4 * 512], BF16)
    k_sb = sb("k_sb", [128, 8 * KR * 128], BF16)
    v_sb = sb("v_sb", [128, KR * 1024], BF16)
    pmb = sb("pmb", [128, 1536], BF16)
    sbc = sb("sbc", [128, 512], BF16)
    rcp = sb("rcp", [128, 512], F32)
    comb = sb("comb", [128, 8 * 512], BF16)
    gsig = sb("gsig", [16, 1024], BF16)
    union = sb("union", [128, 3072], BF16)
    pooled = union[:, 0:1024]
    pooledT = union[:, 1024:1536]
    pm32_v = union[:, 1536:2048]
    sT_v = union[:, 0:512]
    h_v = union[:, 512:1536]
    gm_v = union[:, 1536:2048]
    bout_v = union[:, 2048:3072]
    gkT = sb("gkT", [128, 4 * 64], BF16)
    gv_l = sb("gv_l", [128, 512], BF16)
    x1_sb = sb("x1_sb", [128, 2 * 1024], F32)

    st_s1 = sb("st_s1", [128, 16], F32)
    st_s5 = sb("st_s5", [128, 16], F32)
    st_s7 = sb("st_s7", [128, 16], F32)
    rst_s1 = sb("rst_s1", [128, 4], F32)
    rst_s5 = sb("rst_s5", [128, 8], F32)
    rst_s7 = sb("rst_s7", [128, 4], F32)
    wrm = rcp
    wrmb = sbc

    psA = en(nc.psum_tensor("psA", [128, 2048], F32))
    psB = en(nc.psum_tensor("psB", [128, 2048], F32))

    def bank(i):
        t = psA if i < 4 else psB
        return t[:, (i % 4) * 512:((i % 4) + 1) * 512]

    def bank_t(i):
        return (psA if i < 4 else psB, (i % 4) * 512)

    sems = {}
    for s in list(Rec.CSEM.values()) + list(Rec.DSEM.values()):
        sems[s] = en(nc.semaphore(s))

    PB = [Buf() for _ in range(8)]
    B = Buf
    bw = {n: B() for n in ["w_q", "w_k", "w_v", "w_o", "w_gq", "r_o", "w_in", "w_of",
                           "rrep", "bdh", "gblk", "selg", "mj", "mj0", "mr",
                           "ident", "ones1", "ones64", "w1s", "b1s", "bin", "b2",
                           "eps"]}
    b_xa = [B(), B()]
    b_lnst = [B() for _ in range(3)]
    b_ln1xT = B(); b_qg = B(); b_gqg = B()
    b_ln2xT = B()
    b_k = [B() for _ in range(KR)]
    b_v = [B() for _ in range(KR)]
    b_pmb = [B()]
    b_sbc = B(); b_rcp = B()
    b_comb = [B() for _ in range(8)]
    b_gsig = B()
    b_union = B()
    b_pooled = b_union; b_pooledT = b_union; b_gkT = B(); b_gvl = B()
    b_x1 = [B(), B()]
    b_scrx1 = [B() for _ in range(NT)]
    b_sT = b_union; b_h = [b_union, b_union]; b_gm = b_union
    b_stat = {"s1": B(), "s5": B(), "s7": B()}
    b_rst = {"s1": B(), "s5": B(), "s7": B()}
    b_scr1 = [B() for _ in range(NJT)]
    b_scr2 = [B() for _ in range(NT)]
    b_arena = B()
    b_prem = B(); b_pmrem = B(); b_reml = B(); b_grbc = B(); b_t1 = B(); b_recr = B()
    b_dbg = B()

    AV_PREM, AV_PM, AV_RECR, AV_REML = 0, 512, 1024, 2048

    dbg_done = []

    def dump(key, ap_fn, rows, cols, col0, reads=()):
        if dbg is None:
            return

        def e(e_):
            return e_.dma_start(out=dbg[0:rows, col0:col0 + cols], in_=ap_fn())
        use("gp", e, reads=list(reads), dma=True, tag=f"dbg_{key}")
        dbg_done.append(key)

    def w_dve(e):
        e.memset(wrm[:, :], 1.0)
        e.memset(wrmb[:, 256:512], 0.0)
        e.memset(epsv[:, :], EPS)
        r = None
        for _ in range(2):
            e.bn_stats(st_s1[:, 0:6], wrm[:, 0:512])
            e.bn_stats(st_s1[:, 6:12], wrm[:, 0:512])
            e.bn_aggr(st_s1[:, 12:14], st_s1[:, 0:12])
            e.reciprocal(wrm[:, 0:1], wrm[:, 1:2])
            e.reciprocal(wrm[:, 0:512], wrm[:, 0:512])
            e.memset(wrm[:, 0:512], 1.0)
            r = e.tensor_scalar(out=wrm[:, 2:3], in0=wrm[:, 3:4], scalar1=wrm[:, 4:5],
                                scalar2=wrm[:, 5:6], op0=ALU.mult, op1=ALU.add)
        return r
    t_dw = use("dve", w_dve, writes=[b_rcp, b_sbc, b_stat["s1"], bw["eps"]], tag="warm_dve")

    def w_pe(e):
        return e.matmul(psA[:, 0:512], wrmb[0:1, 256:384], wrmb[0:1, 0:512], start=True, stop=True)
    t_pw = use("pe", w_pe, reads=[b_sbc], writes=[PB[0]], tag="warm_pe")

    def w_dve2(e):
        e.bn_stats(st_s1[:, 0:6], psA[:, 0:512])
        e.bn_stats(st_s1[:, 6:12], psA[:, 0:512])
        e.bn_aggr(st_s1[:, 12:14], st_s1[:, 0:12])
        return e.reciprocal(wrm[:, 0:512], wrm[:, 0:512])
    t_dw2 = use("dve", w_dve2, reads=[PB[0]], writes=[b_rcp, b_stat["s1"]], tag="warm_dve2")

    def w_act(e):
        r = None
        for _ in range(2):
            e.activation(out=wrm[:, 6:7], in_=wrm[:, 7:8], func=AF.Ln, bias=epsv[:, :], scale=1.0)
            e.activation(out=wrm[:, 6:7], in_=wrm[:, 7:8], func=AF.Exp, bias=0.0, scale=-0.5)
            e.activation(out=wrmb[:, 0:64], in_=psA[:, 0:64], func=AF.Exp, bias=0.0, scale=0.125)
            e.activation(out=wrmb[:, 0:64], in_=wrm[:, 72:136], func=AF.Tanh, bias=0.0, scale=0.5)
            e.activation(out=wrmb[:, 0:64], in_=psA[:, 0:64], func=AF.Relu, bias=wrm[:, 1:2],
                         scale=wrm[:, 2:3])
            r = e.activation(out=wrmb[:, 0:64], in_=psA[:, 0:64], func=AF.Copy, bias=0.0, scale=1.0)
        return r
    t_aw = use("act", w_act, reads=[b_rcp, bw["eps"], PB[0]], writes=[b_rcp, b_sbc], tag="warm_act")

    def w_gp(e):
        e.memset(wrmb[:, 256:512], 1.0)
        e.tensor_tensor(out=wrmb[:, 0:128], in0=wrmb[:, 256:384], in1=wrmb[:, 384:512], op=ALU.add)
        e.tensor_copy(wrmb[:, 0:128], wrmb[:, 256:384])
        e.tensor_tensor(out=wrm[:, 128:256], in0=wrm[:, 256:384], in1=wrmb[:, 384:512], op=ALU.add)
        r = e.tensor_scalar(out=wrmb[:, 0:256], in0=wrm[:, 128:384], scalar1=wrm[:, 0:1],
                            scalar2=wrm[:, 1:2], op0=ALU.mult, op1=ALU.add)
        return r
    t_gw = use("gp", w_gp, reads=[b_rcp], writes=[b_rcp, b_sbc], extra=[t_dw], tag="warm_gp")

    def wload(q, dst, src, bufs, tag):
        return use(q, lambda e: e.dma_start(out=dst, in_=src), writes=bufs, dma=True, tag=tag)

    pool_first = {"flag": True}

    def s1_tile(kind, i, xi):
        slot = xi % 2
        lsl = xi % 3
        xap = xa[:, slot * 1024:(slot + 1) * 1024]
        lap = lnst[:, lsl * 1024:(lsl + 1) * 1024]
        src = {"halo": x_halo, "own": x_own, "other": x_other}[kind]
        row = i * 128
        t_ld = use("sp", lambda e, xap=xap, src=src, row=row: e.dma_start(out=xap, in_=src[row:row + 128, :]),
                   writes=[b_xa[slot]], dma=True, tag=f"x_{kind}{i}")

        def e_bn(e, xap=xap):
            e.bn_stats(st_s1[:, 0:6], xap[:, 0:512])
            return e.bn_stats(st_s1[:, 6:12], xap[:, 512:1024])
        t_bn = use("dve", e_bn, reads=[b_xa[slot]], writes=[b_stat["s1"]], extra=[t_dw2], tag=f"bn1_{kind}{i}")
        t_ag = use("dve", lambda e: e.bn_aggr(st_s1[:, 12:14], st_s1[:, 0:12]),
                   reads=[b_stat["s1"]], writes=[b_stat["s1"]], extra=[t_bn], sync_same=True, tag=f"ag1_{kind}{i}")

        def e_rs(e):
            e.activation(out=rst_s1[:, 0:1], in_=st_s1[:, 13:14], func=AF.Ln, bias=epsv[:, :], scale=1.0)
            return e.activation(out=rst_s1[:, 0:1], in_=rst_s1[:, 0:1], func=AF.Exp, bias=0.0, scale=-0.5)
        t_rs = use("act", e_rs, reads=[b_stat["s1"], bw["eps"]], writes=[b_rst["s1"]], extra=[t_aw], tag=f"rs1_{kind}{i}")
        t_nm = use("dve", lambda e: e.tensor_scalar(out=rst_s1[:, 1:2], in0=st_s1[:, 12:13],
                                                    scalar1=rst_s1[:, 0:1], scalar2=-1.0,
                                                    op0=ALU.mult, op1=ALU.mult),
                   reads=[b_stat["s1"], b_rst["s1"]], writes=[b_rst["s1"]], extra=[t_rs], sync_same=True,
                   tag=f"nm1_{kind}{i}")
        t_ap = use("gp", lambda e, xap=xap, lap=lap: e.tensor_scalar(
            out=lap, in0=xap, scalar1=rst_s1[:, 0:1], scalar2=rst_s1[:, 1:2], op0=ALU.mult, op1=ALU.add),
            reads=[b_xa[slot], b_rst["s1"]], writes=[b_lnst[lsl]], extra=[t_nm, t_gw], tag=f"ap1_{kind}{i}")
        if kind != "other":
            jt = i if kind == "halo" else i + 2
            use("sp", lambda e, lap=lap, jt=jt: e.dma_start(out=scr1[jt * 128:(jt + 1) * 128, :], in_=lap),
                reads=[b_lnst[lsl]], writes=[b_scr1[jt]], dma=True, tag=f"w_scr1_{jt}")
        if kind != "halo":
            orow = slice(32, 64) if kind == "own" else slice(0, 32)
            first = pool_first["flag"]
            pool_first["flag"] = False
            last = (kind == "other" and i == NT - 1)

            def e_pool2(e, lap=lap, i=i, orow=orow, kind=kind):
                r = None
                for ds in range(2):
                    r = e.matmul(bank(6 + ds)[orow, :], pm32_v[:, i * 32:(i + 1) * 32],
                                 lap[:, ds * 512:(ds + 1) * 512],
                                 start=(i == 0), stop=(i == NT - 1))
                return r
            use("pe", e_pool2, reads=[b_lnst[lsl], b_union], writes=[PB[6], PB[7]],
                extra=[t_ap], tag=f"pool_{kind}{i}")

    wload("gp", pm32_v[:, :], pm32[:, :], [b_union], "ld_pm32")
    for i in range(2):
        s1_tile("halo", i, i)
    for i in range(0, 4):
        s1_tile("own", i, 2 + i)


    def b1_transpose(g):
        need = [b_scr1[j].w for j in range(4 * g, 4 * g + 6)]

        def e(e_):
            return e_.dma_start_transpose(out=ln1xT[:, :].rearrange("p (c t) -> p c t", c=8),
                                          in_=scr1[g * G:g * G + G + HALO, :])
        tok = use("sp", e, writes=[b_ln1xT], dma=True, tag=f"B1_{g}",
                  extra=[t for t in need if t] + rec.dma_quiesce())
        rec.last_dmat = tok
        return tok

    b1_transpose(0)
    wload("gp", w_q[:, :].rearrange("p (c e) -> p c e", c=8), wq.rearrange("(c p) e -> p c e", p=128), [bw["w_q"]], "ld_wq")
    wload("gp", w_k[:, :].rearrange("p (c e) -> p c e", c=8), wk.rearrange("(c p) e -> p c e", p=128), [bw["w_k"]], "ld_wk")
    wload("gp", w_v[:, :].rearrange("p (c e) -> p c e", c=8), wv.rearrange("(c p) e -> p c e", p=128), [bw["w_v"]], "ld_wv")
    wload("gp", w_o[:, :].rearrange("p (c e) -> p c e", c=8), w_out.rearrange("(c p) e -> p c e", p=128), [bw["w_o"]], "ld_wo")
    wload("gp", w_gq[:, :].rearrange("p (c e) -> p c e", c=8), wgq.rearrange("(c p) e -> p c e", p=128), [bw["w_gq"]], "ld_wgq")
    wload("gp", arena[:, 0:4096].rearrange("p (c e) -> p c e", c=8), wgk.rearrange("(c p) e -> p c e", p=128), [b_arena], "ld_wgk")
    wload("gp", r_o[:, :].rearrange("p (c e) -> p c e", c=4), r_out.rearrange("(c p) e -> p c e", p=128), [bw["r_o"]], "ld_ro")
    wload("gp", w_in[:, :].rearrange("p (c e) -> p c e", c=8), win.rearrange("(c p) e -> p c e", p=128), [bw["w_in"]], "ld_win")
    wload("gp", w_of[:, :], woutf[:, :], [bw["w_of"]], "ld_wof")
    wload("gp", rrep_s[:, :], rrep[:, :], [bw["rrep"]], "ld_rrep")
    wload("gp", bdh_s[:, :], bdh[:, :], [bw["bdh"]], "ld_bdh")
    wload("gp", gblk[:, :], gateblk8[:, :], [bw["gblk"]], "ld_gblk")
    wload("gp", selg_s[:, :], selg[:, :], [bw["selg"]], "ld_selg")
    wload("gp", mj[:, :], maskj0[:, :], [bw["mj"]], "ld_mj0")
    wload("gp", ident_s[:, :], identc[:, :], [bw["ident"]], "ld_ident")
    wload("gp", ones1_s[:, :], onesr[:, :], [bw["ones1"]], "ld_ones1")
    wload("gp", ones64_s[:, :], ones64c[:, :], [bw["ones64"]], "ld_ones64")
    wload("gp", bout_v[0:1, :], boutv[:, :], [b_union], "ld_bout")
    wload("sp", w1s[:, :], w1scat[:, :], [bw["w1s"]], "ld_w1s")
    wload("sp", b1s[:, :], b1scat[:, :], [bw["b1s"]], "ld_b1s")
    wload("sp", bin_s[:, :], binv[:, :], [bw["bin"]], "ld_bin")
    wload("sp", b2_s[:, :], b2v[:, :], [bw["b2"]], "ld_b2")
    for i in range(4, NT):
        s1_tile("own", i, 2 + i)
    for i in range(NT):
        s1_tile("other", i, 2 + NT + i)

    use("dve", lambda e: e.tensor_copy(pooled[0:64, 0:512], bank(6)[0:64, :]),
        reads=[PB[6]], writes=[b_pooled], tag="pool_ev0")
    use("dve", lambda e: e.tensor_copy(pooled[0:64, 512:1024], bank(7)[0:64, :]),
        reads=[PB[7]], writes=[b_pooled], tag="pool_ev1")
    for c in range(8):
        bi = 6 + (c % 2)

        def e_tp(e, c=c, bi=bi):
            return e.transpose(bank(bi).bitcast(BF16)[:, 0:64], pooled[0:64, c * 128:(c + 1) * 128],
                               ident_s[:, :])
        use("pe", e_tp, reads=[b_pooled, bw["ident"]], writes=[PB[bi]], tag=f"poolT{c}")
        use("dve", lambda e, c=c, bi=bi: e.tensor_copy(pooledT[:, c * 64:(c + 1) * 64],
                                                       bank(bi).bitcast(BF16)[:, 0:64]),
            reads=[PB[bi]], writes=[b_pooledT], tag=f"poolTe{c}")
    for gc in range(4):
        def e_gk(e, gc=gc):
            r = None
            for kc in range(8):
                r = e.matmul(bank(6)[:, 0:64], arena[:, kc * 512 + gc * 128: kc * 512 + gc * 128 + 128],
                             pooledT[:, kc * 64:(kc + 1) * 64], start=(kc == 0), stop=(kc == 7))
            return r
        use("pe", e_gk, reads=[b_arena, b_pooledT], writes=[PB[6]], tag=f"gk{gc}")
        use("dve", lambda e, gc=gc: e.tensor_copy(gkT[:, gc * 64:(gc + 1) * 64], bank(6)[:, 0:64]),
            reads=[PB[6]], writes=[b_gkT], tag=f"gke{gc}")

    wload("gp", arena[:, 0:4096].rearrange("p (c e) -> p c e", c=8),
          wgv.rearrange("(c p) e -> p c e", p=128), [b_arena], "ld_wgv")

    def e_gv(e):
        r = None
        for kc in range(8):
            r = e.matmul(bank(7)[0:64, :], pooledT[:, kc * 64:(kc + 1) * 64],
                         arena[:, kc * 512:(kc + 1) * 512], start=(kc == 0), stop=(kc == 7))
        return r
    use("pe", e_gv, reads=[b_arena, b_pooledT], writes=[PB[7]], tag="gv")
    use("dve", lambda e: e.tensor_copy(gv_l[0:64, :], bank(7)[0:64, :]),
        reads=[PB[7]], writes=[b_gvl], tag="gve")
    use("pe", lambda e: e.matmul(bank(7)[64:128, :], ident_s[:, :], gv_l[0:64, :], start=True, stop=True),
        reads=[b_gvl, bw["ident"]], writes=[PB[7]], tag="gv64")
    use("dve", lambda e: e.tensor_copy(gv_l[64:128, :], bank(7)[64:128, :]),
        reads=[PB[7]], writes=[b_gvl], tag="gv64e")
    dump("pooled", lambda: pooled[0:64, :], 64, 1024, 7936, reads=[b_union])

    GEOM = [(-2, 0, 1, 0, 0, 0), (-1, 0, 2, 0, 128, 128), (0, 0, 3, 1, 0, 384),
            (1, 1, 3, 2, 0, 768), (2, 2, 2, 3, 0, 1152), (3, 3, 1, 3, 256, 1408)]

    def s2_group(g):
        for ec in range(8):
            bi = 4 + (ec % 2)

            def e_q(e, ec=ec, bi=bi):
                r = None
                for kc in range(8):
                    r = e.matmul(bank(bi), w_q[:, kc * 1024 + ec * 128: kc * 1024 + (ec + 1) * 128],
                                 ln1xT[:, kc * 768 + 256: kc * 768 + 768], start=(kc == 0), stop=(kc == 7))
                return r
            use("pe", e_q, reads=[bw["w_q"], b_ln1xT], writes=[PB[bi]], tag=f"q{g}_{ec}")
            use("dve", lambda e, ec=ec, bi=bi: e.tensor_copy(q_g[:, ec * 512:(ec + 1) * 512], bank(bi)),
                reads=[PB[bi]], writes=[b_qg], tag=f"qe{g}_{ec}")
        jts_all = list(range(0, 6)) if g == 0 else list(range(4 * g + 2, 4 * g + 6))
        col0_all = 0 if g == 0 else 256
        kchunks = [jts_all[i:i + 4] for i in range(0, len(jts_all), 4)]
        for ci, jts in enumerate(kchunks):
            col0 = col0_all + ci * 512
            nj = len(jts)
            for ec in range(8):
                bi = 4 + (ec % 2)

                def e_k(e, ec=ec, bi=bi, col0=col0, nj=nj):
                    r = None
                    for kc in range(8):
                        r = e.matmul(bank(bi)[:, 0:nj * 128], w_k[:, kc * 1024 + ec * 128: kc * 1024 + (ec + 1) * 128],
                                     ln1xT[:, kc * 768 + col0: kc * 768 + col0 + nj * 128], start=(kc == 0), stop=(kc == 7))
                    return r
                use("pe", e_k, reads=[bw["w_k"], b_ln1xT], writes=[PB[bi]], tag=f"k{g}_{ci}_{ec}")

                def e_ke(e, ec=ec, bi=bi, jts=jts, nj=nj):
                    r = None
                    ji = 0
                    while ji < nj:
                        sl = jts[ji] % KR
                        run = 1
                        while ji + run < nj and (jts[ji + run] % KR) == sl + run:
                            run += 1
                        r = e.tensor_copy(k_sb[:, ec * (KR * 128) + sl * 128: ec * (KR * 128) + (sl + run) * 128],
                                          bank(bi)[:, ji * 128:(ji + run) * 128])
                        ji += run
                    return r
                use("dve", e_ke, reads=[PB[bi]], writes=[b_k[jt % KR] for jt in jts], tag=f"ke{g}_{ci}_{ec}")
        for ji, jt in enumerate(jts_all):
            lcol = col0_all + ji * 128
            for vs in range(2):
                bi = 4 + vs

                def e_v(e, jt=jt, lcol=lcol, vs=vs, bi=bi):
                    r = None
                    for kc in range(8):
                        r = e.matmul(bank(bi), ln1xT[:, kc * 768 + lcol: kc * 768 + lcol + 128],
                                     w_v[:, kc * 1024 + vs * 512: kc * 1024 + (vs + 1) * 512],
                                     start=(kc == 0), stop=(kc == 7))
                    return r
                use("pe", e_v, reads=[bw["w_v"], b_ln1xT], writes=[PB[bi]], tag=f"v{g}_{jt}_{vs}")
                use("dve", lambda e, jt=jt, vs=vs, bi=bi: e.tensor_copy(
                    v_sb[:, (jt % KR) * 1024 + vs * 512: (jt % KR) * 1024 + (vs + 1) * 512], bank(bi)),
                    reads=[PB[bi]], writes=[b_v[jt % KR]], tag=f"ve{g}_{jt}_{vs}")
        for pc in range(4):
            bi = 4 + (pc % 2)

            def e_gq(e, pc=pc, bi=bi):
                r = None
                for kc in range(8):
                    r = e.matmul(bank(bi), w_gq[:, kc * 512 + pc * 128: kc * 512 + (pc + 1) * 128],
                                 ln1xT[:, kc * 768 + 256: kc * 768 + 768], start=(kc == 0), stop=(kc == 7))
                return r
            use("pe", e_gq, reads=[bw["w_gq"], b_ln1xT], writes=[PB[bi]], tag=f"gq{g}_{pc}")
            use("act", lambda e, pc=pc, bi=bi: e.activation(out=gq_g[:, pc * 512:(pc + 1) * 512],
                                                            in_=bank(bi), func=AF.Copy, bias=0.0, scale=1.0),
                reads=[PB[bi]], writes=[b_gqg], extra=[t_aw], tag=f"gqe{g}_{pc}")

        def e_gate(e):
            r = None
            for c in range(8):
                e.matmul(bank(6)[0:16, :], gblk[:, c * 32:c * 32 + 16],
                         q_g[:, c * 512:(c + 1) * 512], start=(c == 0), stop=(c == 7))
            for c in range(8):
                r = e.matmul(bank(3)[0:16, :], gblk[:, c * 32 + 16:(c + 1) * 32],
                             q_g[:, c * 512:(c + 1) * 512], start=(c == 0), stop=(c == 7))
            return r
        use("pe", e_gate, reads=[bw["gblk"], b_qg], writes=[PB[6], PB[3]], tag=f"gate{g}")

        def e_sig(e):
            e.activation(out=gsig[0:16, 0:512], in_=bank(6)[0:16, :], func=AF.Tanh, bias=0.0, scale=0.5)
            return e.activation(out=gsig[0:16, 512:1024], in_=bank(3)[0:16, :], func=AF.Tanh,
                                bias=0.0, scale=0.5)
        use("act", e_sig, reads=[PB[6], PB[3]], writes=[b_gsig], extra=[t_aw], tag=f"sig{g}")
        use("dve", lambda e: e.tensor_scalar(out=gsig[0:16, :], in0=gsig[0:16, :], scalar1=0.5,
                                             scalar2=0.5, op0=ALU.mult, op1=ALU.add),
            reads=[b_gsig], writes=[b_gsig], tag=f"sigf{g}")

    GEOM2 = [(-1, 0, 2, 0, 0), (2, 2, 2, 0, 256), (0, 0, 3, 1, 0), (-2, 0, 1, 1, 384),
             (1, 1, 3, 2, 0), (3, 3, 1, 2, 384)]

    def s3_local(g, dumps=False):
        mjbuf = bw["mj"]

        def rec_scores(h):
            par = h % 2
            pc = h // 2
            rows = slice(par * 64, (par + 1) * 64)
            bb = 0 if par == 0 else 4

            def e_sc(e, pc=pc, rows=rows, bb=bb, g=g):
                r = None
                for (dk, q0, qn, bi, boff) in GEOM2:
                    jt = 4 * g + 2 + dk
                    sl = jt % KR
                    r = e.matmul(bank(bb + bi)[:, boff:boff + qn * 128],
                                 k_sb[rows, pc * (KR * 128) + sl * 128: pc * (KR * 128) + (sl + 1) * 128],
                                 q_g[rows, pc * 512 + q0 * 128: pc * 512 + (q0 + qn) * 128],
                                 start=True, stop=True)
                return r
            kreads = [b_k[(4 * g + 2 + dk) % KR] for (dk, *_r) in GEOM2]
            use("pe", e_sc, reads=kreads + [b_qg],
                writes=[PB[bb], PB[bb + 1], PB[bb + 2]], tag=f"sc{g}_{h}")

        def rec_consume(h):
            par = h % 2
            pc = h // 2
            bb = 0 if par == 0 else 4
            tens = psA if par == 0 else psB

            def e_exp(e, tens=tens):
                return e.activation(out=pmb[:, :], in_=tens[:, 0:1536], func=AF.Exp,
                                    bias=0.0, scale=0.125)
            use("act", e_exp, reads=[PB[bb], PB[bb + 1], PB[bb + 2]],
                writes=[b_pmb[0]], extra=[t_aw], tag=f"exp{g}_{h}")
            use("dve", lambda e: e.tensor_tensor(out=pmb[:, :], in0=pmb[:, :], in1=mj[:, :],
                                                 op=ALU.mult),
                reads=[b_pmb[0], mjbuf], writes=[b_pmb[0]], tag=f"mk{g}_{h}")

            def e_pv(e, pc=pc, par=par, g=g):
                r = None
                h = 2 * pc + par
                orow = slice(0, 64) if par == 0 else slice(64, 128)
                first = True
                for (dk, q0, qn, bi, boff) in GEOM2:
                    jt = 4 * g + 2 + dk
                    sl = jt % KR
                    r = e.matmul(bank(3)[orow, q0 * 128:(q0 + qn) * 128],
                                 v_sb[:, sl * 1024 + h * 64: sl * 1024 + (h + 1) * 64],
                                 pmb[:, bi * 512 + boff: bi * 512 + boff + qn * 128],
                                 start=first, stop=(dk == 3))
                    first = False
                return r
            vreads = [b_v[(4 * g + 2 + dk) % KR] for (dk, *_r) in GEOM2]
            use("pe", e_pv, reads=vreads + [b_pmb[0]], writes=[PB[3]], tag=f"pv{g}_{h}")

            def e_dn(e, par=par):
                r = None
                orow = slice(0, 64) if par == 0 else slice(64, 128)
                first = True
                for (dk, q0, qn, bi, boff) in GEOM2:
                    r = e.matmul(bank(7)[orow, q0 * 128:(q0 + qn) * 128],
                                 ones64_s[:, :],
                                 pmb[:, bi * 512 + boff: bi * 512 + boff + qn * 128],
                                 start=first, stop=(dk == 3))
                    first = False
                return r
            use("pe", e_dn, reads=[b_pmb[0], bw["ones64"]], writes=[PB[7]], tag=f"dn{g}_{h}")

        def rec_pair(pc):
            use("dve", lambda e: e.reciprocal(rcp[:, :], bank(7)),
                reads=[PB[7]], writes=[b_rcp], tag=f"rc{g}_{pc}")

            def e_glbc(e, pc=pc):
                return e.matmul(bank(7), selg_s[:, pc * 128:(pc + 1) * 128], gsig[0:16, 0:512],
                                start=True, stop=True)
            use("pe", e_glbc, reads=[bw["selg"], b_gsig, b_rcp], writes=[PB[7]], tag=f"glbc{g}_{pc}")
            use("dve", lambda e: e.tensor_tensor(out=sbc[:, :], in0=bank(7), in1=rcp[:, :], op=ALU.mult),
                reads=[PB[7], b_rcp], writes=[b_sbc], sync_same=True, tag=f"sbc{g}_{pc}")
            use("dve", lambda e, pc=pc: e.tensor_tensor(out=comb[:, pc * 512:(pc + 1) * 512],
                                                        in0=bank(3), in1=sbc[:, :], op=ALU.mult),
                reads=[PB[3], b_sbc], writes=[b_comb[pc]], tag=f"cmb{g}_{pc}")

        for h in range(16):
            rec_scores(h)
            if h >= 1:
                rec_consume(h - 1)
                if (h - 1) % 2 == 1:
                    rec_pair((h - 1) // 2)
        rec_consume(15)
        rec_pair(7)

    def s3_remote(g):
        use("gp", lambda e, g=g: e.dma_start(out=mr_s[:, :], in_=maskr[:, g * 512:(g + 1) * 512]),
            writes=[bw["mr"]], dma=True, tag=f"ld_mr{g}")

        def rsc(pc):
            bs = 3 if pc % 2 == 0 else 5

            def e_rsc(e, pc=pc, bs=bs):
                e.matmul(bank(bs)[0:64, :], gkT[0:64, pc * 64:(pc + 1) * 64],
                         gq_g[0:64, pc * 512:(pc + 1) * 512], start=True, stop=True)
                return e.matmul(bank(bs)[64:128, :], gkT[64:128, pc * 64:(pc + 1) * 64],
                                gq_g[64:128, pc * 512:(pc + 1) * 512], start=True, stop=True)
            use("pe", e_rsc, reads=[b_gkT, b_gqg], writes=[PB[bs]], tag=f"rsc{g}_{pc}")

        def rcons(pc):
            bs = 3 if pc % 2 == 0 else 5
            bd = 2 if pc % 2 == 0 else 6
            use("act", lambda e, bs=bs: e.activation(out=arena[:, AV_PREM:AV_PREM + 512], in_=bank(bs),
                                                     func=AF.Exp, bias=0.0, scale=0.125),
                reads=[PB[bs]], writes=[b_prem], extra=[t_aw], tag=f"rexp{g}_{pc}")
            use("dve", lambda e: e.tensor_tensor(out=arena[:, AV_PM:AV_PM + 512],
                                                 in0=arena[:, AV_PREM:AV_PREM + 512],
                                                 in1=mr_s[:, :], op=ALU.mult),
                reads=[b_prem, bw["mr"]], writes=[b_pmrem], tag=f"rmk{g}_{pc}")

            def e_rdn(e, bd=bd):
                e.matmul(bank(bd)[0:64, :], ones64_s[0:64, :], arena[0:64, AV_PM:AV_PM + 512],
                         start=True, stop=True)
                return e.matmul(bank(bd)[64:128, :], ones64_s[64:128, :], arena[64:128, AV_PM:AV_PM + 512],
                                start=True, stop=True)
            use("pe", e_rdn, reads=[b_pmrem, bw["ones64"]], writes=[PB[bd]], tag=f"rdn{g}_{pc}")
            use("dve", lambda e, bd=bd: e.reciprocal(arena[:, AV_RECR:AV_RECR + 1024].bitcast(F32), bank(bd)),
                reads=[PB[bd]], writes=[b_recr], tag=f"rrc{g}_{pc}")

            def e_rpv(e, pc=pc, bs=bs):
                e.matmul(bank(bs)[0:64, :], gv_l[0:64, (2 * pc) * 64:(2 * pc + 1) * 64],
                         arena[0:64, AV_PM:AV_PM + 512], start=True, stop=True)
                return e.matmul(bank(bs)[64:128, :], gv_l[64:128, (2 * pc + 1) * 64:(2 * pc + 2) * 64],
                                arena[64:128, AV_PM:AV_PM + 512], start=True, stop=True)
            use("pe", e_rpv, reads=[b_gvl, b_pmrem], writes=[PB[bs]], tag=f"rpv{g}_{pc}")
            use("dve", lambda e, pc=pc, bs=bs: e.tensor_tensor(
                out=arena[:, AV_REML + pc * 512:AV_REML + (pc + 1) * 512], in0=bank(bs),
                in1=arena[:, AV_RECR:AV_RECR + 1024].bitcast(F32), op=ALU.mult),
                reads=[PB[bs], b_recr], writes=[b_reml], sync_same=True, tag=f"rle{g}_{pc}")

        for pc in range(4):
            rsc(pc)
            if pc >= 1:
                rcons(pc - 1)
        rcons(3)
        for oc in range(8):
            bs = 3 if oc % 2 == 0 else 5
            bd = 2 if oc % 2 == 0 else 6

            def e_ro(e, oc=oc, bs=bs):
                r = None
                for pc in range(4):
                    r = e.matmul(bank(bs), r_o[:, pc * 1024 + oc * 128: pc * 1024 + (oc + 1) * 128],
                                 arena[:, AV_REML + pc * 512:AV_REML + (pc + 1) * 512],
                                 start=(pc == 0), stop=(pc == 3))
                return r
            use("pe", e_ro, reads=[bw["r_o"], b_reml], writes=[PB[bs]], tag=f"ro{g}_{oc}")

            def e_grbc(e, oc=oc, bd=bd):
                return e.matmul(bank(bd), selg_s[:, oc * 128:(oc + 1) * 128], gsig[0:16, 512:1024],
                                start=True, stop=True)
            use("pe", e_grbc, reads=[bw["selg"], b_gsig], writes=[PB[bd]], tag=f"grbc{g}_{oc}")
            use("dve", lambda e, bd=bd: e.tensor_copy(grbc_s[:, :], bank(bd)),
                reads=[PB[bd]], writes=[b_grbc], tag=f"gre{g}_{oc}")
            use("dve", lambda e, bs=bs: e.tensor_tensor(out=t1_s[:, :], in0=bank(bs), in1=grbc_s[:, :],
                                                        op=ALU.mult),
                reads=[PB[bs], b_grbc], writes=[b_t1], tag=f"t1{g}_{oc}")
            use("gp", lambda e, oc=oc: e.tensor_tensor(out=comb[:, oc * 512:(oc + 1) * 512],
                                                       in0=comb[:, oc * 512:(oc + 1) * 512],
                                                       in1=t1_s[:, :], op=ALU.add),
                reads=[b_t1, b_comb[oc]], writes=[b_comb[oc]], extra=[t_gw], tag=f"cadd{g}_{oc}")

    def ln_tile(stage, src_is_psum, src_aps, src_bufs, rst_col, extra=()):
        stt = {"s1": st_s1, "s5": st_s5, "s7": st_s7}[stage]
        rt = {"s1": rst_s1, "s5": rst_s5, "s7": rst_s7}[stage]
        sbuf_b, rb = b_stat[stage], b_rst[stage]
        c = rst_col

        def e_bn(e):
            e.bn_stats(stt[:, 0:6], src_aps[0])
            return e.bn_stats(stt[:, 6:12], src_aps[1])
        t1 = use("dve", e_bn, reads=list(src_bufs), writes=[sbuf_b], extra=list(extra), tag=f"bn_{stage}")
        t2 = use("dve", lambda e: e.bn_aggr(stt[:, 12:14], stt[:, 0:12]),
                 reads=[sbuf_b], writes=[sbuf_b], extra=[t1], sync_same=True, tag=f"ag_{stage}")

        def e_rs(e):
            e.activation(out=rt[:, c:c + 1], in_=stt[:, 13:14], func=AF.Ln, bias=epsv[:, :], scale=1.0)
            return e.activation(out=rt[:, c:c + 1], in_=rt[:, c:c + 1], func=AF.Exp, bias=0.0, scale=-0.5)
        t3 = use("act", e_rs, reads=[sbuf_b, bw["eps"]], writes=[rb], extra=[t_aw], tag=f"rs_{stage}")
        t4 = use("dve", lambda e: e.tensor_scalar(out=rt[:, c + 1:c + 2], in0=stt[:, 12:13],
                                                  scalar1=rt[:, c:c + 1], scalar2=-1.0,
                                                  op0=ALU.mult, op1=ALU.mult),
                 reads=[sbuf_b, rb], writes=[rb], extra=[t3], sync_same=True, tag=f"nm_{stage}")
        return t4, rt, c

    def s45_group(g, dumps=False):
        for ts in range(4):
            xslot = ts % 2
            xap = xa[:, xslot * 1024:(xslot + 1) * 1024]
            row = g * G + ts * 128
            t_xb = use("sp", lambda e, xap=xap, row=row: e.dma_start(out=xap, in_=x_own[row:row + 128, :]),
                       writes=[b_xa[xslot]], dma=True, tag=f"xr{g}_{ts}")
            for dh in range(2):
                bi = 4 + dh

                def e_wo(e, ts=ts, dh=dh, bi=bi):
                    r = None
                    for pc in range(8):
                        r = e.matmul(bank(bi), comb[:, pc * 512 + ts * 128: pc * 512 + (ts + 1) * 128],
                                     w_o[:, pc * 1024 + dh * 512: pc * 1024 + (dh + 1) * 512],
                                     start=(pc == 0), stop=(pc == 7))
                    return r
                use("pe", e_wo, reads=[bw["w_o"]] + [b_comb[pc] for pc in range(8)],
                    writes=[PB[bi]], tag=f"wo{g}_{ts}_{dh}")
            t4, rt, c = ln_tile("s5", True, [bank(4), bank(5)], [PB[4], PB[5]], 0)
            if dumps and ts == 0:
                use("dve", lambda e: e.tensor_copy(rcp[:, :], bank(4)), reads=[PB[4]],
                    writes=[b_rcp], tag="dbgcp")
                dump("attnout", lambda: rcp[:, :], 128, 512, 12032, reads=[b_rcp])
                dump("rst5", lambda: rst_s5[:, :], 128, 8, 12544, reads=[b_rst["s5"]])
            lsl = ts % 3
            lap = lnst[:, lsl * 1024:(lsl + 1) * 1024]

            def e_ap(e, lap=lap):
                e.tensor_scalar(out=lap[:, 0:512], in0=bank(4), scalar1=rt[:, c:c + 1],
                                scalar2=rt[:, c + 1:c + 2], op0=ALU.mult, op1=ALU.add)
                return e.tensor_scalar(out=lap[:, 512:1024], in0=bank(5), scalar1=rt[:, c:c + 1],
                                       scalar2=rt[:, c + 1:c + 2], op0=ALU.mult, op1=ALU.add)
            t_ap = use("dve", e_ap, reads=[PB[4], PB[5], b_rst["s5"]], writes=[b_lnst[lsl]],
                       extra=[t4], sync_same=True, tag=f"apA{g}_{ts}")
            if dumps and ts == 0:
                dump("lna", lambda lap=lap: lap, 128, 1024, 11008, reads=[b_lnst[lsl]])
            x1slot = ts % 2
            x1ap = x1_sb[:, x1slot * 1024:(x1slot + 1) * 1024]
            t_x1 = use("gp", lambda e, xap=xap, x1ap=x1ap, lap=lap: e.tensor_tensor(
                out=x1ap, in0=xap, in1=lap, op=ALU.add),
                reads=[b_xa[xslot], b_lnst[lsl]], writes=[b_x1[x1slot]], extra=[t_gw, t_xb], tag=f"x1{g}_{ts}")
            ti = g * 4 + ts
            use("sp", lambda e, x1ap=x1ap, row=row: e.dma_start(out=scrx1[row:row + 128, :], in_=x1ap),
                reads=[b_x1[x1slot]], writes=[b_scrx1[ti]], dma=True, tag=f"wx1_{ti}")
            if dumps and ts == 0:
                dump("x1", lambda x1ap=x1ap: x1ap, 128, 1024, 6912, reads=[b_x1[0]])
            t4b, rt2, c2 = ln_tile("s5", False, [x1ap[:, 0:512], x1ap[:, 512:1024]], [b_x1[x1slot]], 2)

            def e_ap2(e, lap=lap, x1ap=x1ap):
                return e.tensor_scalar(out=lap, in0=x1ap, scalar1=rt2[:, c2:c2 + 1],
                                       scalar2=rt2[:, c2 + 1:c2 + 2], op0=ALU.mult, op1=ALU.add)
            use("dve", e_ap2, reads=[b_x1[x1slot], b_rst["s5"]], writes=[b_lnst[lsl]],
                extra=[t4b], sync_same=True, tag=f"ap2{g}_{ts}")
            use("sp", lambda e, lap=lap, row=row: e.dma_start(out=scr2[row:row + 128, :], in_=lap),
                reads=[b_lnst[lsl]], writes=[b_scr2[ti]], dma=True, tag=f"wscr2_{ti}")

    def b3_transpose(g):
        need = [b_scr2[g * 4 + i].w for i in range(4)]

        def e(e_):
            return e_.dma_start_transpose(out=ln2xT[:, :].rearrange("p (c t) -> p c t", c=8),
                                          in_=scr2[g * G:(g + 1) * G, :])
        tok = use("sp", e, writes=[b_ln2xT], dma=True, tag=f"B3_{g}",
                  extra=[t for t in need if t] + rec.dma_quiesce())
        rec.last_dmat = tok
        return tok

    def s6_group(g):
        def e_sT(e):
            r = None
            for kc in range(8):
                r = e.matmul(bank(6)[0:64, :], w_in[:, kc * 64:(kc + 1) * 64],
                             ln2xT[:, kc * 512:(kc + 1) * 512], start=(kc == 0), stop=(kc == 7))
            return r
        use("pe", e_sT, reads=[bw["w_in"], b_ln2xT], writes=[PB[6]], tag=f"sT{g}")
        use("dve", lambda e: e.tensor_scalar(out=sT_v[0:64, :], in0=bank(6)[0:64, :],
                                             scalar1=bin_s[:, :], scalar2=None, op0=ALU.add),
            reads=[PB[6], bw["bin"]], writes=[b_sT], tag=f"sTe{g}")
        for k in range(8):
            bi = 4 + (k % 2)
            hs = k % 2

            def e_rep(e, k=k, bi=bi):
                return e.matmul(bank(bi), rrep_s[:, k * 128:(k + 1) * 128], sT_v[0:64, :],
                                start=True, stop=True)
            use("pe", e_rep, reads=[bw["rrep"], b_sT], writes=[PB[bi]], tag=f"rep{g}_{k}")
            use("act", lambda e, k=k, bi=bi, hs=hs: e.activation(
                out=h_v[:, hs * 512:(hs + 1) * 512], in_=bank(bi), func=AF.Relu,
                bias=b1s[:, k:k + 1], scale=w1s[:, k:k + 1]),
                reads=[PB[bi], bw["w1s"], bw["b1s"]], writes=[b_h[hs]], extra=[t_aw], tag=f"relu{g}_{k}")

            def e_bd(e, k=k, hs=hs):
                return e.matmul(bank(7)[0:64, :], bdh_s[:, k * 64:(k + 1) * 64],
                                h_v[:, hs * 512:(hs + 1) * 512], start=(k == 0), stop=(k == 7))
            use("pe", e_bd, reads=[bw["bdh"], b_h[hs]], writes=[PB[7]], tag=f"bd{g}_{k}")
        use("dve", lambda e: e.tensor_scalar(out=gm_v[0:64, :], in0=bank(7)[0:64, :],
                                             scalar1=b2_s[:, :], scalar2=None, op0=ALU.add),
            reads=[PB[7], bw["b2"]], writes=[b_gm], tag=f"gme{g}")
        for ts in range(4):
            row = g * G + ts * 128
            x1slot = ts % 2
            x1ap = x1_sb[:, x1slot * 1024:(x1slot + 1) * 1024]
            ti = g * 4 + ts
            t_x1r = use("sp", lambda e, x1ap=x1ap, row=row: e.dma_start(out=x1ap, in_=scrx1[row:row + 128, :]),
                        reads=[b_scrx1[ti]], writes=[b_x1[x1slot]], dma=True, tag=f"x1r{g}_{ts}")
            for dh in range(2):
                bi = 4 + dh

                def e_f(e, ts=ts, dh=dh, bi=bi):
                    e.matmul(bank(bi), gm_v[0:64, ts * 128:(ts + 1) * 128],
                             w_of[0:64, dh * 512:(dh + 1) * 512], start=True, stop=False)
                    return e.matmul(bank(bi), ones1_s[:, :], bout_v[0:1, dh * 512:(dh + 1) * 512],
                                    start=False, stop=True)
                use("pe", e_f, reads=[b_gm, bw["w_of"], bw["ones1"], b_union],
                    writes=[PB[bi]], tag=f"f{g}_{ts}_{dh}")
            t4, rt, c = ln_tile("s7", True, [bank(4), bank(5)], [PB[4], PB[5]], 0)
            lsl = ts % 3
            lap = lnst[:, lsl * 1024:(lsl + 1) * 1024]

            def e_ap(e, lap=lap):
                e.tensor_scalar(out=lap[:, 0:512], in0=bank(4), scalar1=rt[:, c:c + 1],
                                scalar2=rt[:, c + 1:c + 2], op0=ALU.mult, op1=ALU.add)
                return e.tensor_scalar(out=lap[:, 512:1024], in0=bank(5), scalar1=rt[:, c:c + 1],
                                       scalar2=rt[:, c + 1:c + 2], op0=ALU.mult, op1=ALU.add)
            t_ap = use("dve", e_ap, reads=[PB[4], PB[5], b_rst["s7"]], writes=[b_lnst[lsl]],
                       extra=[t4], sync_same=True, tag=f"apF{g}_{ts}")
            yslot = ts % 2
            yap = xa[:, yslot * 1024:(yslot + 1) * 1024]
            use("gp", lambda e, x1ap=x1ap, yap=yap, lap=lap: e.tensor_tensor(
                out=yap, in0=x1ap, in1=lap, op=ALU.add),
                reads=[b_x1[x1slot], b_lnst[lsl]], writes=[b_xa[yslot]], extra=[t_gw, t_x1r], tag=f"yt{g}_{ts}")
            use("sp", lambda e, yap=yap, row=row: e.dma_start(out=y[row:row + 128, :], in_=yap),
                reads=[b_xa[yslot]], dma=True, tag=f"wy_{ti}")

    for g in range(NGRP):
        if g > 0:
            b1_transpose(g)
        s2_group(g)
        if g == 0:
            dump("qg", lambda: q_g[:, 0:1024], 128, 1024, 1792, reads=[b_qg])
            dump("gate", lambda: gsig[0:16, 0:1024], 16, 1024, 2816, reads=[b_gsig])
        s3_local(g, dumps=(g == 0))
        if g == 0:
            use("gp", lambda e: e.dma_start(out=mj[:, :], in_=maskj[:, :]),
                writes=[bw["mj"]], dma=True, tag="ld_mj")
        s3_remote(g)
        if g == 0:
            dump("reml", lambda: arena[:, AV_REML:AV_REML + 512], 128, 512, 6400, reads=[b_reml])
            dump("combF", lambda: comb[:, 0:512], 128, 512, 5888, reads=[b_comb[0]])
        s45_group(g, dumps=(g == 0))
        b3_transpose(g)
        s6_group(g)

    final_waits = [(s, rec.cnt[s]) for s in Rec.DSEM.values()]
    comp_waits = [(s, rec.cnt[s]) for s in Rec.CSEM.values() if rec.cnt[s] > 0 and s != "s_sp"]

    engmap = {"pe": "tensor", "act": "scalar", "dve": "vector", "gp": "gpsimd", "sp": "sync"}
    with nc.Block() as block:
        for q in Rec.QUEUES:
            def mk(q):
                def body(e):
                    hw = {}
                    for (emit, deps, sem, inc, tag) in rec.ops[q]:
                        for (s, v) in deps:
                            if hw.get(s, 0) >= v:
                                continue
                            e.wait_ge(sems[s], v)
                            hw[s] = v
                        inst = emit(e)
                        if isinstance(inst, tuple):
                            inst = inst[-1]
                        inst.then_inc(sems[sem], inc)
                    if q == "sp":
                        for (s, v) in comp_waits + final_waits:
                            if v > 0 and hw.get(s, 0) < v:
                                e.wait_ge(sems[s], v)
                return body
            getattr(block, engmap[q])(mk(q))
    st.close()
    return nc


def _bf(a):
    return np.asarray(a, dtype=np.float32)


def build_consts(W1, b1, W2, glw, grw):
    c = {}
    gateblk8 = np.zeros((128, 8 * 32), np.float32)
    for pc in range(8):
        for par in range(2):
            h = 2 * pc + par
            rows = slice(par * 64, (par + 1) * 64)
            gateblk8[rows, pc * 32 + h] = glw[h]
            gateblk8[rows, pc * 32 + 16 + h] = grw[h]
    c["gateblk8"] = gateblk8
    selgm = np.zeros((16, 8 * 128), np.float32)
    for pc in range(8):
        for par in range(2):
            selgm[2 * pc + par, pc * 128 + par * 64: pc * 128 + (par + 1) * 64] = 1.0
    c["selg"] = selgm
    rrep = np.zeros((64, 8 * 128), np.float32)
    for k in range(8):
        for mm in range(8):
            m = 8 * k + mm
            for i in range(16):
                rrep[m, k * 128 + mm * 16 + i] = 1.0
    c["rrep"] = rrep
    bdh = np.zeros((128, 8 * 64), np.float32)
    w1sc = np.zeros((128, 8), np.float32)
    b1sc = np.zeros((128, 8), np.float32)
    for k in range(8):
        for mm in range(8):
            m = 8 * k + mm
            for i in range(16):
                bdh[mm * 16 + i, k * 64 + m] = W2[m, i]
                w1sc[mm * 16 + i, k] = W1[m, i]
                b1sc[mm * 16 + i, k] = b1[m, i]
    c["bdh"] = bdh
    c["w1scat"] = w1sc
    c["b1scat"] = b1sc
    jj = np.arange(128)[:, None]
    tt = np.arange(128)[None, :]
    Lm = (jj > tt).astype(np.float32)
    Um = (jj <= tt).astype(np.float32)
    Fm = np.ones((128, 128), np.float32)
    old = np.concatenate([Lm, Fm, Lm, Um, Fm, Lm, Um, Fm, Lm, Um, Fm, Um], axis=1)
    seg = {"s0": old[:, 0:128], "s1": old[:, 128:384], "s2": old[:, 384:768],
           "s3": old[:, 768:1152], "s4": old[:, 1152:1408], "s5": old[:, 1408:1536]}
    mjc = np.concatenate([seg["s1"], seg["s4"], seg["s2"], seg["s0"], seg["s3"], seg["s5"]], axis=1)
    c["maskj"] = mjc
    old0 = old.copy()
    old0[:, 0:384] = 0.0
    seg0 = {"s0": old0[:, 0:128], "s1": old0[:, 128:384], "s2": old0[:, 384:768],
            "s3": old0[:, 768:1152], "s4": old0[:, 1152:1408], "s5": old0[:, 1408:1536]}
    c["maskj0_first"] = np.concatenate([seg0["s1"], seg0["s4"], seg0["s2"], seg0["s0"],
                                        seg0["s3"], seg0["s5"]], axis=1)
    c["maskj0_second"] = mjc
    tglob_first = np.arange(T)
    slot = np.arange(64)
    def _mr(second):
        m = np.zeros((64, T), np.float32)
        for s in range(64):
            if second:
                valid = (s < 32) | (s - 32 <= (np.arange(T) // 64))
            else:
                valid = (s >= 32) & (s - 32 <= (np.arange(T) // 64))
            m[s] = valid.astype(np.float32)
        return np.concatenate([m, m], axis=0)
    c["maskr_first"] = _mr(False)
    c["maskr_second"] = _mr(True)
    pm = np.zeros((128, 16 * 32), np.float32)
    for i in range(16):
        for t in range(128):
            pm[t, i * 32 + 2 * i + t // 64] = 1.0 / 64.0
    c["pm32"] = pm
    c["identc"] = np.eye(64, dtype=np.float32)
    c["onesr"] = np.ones((1, 128), np.float32)
    c["ones64c"] = np.ones((128, 64), np.float32)
    return c


_NC_CACHE = {}


def kernel(L0_attn_x, L0_attn_wq, L0_attn_wk, L0_attn_wv, L0_attn_wgq, L0_attn_wgk,
           L0_attn_wgv, L0_attn_r_out_w, L0_attn_gate_local_w, L0_attn_gate_remote_w,
           L0_attn_w_out, L0_ffn_Win, L0_ffn_bin_, L0_ffn_W1, L0_ffn_b1, L0_ffn_W2,
           L0_ffn_b2, L0_ffn_Wout, L0_ffn_bout):
    x = np.ascontiguousarray(np.asarray(L0_attn_x, dtype=np.float32))
    Bx, S, Dm = x.shape
    consts = build_consts(_bf(L0_ffn_W1), _bf(L0_ffn_b1), _bf(L0_ffn_W2),
                          _bf(L0_attn_gate_local_w), _bf(L0_attn_gate_remote_w))
    shared = {
        "wq": _bf(L0_attn_wq), "wk": _bf(L0_attn_wk), "wv": _bf(L0_attn_wv),
        "wgq": _bf(L0_attn_wgq), "wgk": _bf(L0_attn_wgk), "wgv": _bf(L0_attn_wgv),
        "r_out": _bf(L0_attn_r_out_w), "w_out": _bf(L0_attn_w_out),
        "win": _bf(L0_ffn_Win), "binv": _bf(L0_ffn_bin_).reshape(64, 1),
        "woutf": _bf(L0_ffn_Wout), "boutv": _bf(L0_ffn_bout).reshape(1, D),
        "b2v": _bf(L0_ffn_b2).reshape(64, 1),
        "gateblk8": consts["gateblk8"], "selg": consts["selg"], "rrep": consts["rrep"],
        "bdh": consts["bdh"], "w1scat": consts["w1scat"], "b1scat": consts["b1scat"],
        "maskj": consts["maskj"], "pm32": consts["pm32"], "identc": consts["identc"],
        "onesr": consts["onesr"], "ones64c": consts["ones64c"],
    }
    in_maps = []
    zeros_h = np.zeros((HALO, D), np.float32)
    zeros_o = np.zeros((T, D), np.float32)
    for core in range(8):
        b = core // 2
        hf = core % 2
        m = dict(shared)
        m["x_own"] = np.ascontiguousarray(x[b, hf * T:(hf + 1) * T, :])
        m["x_halo"] = np.ascontiguousarray(x[b, T - HALO:T, :]) if hf == 1 else zeros_h
        m["x_other"] = np.ascontiguousarray(x[b, 0:T, :]) if hf == 1 else zeros_o
        m["maskj0"] = consts["maskj0_second"] if hf == 1 else consts["maskj0_first"]
        m["maskr"] = consts["maskr_second"] if hf == 1 else consts["maskr_first"]
        in_maps.append(m)

    key = "main"
    if key not in _NC_CACHE:
        _NC_CACHE[key] = build_nc()
    nc = _NC_CACHE[key]
    res = run_bass_kernel_spmd(nc, in_maps, core_ids=list(range(8)))
    out = np.empty((Bx, S, Dm), np.float32)
    for core in range(8):
        b = core // 2
        hf = core % 2
        out[b, hf * T:(hf + 1) * T, :] = res.results[core]["y"]
    return out


if __name__ == "__main__":
    pass
